# version 23
# baseline (speedup 1.0000x reference)
import numpy as np
import ml_dtypes
from contextlib import ExitStack

import concourse.bass as bass
import concourse.mybir as mybir
from concourse.bass_utils import run_bass_kernel_spmd

F32 = mybir.dt.float32
BF16 = mybir.dt.bfloat16
F16 = mybir.dt.float16
U16 = mybir.dt.uint16
I16 = mybir.dt.int16
AF = mybir.ActivationFunctionType
ALU = mybir.AluOpType
AX = mybir.AxisListType

DM = 2048
NT = 16
NH = 16
NI = 5632
NIC = NI // 128
NFC = 16
EPS32 = float(np.finfo(np.float32).eps)
SCALE_O = float(128 ** -0.5)
QN = 3 * DM * DM
PN = DM * DM


class Sem:
    def __init__(self, ctx, nc, name):
        self.h = ctx.enter_context(nc.semaphore(name))
        self.n = 0

    def bump(self, k=1):
        self.n += k
        return self.n


class Plan:
    def __init__(self):
        self.ops = []

    def on(self, eng, fn):
        self.ops.append((eng, fn))

    def wait(self, eng, sem, val):
        if val is None or val <= 0:
            return
        self.ops.append((eng, lambda e, s=sem.h, v=val: e.wait_ge(s, v)))

    def wait_lazy(self, eng, sem, fn):
        self.ops.append((eng, lambda e, s=sem.h, f=fn: e.wait_ge(s, f())))

    def run(self, eng_key, eng):
        for k, fn in self.ops:
            if k == eng_key:
                fn(eng)


def build_nc(stage="full", debug=False):
    nc = bass.Bass("TRN2", target_bir_lowering=False, debug=False)
    ctx = ExitStack()

    dx = nc.dram_tensor("x", [DM, DM], F32, kind="ExternalInput").ap()
    dqkvT = nc.dram_tensor("qkvT", [DM, 3 * DM], F32, kind="ExternalInput").ap()
    dprojT = nc.dram_tensor("projT", [DM, DM], F32, kind="ExternalInput").ap()
    dgcol = nc.dram_tensor("gcol", [128, NFC], F32, kind="ExternalInput").ap()
    dw1col = nc.dram_tensor("w1col", [128, NFC], F32, kind="ExternalInput").ap()
    dw2col = nc.dram_tensor("w2col", [128, NFC], F32, kind="ExternalInput").ap()
    dw1row = nc.dram_tensor("w1row", [1, DM], F32, kind="ExternalInput").ap()
    dsrow = nc.dram_tensor("srow", [1, NH], F32, kind="ExternalInput").ap()
    dguw = nc.dram_tensor("guw", [DM, 2 * NI], U16, kind="ExternalInput").ap()
    dgus = nc.dram_tensor("gus", [NFC, 2 * NI], F16, kind="ExternalInput").ap()
    ddnw = nc.dram_tensor("dnw", [NI, DM], U16, kind="ExternalInput").ap()
    ddns = nc.dram_tensor("dns", [NIC, DM], F16, kind="ExternalInput").ap()
    dy = nc.dram_tensor("y", [DM, DM], F32, kind="ExternalOutput").ap()
    if debug:
        dbg_hT = nc.dram_tensor("dbg_hT", [128, 32768], BF16, kind="ExternalOutput").ap()
        dbg_o = nc.dram_tensor("dbg_o", [128, 32768], BF16, kind="ExternalOutput").ap()
        dbg_small = nc.dram_tensor("dbg_small", [128, 64], F32, kind="ExternalOutput").ap()
        dbg_qkv = nc.dram_tensor("dbg_qkv", [128, 6144], BF16, kind="ExternalOutput").ap()
        dbg_fin = nc.dram_tensor("dbg_fin", [128, 32768], BF16, kind="ExternalOutput").ap()
        dbg_wdq = nc.dram_tensor("dbg_wdq", [128, 32768], BF16, kind="ExternalOutput").ap()
        dbg_hact = nc.dram_tensor("dbg_hact", [128, 8192], BF16, kind="ExternalOutput").ap()
        dbg_ssq = nc.dram_tensor("dbg_ssq", [128, 80], F32, kind="ExternalOutput").ap()
        dbg_p5 = nc.dram_tensor("dbg_p5", [128, 8192], BF16, kind="ExternalOutput").ap()

    dwpq = nc.dram_tensor("wpq_s", [NFC, 128, DM], BF16, kind="Internal").ap()
    dx2 = nc.dram_tensor("x2_s", [NT, 128, DM], F32, kind="Internal").ap()
    dffn = nc.dram_tensor("ffn_s", [DM, DM], BF16, kind="Internal").ap()
    dhactT = nc.dram_tensor("hactT_s", [NT, 128, NIC, 128], BF16,
                            kind="ExternalOutput" if debug else "Internal").ap()
    dyh = nc.dram_tensor("yh_s", [NT, 128, 1024], F32, kind="Internal").ap()
    ddnq = nc.dram_tensor("dnq_s", [2, 28, 128, 1024], BF16, kind="Internal").ap()

    sb = lambda n, s, d, **kw: ctx.enter_context(nc.sbuf_tensor(n, s, d, **kw))
    big1 = sb("big1", [128, 32768], BF16, is_transpose_dest=True)
    big2 = sb("big2", [128, 32768], BF16, is_transpose_dest=True)
    pool3 = sb("pool3", [128, 32768], BF16, is_transpose_dest=True)

    def v3(a, b, dt=BF16):
        ap = pool3[:, a:b]
        return ap if dt == BF16 else ap.bitcast(dt)

    wraw = [v3(0, 4096, F32), v3(4096, 8192, F32)]
    wq = pool3[:, 8192:14336]
    wq3 = wq.rearrange("p (t3 fc j) -> p t3 fc j", t3=3, j=128)
    qlbuf = pool3[:, 14336:16384]
    qkvsb = [pool3[:, 16384:22528], pool3[:, 22528:28672]]
    qkv3 = [q.rearrange("p (t3 t) -> p t3 t", t3=3) for q in qkvsb]
    passA = [v3(16384 + 4096 * i, 16384 + 4096 * (i + 1), F32) for i in range(3)]
    pstage = pool3[:, 28672:30720]
    w1bc = pool3[:, 30720:32768]
    ht = [pool3[:, 8192:10240], pool3[:, 10240:12288]]
    xrl = [v3(0, 4096, F32), v3(4096, 8192, F32)]
    oTt = [pool3[:, 8192:10240], pool3[:, 10240:12288]]
    fint = [pool3[:, 12288:14336], pool3[:, 14336:16384]]
    x2t = [v3(16384, 20480, F32), v3(20480, 24576, F32)]
    w1o = pool3[:, 24576:26624]
    finTt = pool3[:, 24576:26624]
    dmybf = pool3[:, 26624:28672]
    hact = [pool3[:, 0:8192], pool3[:, 8192:16384]]
    hactT = [pool3[:, 24576:32768]]
    wu16 = [v3(16384 + 512 * i, 16384 + 512 * (i + 1), U16) for i in range(4)]
    sbct = [pool3[:, 18432 + 512 * i: 18432 + 512 * (i + 1)] for i in range(4)]
    ntl = [v3(20480 + 512 * i, 20480 + 512 * (i + 1), U16) for i in range(2)]
    atl = [v3(21504 + 512 * i, 21504 + 512 * (i + 1), U16) for i in range(2)]
    btl = [v3(22528 + 512 * i, 22528 + 512 * (i + 1), U16) for i in range(2)]
    sg = [pool3[:, 23552 + 512 * i: 23552 + 512 * (i + 1)] for i in range(2)]
    dmy512 = pool3[:, 24064:24576]
    hA = [pool3[:, 0:5632], pool3[:, 5632:11264]]
    y0t = v3(11264, 13312, F32)
    y1t = v3(13312, 15360, F32)
    x2rl = v3(25600, 29696, F32)

    constt = sb("constt", [128, 8], F32)
    _cvals = [EPS32, 1e-6, -1.0, -0.5, SCALE_O, 1e-5, 1.0 / DM, 16384.0]
    for _i, _v in enumerate(_cvals):
        nc.const_aps.register(F32, float(_v), constt[:, _i:_i + 1])

    mk = [sb(f"mk{i}", [128, 128], BF16) for i in range(2)]
    ql = [sb(f"ql{i}", [128, 128], BF16) for i in range(2)]
    kl = [sb(f"kl{i}", [128, 128], BF16) for i in range(2)]
    vtk = [sb(f"vtk{i}", [128, 128], BF16) for i in range(2)]
    Sst = sb("Sst", [128, 128], F32)
    Sbf = [sb(f"Sbf{i}", [128, 128], BF16) for i in range(2)]
    masks = sb("masks", [128, NH * 128], BF16)
    dnstg = [masks[:, 512 * i: 512 * (i + 1)] for i in range(2)]
    lamb = [sb(f"lamb{i}", [128, 128], BF16) for i in range(2)]
    gcol = sb("gcolS", [128, NFC], F32)
    w1col = sb("w1colS", [128, NFC], F32)
    w2col = sb("w2colS", [128, NFC], F32)
    w12col = sb("w12colS", [128, NFC], F32)
    srep = sb("srep", [128, NH], F32)
    negs = sb("negs", [128, NH], F32)
    lamrev = sb("lamrev", [128, NH], F32)
    lamT = sb("lamT", [128, NH], F32)
    tcolt = sb("tcolt", [128, 128], F32)
    diff = sb("diff", [128, 128], F32)
    tri = sb("tri", [128, 128], F32)
    ident = sb("identS", [128, 128], BF16)
    trev = sb("trev", [128, 1], F32)
    c128 = sb("c128", [128, 1], F32)
    pcol = sb("pcol", [128, 1], I16)
    shiftc = sb("shiftc", [128, 1], I16)
    ones1 = sb("ones1", [1, 128], F32)
    onescol = sb("onescol", [128, 1], F32)
    absacc = sb("absacc", [128, 64], F32)
    redt = sb("redt", [128, 2], F32)
    mrow = sb("mrow", [1, 8], F32)
    m2row = sb("m2row", [1, 8], F32)
    brow = sb("brow", [1, 8], F32)
    scol = sb("scolS", [128, 8], F32)
    sumc = sb("sumc", [128, NT], F32)
    sqc = sb("sqc", [128, NT], F32)
    c1c = sb("c1c", [128, NT], F32)
    b1c = sb("b1c", [128, NT], F32)
    tnyA = sb("tnyA", [128, NT], F32)
    tnyB = sb("tnyB", [128, NT], F32)
    tnyC = sb("tnyC", [128, NT], F32)
    tnyD = sb("tnyD", [128, NT], F32)
    s1c = sb("s1c", [128, NT], F32)
    s2c = sb("s2c", [128, NT], F32)
    betac = sb("betac", [128, NT], F32)
    ssqs = sb("ssqs", [128, 4 * NT], F32)
    r3c = sb("r3c", [128, NT], F32)
    sum2 = sb("sum2", [128, NT], F32)
    sq2 = sb("sq2", [128, NT], F32)
    c2c = sb("c2c", [128, NT], F32)
    b2c = sb("b2c", [128, NT], F32)

    hTr = big1[:].rearrange("p (fc t) -> p fc t", t=2048)
    wpqv = big1[:].rearrange("p (fc o) -> p fc o", o=2048)
    finTv = big1[:].rearrange("p (fc t) -> p fc t", t=2048)
    o3 = big2[:].rearrange("p (c f) -> p c f", f=2048)
    wdqv = big2[:].rearrange("p (w h fc j) -> p w h fc j", w=2, h=2, j=512)
    wdn1 = big1[:].rearrange("p (ic o) -> p ic o", o=1024)
    wdn2 = big2[:, 0:12288].rearrange("p (ic o) -> p ic o", o=1024)
    dmyP1 = pool3[:, 28672:30720]
    mskP0 = pool3[:, 14336:14592].bitcast(F32)[:, 0:128]

    ps = lambda n: ctx.enter_context(nc.psum_tensor(n, [128, 512], F32))
    pA0, pA1, pB, pC, pD, pE, pF, pG = [ps(f"ps{i}") for i in range(8)]

    sy = Sem(ctx, nc, "sy")
    sc = Sem(ctx, nc, "sc")
    scb = Sem(ctx, nc, "scb")
    gq = Sem(ctx, nc, "gq")
    gqb = Sem(ctx, nc, "gqb")
    wr_s = [Sem(ctx, nc, f"wr{i}") for i in range(2)]
    pa_s = [Sem(ctx, nc, f"pa{i}") for i in range(3)]
    wu_s = [Sem(ctx, nc, f"wu{i}") for i in range(4)]
    sb_s = [Sem(ctx, nc, f"sb{i}") for i in range(4)]
    oT_s = [Sem(ctx, nc, f"oT{i}") for i in range(2)]
    hT_s = [Sem(ctx, nc, f"hT{i}") for i in range(2)]
    ha_s = [Sem(ctx, nc, f"ha{i}") for i in range(2)]
    ft_s = Sem(ctx, nc, "ft")
    hx_s = Sem(ctx, nc, "hx")
    wpq_st = Sem(ctx, nc, "wpqst")
    wpq_ld = Sem(ctx, nc, "wpqld")
    fl_s = Sem(ctx, nc, "fl")
    xy_s = Sem(ctx, nc, "xy")
    x2_s = Sem(ctx, nc, "x2s")
    fs_s = Sem(ctx, nc, "fss")
    hs_s = Sem(ctx, nc, "hss")
    ys_s = Sem(ctx, nc, "yss")
    yo_s = Sem(ctx, nc, "yos")
    dbg_s = Sem(ctx, nc, "dbgs")
    dnq_st = Sem(ctx, nc, "dnqst")
    dnq_ld = Sem(ctx, nc, "dnqld")
    pe = Sem(ctx, nc, "pe")
    dv = Sem(ctx, nc, "dv")
    ac = Sem(ctx, nc, "ac")
    po = Sem(ctx, nc, "po")
    P = Plan()
    RING = {"sy": "sync", "sc": "scalar", "gq": "gpsimd"}
    RSEM = {"sy": sy, "sc": sc, "gq": gq}

    def dma(ring, dst, src, transpose=False, slow=False, sem=None):
        if sem is None:
            sem = RSEM[ring]
        v = sem.bump(16)
        kw = {}
        if transpose:
            kw["transpose"] = True
        if slow:
            kw["allow_slow_non_contiguous"] = True
        P.on(RING[ring], lambda e, d=dst, s=src, k=dict(kw): e.dma_start(out=d, in_=s, **k).then_inc(sem.h, 16))
        return v

    ESEM = {"vector": dv, "scalar": ac, "gpsimd": po, "tensor": pe}

    def op(eng, fn):
        sem = ESEM[eng]
        v = sem.bump(1)
        P.on(eng, lambda e, f=fn: f(e).then_inc(sem.h, 1))
        return v

    def selfsem(eng):
        v = op(eng, lambda e: e.nop())
        P.wait(eng, ESEM[eng], v)
        return v

    dma("gq", gcol[:], dgcol)
    dma("gq", w1col[:], dw1col)
    dma("gq", w2col[:], dw2col)
    dma("gq", srep[:], dsrow.broadcast_to([128, NH]), sem=gqb)
    v_w1bc = dma("gq", w1bc, dw1row.broadcast_to([128, DM]), sem=gqb)
    op("gpsimd", lambda e: e.iota(out=diff[:], pattern=[[1, 128]], base=0,
                                  channel_multiplier=-1, allow_small_or_imprecise_dtypes=True))
    op("gpsimd", lambda e: e.iota(out=tcolt[:], pattern=[[1, 128]], base=0,
                                  channel_multiplier=0, allow_small_or_imprecise_dtypes=True))
    op("gpsimd", lambda e: e.iota(out=trev[:], pattern=[[0, 1]], base=128,
                                  channel_multiplier=-1, allow_small_or_imprecise_dtypes=True))
    op("gpsimd", lambda e: e.iota(out=pcol[:], pattern=[[0, 1]], base=0,
                                  channel_multiplier=1))
    v_iota = op("gpsimd", lambda e: e.memset(c128[:], 128.0))
    P.wait("vector", po, v_iota)
    P.wait("vector", gqb, v_w1bc)
    op("vector", lambda e: e.tensor_scalar(out=tri[:], in0=diff[:], scalar1=0.0, scalar2=None, op0=ALU.is_ge))
    op("vector", lambda e: e.tensor_scalar(out=ident[:], in0=diff[:], scalar1=0.0, scalar2=None, op0=ALU.is_equal))
    op("vector", lambda e: e.tensor_scalar(out=negs[:], in0=srep[:], scalar1=-1.0, scalar2=None, op0=ALU.mult))
    op("vector", lambda e: e.tensor_tensor(out=w12col[:], in0=w1col[:], in1=w2col[:], op=ALU.mult))
    op("vector", lambda e: e.tensor_scalar(out=shiftc[:], in0=pcol[:], scalar1=3, scalar2=2,
                                           op0=ALU.bitwise_and, op1=ALU.logical_shift_left))
    op("vector", lambda e: e.memset(ones1[:], 1.0))
    for _i, _v in enumerate(_cvals):
        op("vector", lambda e, _i=_i, _v=_v: e.memset(constt[:, _i:_i + 1], float(_v)))
    v_dvsmall = selfsem("vector")
    P.wait("scalar", dv, v_dvsmall)
    v_mtt = 0
    for h in range(NH):
        op("scalar", lambda e, h=h: e.activation(out=lamrev[:, h:h + 1], in_=trev[:], func=AF.Exp,
                                                 scale=negs[:, h:h + 1]))
        op("scalar", lambda e, h=h: e.activation(out=lamT[:, h:h + 1], in_=c128[:], func=AF.Exp,
                                                 scale=negs[:, h:h + 1]))
        P.wait("scalar", dv, v_mtt)
        v_me = op("scalar", lambda e, h=h: e.activation(out=mskP0, in_=diff[:], func=AF.Exp,
                                                        scale=negs[:, h:h + 1]))
        P.wait("vector", ac, v_me)
        v_mtt = op("vector", lambda e, h=h: e.tensor_tensor(out=masks[:, h * 128:(h + 1) * 128],
                                                            in0=mskP0, in1=tri[:], op=ALU.mult))
    v_masks = dv.n
    v_lamtabs = ac.n

    qkvT3 = dqkvT.rearrange("(fc p) c -> fc p c", p=128)
    projT3 = dprojT.rearrange("(fc p) c -> fc p c", p=128)
    vA_red = {}
    PA_RING = {0: "gq", 1: "sy", 2: "sc"}
    PA_ENG = {"gq": "gpsimd", "sy": "sync", "sc": "scalar"}
    idx = 0
    srcs = [qkvT3[fc, :, j * 2048:(j + 1) * 2048] for fc in range(NFC) for j in range(3)]
    srcs += [projT3[fc] for fc in range(NFC)]
    vA_eng = {}
    for src in srcs:
        s3 = idx % 3
        ring = PA_RING[s3]
        eng = "vector" if (idx % 2 == 0) else "scalar"
        if idx >= 3:
            peng, pv = vA_eng[idx - 3], vA_red[idx - 3]
            P.wait(PA_ENG[ring], dv if peng == "vector" else ac, pv)
        vld = dma(ring, passA[s3], src, sem=pa_s[s3])
        P.wait(eng, pa_s[s3], vld)
        if eng == "vector":
            vA_red[idx] = op("vector", lambda e, s3=s3, i=idx: e.tensor_reduce(
                out=absacc[:, i:i + 1], in_=passA[s3], axis=AX.X, op=ALU.add, apply_absolute_value=True))
        else:
            vA_red[idx] = op("scalar", lambda e, s3=s3, i=idx: e.activation(
                out=dmyP1, in_=passA[s3], func=AF.Abs, accum_out=absacc[:, i:i + 1]))
        vA_eng[idx] = eng
        idx += 1
    v_passA_ac = ac.n
    P.wait("vector", ac, v_passA_ac)
    op("vector", lambda e: e.tensor_reduce(out=redt[:, 0:1], in_=absacc[:, 0:48], axis=AX.X, op=ALU.add))
    v_red = op("vector", lambda e: e.tensor_reduce(out=redt[:, 1:2], in_=absacc[:, 48:64], axis=AX.X, op=ALU.add))
    op("vector", lambda e: e.memset(onescol[:], 1.0))
    v_red = selfsem("vector")
    P.wait("tensor", dv, v_red)
    op("tensor", lambda e: e.matmul(pG[0:1, 0:1], redt[:, 0:1], onescol[:], start=True, stop=True))
    v_mmr = op("tensor", lambda e: e.matmul(pG[0:1, 1:2], redt[:, 1:2], onescol[:], start=True, stop=True))
    P.wait("vector", pe, v_mmr)
    op("vector", lambda e: e.tensor_copy(out=mrow[0:1, 0:2], in_=pG[0:1, 0:2]))
    selfsem("vector")
    op("vector", lambda e: e.tensor_scalar(out=mrow[0:1, 2:3], in0=mrow[0:1, 0:1],
                                           scalar1=1.0 / QN, scalar2=1e-5, op0=ALU.mult, op1=ALU.max))
    op("vector", lambda e: e.tensor_scalar(out=mrow[0:1, 3:4], in0=mrow[0:1, 1:2],
                                           scalar1=1.0 / PN, scalar2=1e-5, op0=ALU.mult, op1=ALU.max))
    selfsem("vector")
    op("vector", lambda e: e.tensor_scalar(out=brow[0:1, 0:1], in0=mrow[0:1, 2:3], scalar1=0.5, scalar2=None, op0=ALU.mult))
    op("vector", lambda e: e.tensor_scalar(out=brow[0:1, 1:2], in0=mrow[0:1, 2:3], scalar1=-0.5, scalar2=None, op0=ALU.mult))
    op("vector", lambda e: e.tensor_scalar(out=brow[0:1, 2:3], in0=mrow[0:1, 3:4], scalar1=0.5, scalar2=None, op0=ALU.mult))
    op("vector", lambda e: e.tensor_scalar(out=brow[0:1, 3:4], in0=mrow[0:1, 3:4], scalar1=-0.5, scalar2=None, op0=ALU.mult))
    op("vector", lambda e: e.tensor_tensor(out=m2row[0:1, 0:1], in0=mrow[0:1, 2:3], in1=mrow[0:1, 2:3], op=ALU.mult))
    selfsem("vector")
    op("vector", lambda e: e.tensor_tensor(out=m2row[0:1, 1:2], in0=m2row[0:1, 0:1], in1=mrow[0:1, 2:3], op=ALU.mult))
    selfsem("vector")
    op("vector", lambda e: e.tensor_tensor(out=m2row[0:1, 2:3], in0=m2row[0:1, 1:2], in1=mrow[0:1, 3:4], op=ALU.mult))
    op("vector", lambda e: e.tensor_tensor(out=m2row[0:1, 3:4], in0=m2row[0:1, 1:2], in1=m2row[0:1, 1:2], op=ALU.mult))
    v_mf = selfsem("vector")
    op("vector", lambda e: e.tensor_scalar(out=brow[0:1, 5:6], in0=m2row[0:1, 3:4],
                                           scalar1=1.0 / DM, scalar2=None, op0=ALU.mult))
    P.wait("scalar", dv, v_mf)
    v_lnmf = op("scalar", lambda e: e.activation(out=brow[0:1, 4:5], in_=m2row[0:1, 2:3], func=AF.Ln))
    P.wait("vector", ac, v_lnmf)
    v_brow = selfsem("vector")
    P.wait("tensor", dv, v_brow)
    v_bc = op("tensor", lambda e: e.matmul(pG[:, 0:6], ones1[:], brow[0:1, 0:6], start=True, stop=True))
    P.wait("vector", pe, v_bc)
    op("vector", lambda e: e.tensor_copy(out=scol[:, 0:6], in_=pG[:, 0:6]))
    v_scol = selfsem("vector")

    v_xload = [0] * NT
    v_hact1 = [0] * NT
    v_htrans = [0] * NT
    v_st1 = [0] * NT
    v_xstash_by = [0] * NT
    for tt in range(NT):
        s2 = tt % 2
        if tt >= 2:
            P.wait("scalar", ac, v_st1[tt - 2])
            P.wait("scalar", dv, v_xstash_by[tt - 2])
        v_xload[tt] = dma("sc", wraw[s2], dx[tt * 128:(tt + 1) * 128, :], sem=wr_s[s2])
        P.wait("scalar", wr_s[s2], v_xload[tt])
        op("scalar", lambda e, s2=s2, tt=tt: e.activation(out=dmyP1, in_=wraw[s2], func=AF.Copy,
                                                          accum_out=sumc[:, tt:tt + 1]))
        v_st1[tt] = op("scalar", lambda e, s2=s2, tt=tt: e.activation(out=dmyP1, in_=wraw[s2], func=AF.Square,
                                                                      accum_out=sqc[:, tt:tt + 1]))
        P.wait("vector", wr_s[s2], v_xload[tt])
        v_xstash_by[tt] = op("vector", lambda e, s2=s2, tt=tt: e.tensor_copy(out=o3[:, tt, :], in_=wraw[s2]))
    P.wait("vector", ac, v_st1[NT - 1])
    op("vector", lambda e: e.tensor_scalar(out=tnyA[:], in0=sumc[:], scalar1=1.0 / DM, scalar2=None, op0=ALU.mult))
    op("vector", lambda e: e.tensor_scalar(out=tnyB[:], in0=sqc[:], scalar1=1.0 / DM, scalar2=None, op0=ALU.mult))
    selfsem("vector")
    op("vector", lambda e: e.tensor_tensor(out=tnyC[:], in0=tnyA[:], in1=tnyA[:], op=ALU.mult))
    selfsem("vector")
    op("vector", lambda e: e.tensor_tensor(out=tnyD[:], in0=tnyB[:], in1=tnyC[:], op=ALU.subtract))
    selfsem("vector")
    v_t1 = op("vector", lambda e: e.tensor_scalar(out=tnyB[:], in0=tnyD[:], scalar1=1e-5, scalar2=None, op0=ALU.add))
    P.wait("scalar", dv, v_t1)
    op("scalar", lambda e: e.activation(out=tnyC[:], in_=tnyB[:], func=AF.Ln))
    selfsem("scalar")
    v_e1 = op("scalar", lambda e: e.activation(out=sqc[:], in_=tnyC[:], func=AF.Exp, scale=-1.0))
    P.wait("vector", ac, v_e1)
    v_q = op("vector", lambda e: e.scalar_tensor_tensor(out=tnyB[:], in0=tnyD[:], scalar=1.0, in1=sqc[:],
                                                        op0=ALU.mult, op1=ALU.mult))
    P.wait("scalar", dv, v_q)
    op("scalar", lambda e: e.activation(out=tnyB[:], in_=tnyB[:], func=AF.Ln, bias=EPS32))
    selfsem("scalar")
    v_l2 = ac.n
    P.wait("vector", ac, v_l2)
    v_ls = op("vector", lambda e: e.tensor_tensor(out=tnyD[:], in0=tnyC[:], in1=tnyB[:], op=ALU.add))
    P.wait("scalar", dv, v_ls)
    v_c1 = op("scalar", lambda e: e.activation(out=c1c[:], in_=tnyD[:], func=AF.Exp, scale=-0.5))
    P.wait("vector", ac, v_c1)
    op("vector", lambda e: e.tensor_scalar(out=b1c[:], in0=tnyA[:], scalar1=-1.0, scalar2=None, op0=ALU.mult))
    v_p1b = selfsem("vector")
    for tt in range(NT):
        s2 = tt % 2
        if tt >= 2:
            P.wait("vector", hT_s[s2], v_htrans[tt - 2])
        v_hact1[tt] = op("vector", lambda e, s2=s2, tt=tt: e.tensor_scalar(
            out=ht[s2], in0=o3[:, tt, :], scalar1=b1c[:, tt:tt + 1], scalar2=c1c[:, tt:tt + 1],
            op0=ALU.add, op1=ALU.mult))
        P.wait("sync", dv, v_hact1[tt])
        v_htrans[tt] = dma("sy", hTr[:, :, tt * 128:(tt + 1) * 128],
                           ht[s2].rearrange("p (fc j) -> p fc j", j=128), transpose=True, sem=hT_s[s2])
    v_hT_all2 = (hT_s[0], v_htrans[NT - 2]); v_hT_all = v_htrans[NT - 1]
    P.wait("vector", hT_s[0], v_htrans[NT - 2])
    P.wait("vector", hT_s[1], v_htrans[NT - 1])
    for fc in range(NFC):
        op("vector", lambda e, fc=fc: e.tensor_scalar(out=hTr[:, fc, :], in0=hTr[:, fc, :],
                                                      scalar1=gcol[:, fc:fc + 1], scalar2=None, op0=ALU.mult))
    v_hT_ready = dv.n
    if debug:
        P.wait("scalar", dv, v_hT_ready)
        dma("sc", dbg_hT, big1[:], sem=dbg_s)

    v_quant = {}
    v_qkvmm_done = {-1: 0}
    v_qkvevac_done = {}
    v_scan_done = {-2: 0, -1: 0}
    v_ql_last = {-2: 0, -1: 0}
    g_masked = 0
    g_klam = 0
    g_vevac = 0
    g_oevac = 0
    g_supd = 0
    g_sbf_prev = 0
    g_cross = {}
    g_intra = {}
    g_kv = {}
    cglob = 0
    scan_tmp = None
    g_masked2 = {}
    bankA_free = [0, 0]
    v_projq_store = [0] * NFC
    pieceidx = 0
    v_piece_red = {}
    v_lamb = {}
    dqkvTp = dqkvT.rearrange("(fc p) c -> p fc c", p=128)

    def emit_quant_steps(h):
        nonlocal pieceidx
        steps = []
        for t3 in range(3):
            pslot = pieceidx % 2
            pidx = pieceidx
            if pieceidx >= 2:
                P.wait_lazy("gpsimd", dv, lambda k=pieceidx - 2: v_piece_red[k])
            else:
                P.wait("gpsimd", dv, v_hact1[NT - 1])
            cols = t3 * 2048 + h * 128
            vld = dma("gq", wraw[pslot].rearrange("p (fc j) -> p fc j", j=128),
                      dqkvTp[:, :, cols:cols + 128], sem=wr_s[pslot])
            pieceidx += 1

            def step1(pslot=pslot, pidx=pidx, vld=vld, t3=t3):
                P.wait("vector", wr_s[pslot], vld)
                if pidx == 0:
                    P.wait("vector", dv, v_scol)
                    P.wait("vector", hT_s[0], v_htrans[NT - 2])
                    P.wait("vector", hT_s[1], v_htrans[NT - 1])
                P.wait("vector", pe, v_scan_done[h - 2])
                wqr = wq[:, t3 * 2048:(t3 + 1) * 2048]
                op("vector", lambda e, pslot=pslot, wqr=wqr: e.tensor_scalar(
                    out=wqr, in0=wraw[pslot], scalar1=scol[:, 0:1], scalar2=None, op0=ALU.is_gt))

            def step2(pslot=pslot, pidx=pidx, t3=t3, h=h):
                wqr = wq[:, t3 * 2048:(t3 + 1) * 2048]
                v_piece_red[pidx] = op("vector", lambda e, pslot=pslot, wqr=wqr: e.scalar_tensor_tensor(
                    out=wqr, in0=wraw[pslot], scalar=scol[:, 1:2], in1=wqr,
                    op0=ALU.is_lt, op1=ALU.subtract))
                v_quant[(h, t3)] = v_piece_red[pidx]
            steps.append(step1)
            steps.append(step2)
        return steps

    def emit_quant(h):
        for st_ in emit_quant_steps(h):
            st_()

    def emit_lamb(h):
        s2 = h % 2
        P.wait("scalar", dv, v_ql_last[h - 2])
        v_lamb[h] = op("scalar", lambda e, h=h, s2=s2: e.activation(
            out=lamb[s2][:], in_=tcolt[:], func=AF.Exp, scale=negs[:, h:h + 1]))

    first_mm = [True]

    def emit_qkv_group(h, g):
        s2 = h % 2
        t3, tb = g // 4, g % 4
        gb = g % 2
        pbank = [pA0, pA1][gb]
        P.wait("tensor", dv, v_quant[(h, t3)])
        if first_mm[0]:
            P.wait("tensor", dv, v_hT_ready)
            first_mm[0] = False
        if isinstance(bankA_free[gb], tuple):
            P.wait("tensor", ac, bankA_free[gb][1])
        else:
            P.wait("tensor", dv, bankA_free[gb])
        vmm = 0
        for fc in range(NFC):
            vmm = op("tensor", lambda e, t3=t3, tb=tb, fc=fc, pbank=pbank: e.matmul(
                pbank[:], wq3[:, t3, fc, :], hTr[:, fc, tb * 512:(tb + 1) * 512],
                start=(fc == 0), stop=(fc == NFC - 1)))
        P.wait("scalar", pe, vmm)
        P.wait("scalar", pe, v_scan_done[h - 2])
        bankA_free[gb] = ("ac", op("scalar", lambda e, s2=s2, t3=t3, tb=tb, pbank=pbank: e.activation(
            out=qkv3[s2][:, t3, tb * 512:(tb + 1) * 512], in_=pbank[:], func=AF.Copy)))

    def emit_scan_pre(h, c):
        nonlocal g_masked, g_klam, g_vevac, scan_tmp
        s2 = h % 2
        cs = c % 2
        qTc = qkv3[s2][:, 0, c * 128:(c + 1) * 128]
        kTc = qkv3[s2][:, 1, c * 128:(c + 1) * 128]
        vTc = qkv3[s2][:, 2, c * 128:(c + 1) * 128]
        if c == 0:
            P.wait("tensor", ac, v_qkvevac_done[h])
        sbank = [pB, pG][cglob % 2]
        P.wait("tensor", dv, g_masked2.get(cglob - 2, 0))
        v_sc_mm = op("tensor", lambda e, kTc=kTc, qTc=qTc, sbank=sbank: e.matmul(
            sbank[:, 0:128], kTc, qTc, start=True, stop=True))
        P.wait("tensor", ac, g_klam)
        v_kt = op("tensor", lambda e, kTc=kTc: e.matmul(pE[:, 0:128], kTc, ident[:], start=True, stop=True))
        P.wait("tensor", dv, g_vevac)
        v_vt = op("tensor", lambda e, vTc=vTc: e.matmul(pF[:, 0:128], vTc, ident[:], start=True, stop=True))
        P.wait("vector", pe, v_sc_mm)
        if cglob == 0:
            P.wait("vector", dv, v_masks)
        P.wait("vector", pe, g_intra.get(cglob - 2, 0))
        g_masked = op("vector", lambda e, cs=cs, h=h, sbank=sbank: e.tensor_tensor(
            out=mk[cs][:], in0=sbank[:, 0:128], in1=masks[:, h * 128:(h + 1) * 128], op=ALU.mult))
        g_masked2[cglob] = g_masked
        P.wait("vector", pe, v_vt)
        P.wait("vector", pe, g_kv.get(cglob - 2, 0))
        g_vevac = op("vector", lambda e, cs=cs: e.tensor_copy(out=vtk[cs][:], in_=pF[:, 0:128]))
        P.wait("scalar", pe, v_kt)
        if cglob == 0:
            P.wait("scalar", ac, v_lamtabs)
        P.wait("scalar", pe, g_kv.get(cglob - 2, 0))
        g_klam = op("scalar", lambda e, cs=cs, h=h: e.activation(
            out=kl[cs][:], in_=pE[:, 0:128], func=AF.Copy, scale=lamrev[:, h:h + 1]))
        scan_tmp = (v_kt, v_vt)

    def emit_scan_post(h, c):
        nonlocal cglob, g_masked, g_klam, g_vevac, g_oevac, g_supd, g_sbf_prev
        s2 = h % 2
        cs = c % 2
        P.wait("tensor", dv, g_masked)
        P.wait("tensor", dv, g_vevac)
        P.wait("tensor", ac, g_oevac)
        v_intra = op("tensor", lambda e, cs=cs, c=c: e.matmul(
            pC[:, 0:128], mk[cs][:], vtk[cs][:], start=True, stop=(c == 0)))
        g_intra[cglob] = v_intra
        if c > 0:
            P.wait("tensor", dv, v_qlb[h])
            P.wait("tensor", dv, g_sbf_prev)
            v_cross = op("tensor", lambda e, cs=cs, c=c: e.matmul(
                pC[:, 0:128], qlbuf[:, c * 128:(c + 1) * 128], Sbf[(cs + 1) % 2][:], start=False, stop=True))
        else:
            v_cross = v_intra
        g_cross[cglob] = v_cross
        P.wait("tensor", ac, g_klam)
        P.wait("tensor", dv, g_supd)
        v_kv = op("tensor", lambda e, cs=cs: e.matmul(pD[:, 0:128], kl[cs][:], vtk[cs][:], start=True, stop=True))
        g_kv[cglob] = v_kv
        P.wait("scalar", pe, v_cross)
        g_oevac = op("scalar", lambda e, c=c, h=h: e.activation(
            out=o3[:, c, h * 128:(h + 1) * 128], in_=pC[:, 0:128], func=AF.Copy, scale=SCALE_O))
        P.wait("vector", pe, v_kv)
        P.wait("vector", pe, g_cross.get(cglob - 1, 0))
        if c == 0:
            g_sbf_prev = op("vector", lambda e, cs=cs: e.tensor_copy(out=Sbf[cs][:], in_=pD[:, 0:128]))
            g_supd = op("vector", lambda e: e.tensor_copy(out=Sst[:], in_=pD[:, 0:128]))
        else:
            g_sbf_prev = op("vector", lambda e, cs=cs, h=h: e.scalar_tensor_tensor(
                out=Sbf[cs][:], in0=Sst[:], scalar=lamT[:, h:h + 1], in1=pD[:, 0:128],
                op0=ALU.mult, op1=ALU.add))
            g_supd = op("vector", lambda e, h=h: e.scalar_tensor_tensor(
                out=Sst[:], in0=Sst[:], scalar=lamT[:, h:h + 1], in1=pD[:, 0:128],
                op0=ALU.mult, op1=ALU.add))
        cglob += 1

    def emit_projq(h):
        nonlocal pieceidx
        for k in range(2):
            fc = (h - 8) * 2 + k
            pslot = pieceidx % 2
            P.wait("gpsimd", dv, v_piece_red[pieceidx - 2])
            vld = dma("gq", wraw[pslot], projT3[fc], sem=wr_s[pslot])
            P.wait("vector", wr_s[pslot], vld)
            op("vector", lambda e, pslot=pslot: e.tensor_scalar(
                out=pstage, in0=wraw[pslot], scalar1=scol[:, 2:3], scalar2=None, op0=ALU.is_gt))
            op("vector", lambda e, pslot=pslot: e.scalar_tensor_tensor(
                out=pstage, in0=wraw[pslot], scalar=scol[:, 3:4], in1=pstage,
                op0=ALU.is_lt, op1=ALU.subtract))
            v_piece_red[pieceidx] = op("vector", lambda e, fc=fc: e.tensor_scalar(
                out=pstage, in0=pstage, scalar1=w12col[:, fc:fc + 1], scalar2=None, op0=ALU.mult))
            pieceidx += 1
            P.wait("gpsimd", dv, v_piece_red[pieceidx - 1])
            v_projq_store[fc] = dma("gq", dwpq[fc], pstage, sem=wpq_st)

    emit_quant(0)
    emit_lamb(0)
    for g in range(12):
        emit_qkv_group(0, g)
    v_qkvevac_done[0] = ac.n
    v_scan_done[-1] = pe.n
    v_qlb = {}
    for h in range(NH):
        pendq = []
        if h + 1 < NH:
            pendq = emit_quant_steps(h + 1)
            emit_lamb(h + 1)
        P.wait("vector", ac, v_lamb[h])
        P.wait("vector", ac, v_qkvevac_done[h])
        P.wait("vector", pe, v_scan_done[h - 1])
        v_qlb[h] = op("vector", lambda e, s2=h % 2: e.tensor_tensor(
            out=qlbuf.rearrange("p (c t) -> p c t", t=128),
            in0=qkv3[s2][:, 0, :].rearrange("p (c t) -> p c t", t=128),
            in1=lamb[s2][:].unsqueeze(1).broadcast_to([128, NT, 128]),
            op=ALU.mult))
        for c in range(NT):
            emit_scan_pre(h, c)
            if c >= 4:
                for _ in range(2):
                    if pendq:
                        pendq.pop(0)()
            if h + 1 < NH and c >= 4:
                emit_qkv_group(h + 1, c - 4)
            emit_scan_post(h, c)
        if h + 1 < NH:
            v_qkvevac_done[h + 1] = ac.n
        v_scan_done[h] = pe.n
        v_ql_last[h] = dv.n
        if h >= 8:
            emit_projq(h)
    v_qkvmm_done[NH - 1] = pe.n
    v_o_all = ac.n
    v_lastpiece = v_piece_red[pieceidx - 1]

    v_beta = [0] * NT
    for tt in range(NT):
        if tt == 0:
            P.wait("vector", ac, v_o_all)
            P.wait("scalar", ac, v_o_all)
        v_w1o = op("vector", lambda e, tt=tt: e.tensor_tensor(out=w1o, in0=o3[:, tt, :], in1=w1bc, op=ALU.mult))
        op("scalar", lambda e, tt=tt: e.activation(out=dmybf, in_=o3[:, tt, :], func=AF.Square,
                                                   accum_out=s1c[:, tt:tt + 1]))
        P.wait("scalar", dv, v_w1o)
        op("scalar", lambda e, tt=tt: e.activation(out=dmybf, in_=w1o, func=AF.Square,
                                                   accum_out=s2c[:, tt:tt + 1]))
        op("scalar", lambda e, tt=tt: e.activation(out=tnyA[:, tt:tt + 1], in_=s1c[:, tt:tt + 1],
                                                   func=AF.Ln, scale=scol[:, 5:6], bias=1e-6))
        selfsem("scalar")
        op("scalar", lambda e, tt=tt: e.activation(out=tnyB[:, tt:tt + 1], in_=tnyA[:, tt:tt + 1],
                                                   func=AF.Exp, scale=-1.0))
        v_e2 = selfsem("scalar")
        P.wait("vector", ac, v_e2)
        op("vector", lambda e, tt=tt: e.scalar_tensor_tensor(out=tnyC[:, tt:tt + 1], in0=s2c[:, tt:tt + 1],
                                                             scalar=scol[:, 5:6], in1=tnyB[:, tt:tt + 1],
                                                             op0=ALU.mult, op1=ALU.mult))
        v_m2 = selfsem("vector")
        P.wait("scalar", dv, v_m2)
        op("scalar", lambda e, tt=tt: e.activation(out=tnyB[:, tt:tt + 1], in_=tnyC[:, tt:tt + 1],
                                                   func=AF.Ln, bias=EPS32))
        v_l2 = selfsem("scalar")
        P.wait("vector", ac, v_l2)
        op("vector", lambda e, tt=tt: e.tensor_tensor(out=tnyD[:, tt:tt + 1], in0=tnyA[:, tt:tt + 1],
                                                      in1=tnyB[:, tt:tt + 1], op=ALU.add))
        v_ls = selfsem("vector")
        P.wait("scalar", dv, v_ls)
        v_beta[tt] = op("scalar", lambda e, tt=tt: e.activation(out=betac[:, tt:tt + 1], in_=tnyD[:, tt:tt + 1],
                                                                func=AF.Exp, scale=-0.5, bias=scol[:, 4:5]))
    dv_p3pre = dv.n
    ac_p25 = ac.n
    if debug:
        P.wait("scalar", ac, ac.n)
        dma("sc", dbg_o, big2[:], sem=dbg_s)
        dma("sc", dbg_small[:, 0:8], scol[:], sem=dbg_s)
        dma("sc", dbg_small[:, 8:24], betac[:], sem=dbg_s)
        dma("sc", dbg_small[:, 24:40], c1c[:], sem=dbg_s)
        dma("sc", dbg_small[:, 40:56], s1c[:], sem=dbg_s)
        vdbg = dma("sc", dbg_small[0:1, 56:64], mrow[0:1, :], sem=dbg_s)
        P.wait("sync", dbg_s, vdbg)

    P.wait("gpsimd", pe, v_scan_done[NH - 2])
    v_wpq = [0] * NFC
    for fc in range(NFC):
        P.wait("gpsimd", wpq_st, v_projq_store[fc])
        v_wpq[fc] = dma("gq", wpqv[:, fc, :], dwpq[fc], sem=wpq_ld)
    v_wpq_all = v_wpq[NFC - 1]

    deq_dv = {}
    deq_po = {}
    kglob = [0]

    def dequant_tile(dst_ap, wsrc, ssrc):
        k = kglob[0]
        kglob[0] += 1
        b4 = k % 4
        b2r = k % 2
        if k >= 4:
            P.wait("sync", dv, deq_dv[k - 4])
            P.wait("gpsimd", dv, deq_dv[k - 4])
        vw = dma("sy", wu16[b4], wsrc, sem=wu_s[b4])
        vs = dma("gq", sbct[b4], ssrc, sem=sb_s[b4])
        P.wait("vector", wu_s[b4], vw)
        op("vector", lambda e, b4=b4, b2r=b2r: e.tensor_scalar(
            out=ntl[b2r], in0=wu16[b4], scalar1=shiftc[:, 0:1], scalar2=15,
            op0=ALU.logical_shift_right, op1=ALU.bitwise_and))
        op("vector", lambda e, b4=b4, b2r=b2r: e.tensor_scalar(
            out=atl[b2r], in0=ntl[b2r], scalar1=9, scalar2=0x0E00,
            op0=ALU.logical_shift_left, op1=ALU.bitwise_and))
        op("vector", lambda e, b4=b4, b2r=b2r: e.tensor_scalar(
            out=btl[b2r], in0=ntl[b2r], scalar1=12, scalar2=0x8000,
            op0=ALU.logical_shift_left, op1=ALU.bitwise_and))
        op("vector", lambda e, b4=b4, b2r=b2r: e.tensor_tensor(
            out=wu16[b4], in0=atl[b2r], in1=btl[b2r], op=ALU.bitwise_or))
        P.wait("vector", sb_s[b4], vs)
        deq_dv[k] = op("vector", lambda e, b4=b4, dst_ap=dst_ap: e.scalar_tensor_tensor(
            out=dst_ap, in0=wu16[b4].bitcast(F16), scalar=16384.0, in1=sbct[b4],
            op0=ALU.mult, op1=ALU.mult))
        return deq_dv[k]

    segB_pend = []
    segB_store = {}
    segB_k = [0]

    def emit_segB_tile(half, ic, ob):
        k = segB_k[0]
        segB_k[0] += 1
        stg = dnstg[k % 2]
        cols = half * 1024 + ob * 512
        if k >= 2:
            P.wait("vector", dnq_st, segB_store[k - 2])
        dequant_tile(stg, ddnw[ic * 128:(ic + 1) * 128, cols:cols + 512],
                     ddns[ic:ic + 1, cols:cols + 512].broadcast_to([128, 512]))
        P.wait("gpsimd", dv, deq_dv[kglob[0] - 1])
        segB_store[k] = dma("gq", ddnq[half, ic - 16, :, ob * 512:(ob + 1) * 512], stg, sem=dnq_st)

    for half in range(2):
        for ic in range(16, NIC):
            for ob in range(2):
                segB_pend.append((half, ic, ob))


    PB3R = [[pA0, pA1, pB, pC], [pD, pE, pF, pG]]
    v_projmm = {-2: 0, -1: 0}
    v_evac3 = {-2: 0, -1: 0}
    v_x2add = {-2: 0, -1: 0}
    v_x2dma = [0] * NT
    v_finstore = [0] * NT
    v_fint_done = {-2: 0, -1: 0}
    v_oT = 0
    v_oT_by_slot = {0: 0, 1: 0}
    v_oT_by_tt = {}

    def emit_oT(tt):
        nonlocal v_oT
        s2 = tt % 2
        if tt <= 1:
            P.wait("sync", ac, v_o_all)
        P.wait("sync", pe, v_projmm[tt - 2])
        v_oT = dma("sy", oTt[s2].rearrange("p (fc j) -> p fc j", j=128),
                   o3[:, tt, :].rearrange("p (fc j) -> p fc j", j=128), transpose=True, sem=oT_s[s2])
        v_oT_by_slot[s2] = v_oT
        v_oT_by_tt[tt] = v_oT

    emit_oT(0)
    emit_oT(1)
    for tt in range(NT):
        s2 = tt % 2
        if tt >= 2:
            P.wait("scalar", dv, v_x2add[tt - 2])
        else:
            P.wait("scalar", dv, v_lastpiece)
        v_xr = dma("sc", xrl[s2], dx[tt * 128:(tt + 1) * 128, :], sem=wr_s[s2])
        P.wait("tensor", oT_s[s2], v_oT_by_tt[tt])
        PB3 = PB3R[tt % 2]
        if tt == 1:
            P.wait("tensor", wpq_ld, v_wpq_all)
        if tt <= 1:
            P.wait("tensor", dv, dv_p3pre)
            P.wait("tensor", ac, v_o_all)
        pv = v_evac3.get(tt - 2, 0)
        if isinstance(pv, tuple):
            P.wait("tensor", ac, pv[0])
            P.wait("tensor", dv, pv[1])
        else:
            P.wait("tensor", ac, pv)
        vstop = [0] * 4
        for b in range(4):
            for fc in range(NFC):
                if tt == 0 and b == 0:
                    P.wait("tensor", wpq_ld, v_wpq[fc])
                vstop[b] = op("tensor", lambda e, b=b, fc=fc, s2=s2, PB3=PB3: e.matmul(
                    PB3[b][:], oTt[s2].rearrange("p (fc j) -> p fc j", j=128)[:, fc, :],
                    wpqv[:, fc, b * 512:(b + 1) * 512], start=(fc == 0), stop=(fc == NFC - 1)))
        v_projmm[tt] = pe.n
        if tt + 2 < NT:
            emit_oT(tt + 2)
        P.wait("scalar", ac, v_beta[tt])
        if tt >= 2:
            P.wait("scalar", x2_s, v_x2dma[tt - 2])
            if stage == "full":
                P.wait("scalar", fs_s, v_finstore[tt - 2])
        v_eva = 0
        v_evd = 0
        for b in range(4):
            if b < 2:
                P.wait("scalar", pe, vstop[b])
                v_eva = op("scalar", lambda e, b=b, s2=s2, tt=tt, PB3=PB3: e.activation(
                    out=x2t[s2][:, b * 512:(b + 1) * 512], in_=PB3[b][:], func=AF.Copy,
                    scale=betac[:, tt:tt + 1]))
            else:
                P.wait("vector", pe, vstop[b])
                v_evd = op("vector", lambda e, b=b, s2=s2, tt=tt, PB3=PB3: e.tensor_scalar(
                    out=x2t[s2][:, b * 512:(b + 1) * 512], in0=PB3[b][:], scalar1=betac[:, tt:tt + 1],
                    scalar2=None, op0=ALU.mult))
        v_evac3[tt] = (v_eva, v_evd)
        P.wait("vector", ac, v_eva)
        P.wait("vector", wr_s[s2], v_xr)
        v_x2add[tt] = op("vector", lambda e, s2=s2: e.tensor_tensor(out=x2t[s2], in0=x2t[s2], in1=xrl[s2], op=ALU.add))
        if stage == "attn":
            P.wait("scalar", dv, v_x2add[tt])
            v_x2dma[tt] = dma("sc", dy[tt * 128:(tt + 1) * 128, :], x2t[s2], sem=x2_s)
            continue
        P.wait("scalar", dv, v_x2add[tt])
        v_x2dma[tt] = dma("sc", dx2[tt], x2t[s2], sem=x2_s)
        op("vector", lambda e, s2=s2, tt=tt: e.tensor_reduce(out=sum2[:, tt:tt + 1], in_=x2t[s2],
                                                             axis=AX.X, op=ALU.add))
        v_st2a = op("scalar", lambda e, s2=s2, tt=tt: e.activation(out=dmybf, in_=x2t[s2], func=AF.Square,
                                                                   accum_out=sq2[:, tt:tt + 1]))
        P.wait("vector", ac, v_st2a)
        op("vector", lambda e, tt=tt: e.tensor_scalar(out=tnyA[:, tt:tt + 1], in0=sum2[:, tt:tt + 1],
                                                      scalar1=1.0 / DM, scalar2=None, op0=ALU.mult))
        op("vector", lambda e, tt=tt: e.tensor_scalar(out=tnyB[:, tt:tt + 1], in0=sq2[:, tt:tt + 1],
                                                      scalar1=1.0 / DM, scalar2=None, op0=ALU.mult))
        selfsem("vector")
        op("vector", lambda e, tt=tt: e.tensor_tensor(out=tnyC[:, tt:tt + 1], in0=tnyA[:, tt:tt + 1],
                                                      in1=tnyA[:, tt:tt + 1], op=ALU.mult))
        selfsem("vector")
        op("vector", lambda e, tt=tt: e.tensor_tensor(out=tnyD[:, tt:tt + 1], in0=tnyB[:, tt:tt + 1],
                                                      in1=tnyC[:, tt:tt + 1], op=ALU.subtract))
        v_var = selfsem("vector")
        P.wait("scalar", dv, v_var)
        op("scalar", lambda e, tt=tt: e.activation(out=tnyB[:, tt:tt + 1], in_=tnyD[:, tt:tt + 1],
                                                   func=AF.Ln, bias=1e-5))
        selfsem("scalar")
        op("scalar", lambda e, tt=tt: e.activation(out=c2c[:, tt:tt + 1], in_=tnyB[:, tt:tt + 1],
                                                   func=AF.Exp, scale=-0.5))
        v_c2 = selfsem("scalar")
        P.wait("vector", ac, v_c2)
        op("vector", lambda e, tt=tt: e.scalar_tensor_tensor(out=b2c[:, tt:tt + 1], in0=tnyA[:, tt:tt + 1],
                                                             scalar=-1.0, in1=c2c[:, tt:tt + 1],
                                                             op0=ALU.mult, op1=ALU.mult))
        v_b2 = selfsem("vector")
        P.wait("scalar", dv, v_b2)
        P.wait("scalar", ft_s, v_fint_done[tt - 2])
        v_fin = op("scalar", lambda e, s2=s2, tt=tt: e.activation(
            out=fint[s2], in_=x2t[s2], func=AF.Identity, scale=c2c[:, tt:tt + 1], bias=b2c[:, tt:tt + 1]))
        P.wait("sync", ac, v_fin)
        if tt == 0:
            P.wait("sync", ac, ac_p25)
            P.wait("sync", dv, dv_p3pre)
        if tt >= 1:
            P.wait("sync", fs_s, v_finstore[tt - 1])
        v_ft = dma("sy", finTt.rearrange("p (fc j) -> p fc j", j=128),
                   fint[s2].rearrange("p (fc j) -> p fc j", j=128), transpose=True, sem=ft_s)
        v_fint_done[tt] = v_ft
        P.wait("scalar", ft_s, v_ft)
        v_finstore[tt] = dma("sc", dffn[tt * 128:(tt + 1) * 128, :], finTt, sem=fs_s)

    if stage == "attn":
        P.wait("sync", x2_s, v_x2dma[NT - 1])
        _emit(nc, P)
        ctx.close()
        return nc

    P.wait("gpsimd", pe, v_projmm[NT - 1])
    v_fload = [0] * NT
    for tt in range(NT):
        P.wait("gpsimd", fs_s, v_finstore[tt])
        v_fload[tt] = dma("gq", finTv[:, :, tt * 128:(tt + 1) * 128],
                          dffn[tt * 128:(tt + 1) * 128, :].rearrange("p (fc j) -> p fc j", j=128), sem=fl_s)
    v_fload_all = v_fload[NT - 1]
    if debug:
        P.wait("scalar", fl_s, v_fload_all)
        dma("sc", dbg_fin, big1[:], sem=dbg_s)

    PBG = [[pA0, pA1], [pB, pC], [pD, pE], [pF, pG]]
    v_deq_done = {}
    v_mm4_done = {-2: 0, -1: 0}
    v_silu = {}
    v_hactm = {}
    v_hstore = {-2: 0, -1: 0}
    v_htr_by_ib = {}

    def deq_args(ib):
        wslot = ib % 2
        out = []
        for half in range(2):
            base_o = half * NI + ib * 512
            for fc in range(NFC):
                out.append((wdqv[:, wslot, half, fc, :],
                            dguw[fc * 128:(fc + 1) * 128, base_o:base_o + 512],
                            dgus[fc:fc + 1, base_o:base_o + 512].broadcast_to([128, 512])))
        return out

    P.wait("vector", oT_s[0], v_oT_by_slot[0])
    P.wait("vector", oT_s[1], v_oT_by_slot[1])
    P.wait("vector", ac, v_beta[NT - 1])
    for a in deq_args(0):
        dequant_tile(*a)
    v_deq_done[0] = dv.n
    v_p5segA = 0
    for ib in range(11):
        wslot = ib % 2
        if ib < 10:
            pend = deq_args(ib + 1)
        else:
            pend = []
            for ic in range(16):
                for ob in range(2):
                    cols = ob * 512
                    pend.append((big2[:, 16384 + ic * 1024 + ob * 512: 16384 + ic * 1024 + (ob + 1) * 512],
                                 ddnw[ic * 128:(ic + 1) * 128, cols:cols + 512],
                                 ddns[ic:ic + 1, cols:cols + 512].broadcast_to([128, 512])))
        if pend:
            P.wait("vector", pe, v_mm4_done[ib - 1])
        pi = 0
        for tt in range(NT):
            bsel = tt % 4
            gbank, ubank = PBG[bsel]
            P.wait("tensor", dv, v_deq_done[ib])
            if ib == 0 and tt == 0:
                P.wait("tensor", fl_s, v_fload_all)
            if ib == 0 and tt <= 3:
                P.wait("tensor", ac, v_evac3[NT - 1][0])
                P.wait("tensor", dv, v_evac3[NT - 1][1])
            P.wait("tensor", ac, v_silu.get((ib, tt - 4), v_silu.get((ib - 1, tt + 12), 0)))
            P.wait("tensor", dv, v_hactm.get((ib, tt - 4), v_hactm.get((ib - 1, tt + 12), 0)))
            vg = 0
            for fc in range(NFC):
                vg = op("tensor", lambda e, fc=fc, tt=tt, wslot=wslot, gbank=gbank: e.matmul(
                    gbank[:], finTv[:, fc, tt * 128:(tt + 1) * 128], wdqv[:, wslot, 0, fc, :],
                    start=(fc == 0), stop=(fc == NFC - 1)))
            vu = 0
            for fc in range(NFC):
                vu = op("tensor", lambda e, fc=fc, tt=tt, wslot=wslot, ubank=ubank: e.matmul(
                    ubank[:], finTv[:, fc, tt * 128:(tt + 1) * 128], wdqv[:, wslot, 1, fc, :],
                    start=(fc == 0), stop=(fc == NFC - 1)))
            P.wait("scalar", pe, vg)
            if tt >= 2:
                P.wait("scalar", dv, v_hactm[(ib, tt - 2)])
            elif ib >= 1:
                P.wait("scalar", dv, v_hactm[(ib - 1, tt + 14)])
            v_silu[(ib, tt)] = op("scalar", lambda e, tt=tt, gbank=gbank: e.activation(
                out=sg[tt % 2][:], in_=gbank[:], func=AF.Silu))
            P.wait("vector", pe, vu)
            P.wait("vector", ac, v_silu[(ib, tt)])
            if tt == 0 and ib >= 2:
                P.wait("vector", hx_s, v_htr_by_ib[ib - 2])
            v_hactm[(ib, tt)] = op("vector", lambda e, tt=tt, ubank=ubank, ib=ib: e.tensor_tensor(
                out=hact[ib % 2][:, tt * 512:(tt + 1) * 512], in0=sg[tt % 2][:], in1=ubank[:], op=ALU.mult))
            while pend and pi < 3 * (tt + 1):
                dequant_tile(*pend.pop(0))
                pi += 1

        while pend:
            dequant_tile(*pend.pop(0))
        if ib < 10:
            v_deq_done[ib + 1] = dv.n
        else:
            v_p5segA = dv.n
        for _ in range(11):
            if segB_pend:
                emit_segB_tile(*segB_pend.pop(0))
        v_mm4_done[ib] = pe.n
        P.wait("sync", dv, v_hactm[(ib, NT - 1)])
        if ib >= 1:
            P.wait("sync", hs_s, v_hstore[ib - 1])
        v_htr_by_ib[ib] = dma("sy", hactT[0].rearrange("p (t j) -> p t j", j=128),
                              hact[ib % 2].rearrange("p (t j) -> p t j", j=128), transpose=True, sem=hx_s)
        P.wait("scalar", hx_s, v_htr_by_ib[ib])
        for jj in range(4):
            v_hstore[ib] = dma("sc", dhactT[:, :, ib * 4 + jj, :].rearrange("t p q -> p t q"),
                               hactT[0].rearrange("p (t j q) -> p t j q", j=4, q=128)[:, :, jj, :], sem=hs_s)
    while segB_pend:
        emit_segB_tile(*segB_pend.pop(0))
    v_segB_store_last = segB_store[segB_k[0] - 1]
    v_p4_done = pe.n
    ac_p4_done = ac.n
    dv_p4_done = dv.n

    def wdn_ap(half, ic, lo, hi):
        if ic < 16:
            base = (16384 if half == 0 else 0) + ic * 1024
            return big2[:, base + lo: base + hi]
        base = (ic - 16) * 1024
        return big1[:, base + lo: base + hi]

    v_out = 0
    v_segA = {}
    v_segB = {}
    v_segA[0] = v_p5segA
    for half in range(2):
        if half == 0:
            P.wait("gpsimd", pe, v_p4_done)
        else:
            P.wait("gpsimd", pe, v_half_done)
        P.wait("gpsimd", dnq_st, v_segB_store_last)
        v_segB[half] = dma("gq", big1[:, 0:28 * 1024].rearrange("p (ic o) -> p ic o", o=1024),
                           ddnq[half].rearrange("ic p o -> p ic o"), sem=dnq_ld)
        P.wait("gpsimd", hs_s, v_hstore[10])
        v_dnmm = {-2: v_p4_done if half == 0 else v_half_done,
                  -1: v_p4_done if half == 0 else v_half_done}
        if half == 0:
            v_sq_prev = ac_p4_done
            v_yev_prev = dv_p4_done
        else:
            v_sq_prev = v_sq2[NT - 1]
            v_yev_prev = v_yev2[NT - 1]
        v_sq2 = {}
        v_yev2 = {}
        segA1_pend = []
        if half == 0:
            for ic in range(16):
                for ob in range(2):
                    cols = 1024 + ob * 512
                    segA1_pend.append((wdn_ap(1, ic, ob * 512, (ob + 1) * 512),
                                       ddnw[ic * 128:(ic + 1) * 128, cols:cols + 512],
                                       ddns[ic:ic + 1, cols:cols + 512].broadcast_to([128, 512])))
        for tt in range(NT):
            s2 = tt % 2
            if tt >= 2:
                P.wait("gpsimd", pe, v_dnmm[tt - 2])
            v_hl = dma("gq", hA[s2], dhactT[tt].rearrange("p ic q -> p (ic q)"), sem=ha_s[s2])
            P.wait("tensor", ha_s[s2], v_hl)
            P.wait("tensor", dv, v_segA[half])
            if tt >= 2 or half == 1:
                P.wait("tensor", dv, v_yev2.get(tt - 2, v_yev_prev))
                P.wait("tensor", ac, v_sq2.get(tt - 2, v_sq_prev))
            else:
                P.wait("tensor", dv, v_yev_prev)
                P.wait("tensor", ac, v_sq_prev)
            pd0, pd1 = ([pA0, pA1] if tt % 2 == 0 else [pB, pC])
            v0 = 0
            for ic in range(NIC):
                if tt == 0 and ic == 16:
                    P.wait("tensor", dnq_ld, v_segB[half])
                lhs = hA[s2][:, ic * 128:(ic + 1) * 128]
                rhs = wdn_ap(half, ic, 0, 512)
                v0 = op("tensor", lambda e, lhs=lhs, rhs=rhs, ic=ic, pd0=pd0: e.matmul(
                    pd0[:], lhs, rhs, start=(ic == 0), stop=(ic == NIC - 1)))
            v1 = 0
            for ic in range(NIC):
                lhs = hA[s2][:, ic * 128:(ic + 1) * 128]
                rhs = wdn_ap(half, ic, 512, 1024)
                v1 = op("tensor", lambda e, lhs=lhs, rhs=rhs, ic=ic, pd1=pd1: e.matmul(
                    pd1[:], lhs, rhs, start=(ic == 0), stop=(ic == NIC - 1)))
            v_dnmm[tt] = pe.n
            for _ in range(2):
                if segA1_pend:
                    dequant_tile(*segA1_pend.pop(0))
            if half == 0 and tt == NT - 1:
                while segA1_pend:
                    dequant_tile(*segA1_pend.pop(0))
                v_segA[1] = dv.n
            P.wait("scalar", pe, v0)
            col0 = (2 * half) * NT + tt
            op("scalar", lambda e, col0=col0, pd0=pd0: e.activation(
                out=dmy512, in_=pd0[:], func=AF.Square, accum_out=ssqs[:, col0:col0 + 1]))
            P.wait("scalar", pe, v1)
            col1 = (2 * half + 1) * NT + tt
            v_sq = op("scalar", lambda e, col1=col1, pd1=pd1: e.activation(
                out=dmy512, in_=pd1[:], func=AF.Square, accum_out=ssqs[:, col1:col1 + 1]))
            v_sq2[tt] = v_sq
            ybuf = y0t if half == 0 else y1t
            P.wait("vector", ac, v_sq)
            op("vector", lambda e, ybuf=ybuf, pd0=pd0: e.tensor_copy(out=ybuf[:, 0:512], in_=pd0[:]))
            v_yev = op("vector", lambda e, ybuf=ybuf, pd1=pd1: e.tensor_copy(out=ybuf[:, 512:1024], in_=pd1[:]))
            v_yev2[tt] = v_yev
            if half == 0:
                P.wait("scalar", dv, v_yev)
                vst = dma("sc", dyh[tt], y0t, sem=ys_s)
                P.wait("vector", ys_s, vst)
            else:
                P.wait("vector", ac, v_sq)
                op("vector", lambda e, tt=tt: e.tensor_tensor(
                    out=tnyA[:, tt:tt + 1], in0=ssqs[:, tt:tt + 1], in1=ssqs[:, NT + tt:NT + tt + 1], op=ALU.add))
                selfsem("vector")
                op("vector", lambda e, tt=tt: e.tensor_tensor(
                    out=tnyB[:, tt:tt + 1], in0=ssqs[:, 2 * NT + tt:2 * NT + tt + 1],
                    in1=ssqs[:, 3 * NT + tt:3 * NT + tt + 1], op=ALU.add))
                selfsem("vector")
                v_sst = op("vector", lambda e, tt=tt: e.tensor_tensor(
                    out=tnyC[:, tt:tt + 1], in0=tnyA[:, tt:tt + 1], in1=tnyB[:, tt:tt + 1], op=ALU.add))
                P.wait("scalar", dv, v_sst)
                op("scalar", lambda e, tt=tt: e.activation(out=tnyD[:, tt:tt + 1], in_=tnyC[:, tt:tt + 1],
                                                           func=AF.Ln, scale=1.0 / DM, bias=1e-6))
                selfsem("scalar")
                v_r3 = op("scalar", lambda e, tt=tt: e.activation(out=r3c[:, tt:tt + 1], in_=tnyD[:, tt:tt + 1],
                                                                  func=AF.Exp, scale=-0.5))
                v_x2l = dma("sc", x2rl, dx2[tt], sem=xy_s)
                v_y0l = dma("sc", y0t, dyh[tt], sem=xy_s)
                P.wait("vector", xy_s, v_y0l)
                P.wait("vector", ac, v_r3)
                op("vector", lambda e, tt=tt: e.scalar_tensor_tensor(
                    out=y0t, in0=y0t, scalar=r3c[:, tt:tt + 1], in1=x2rl[:, 0:1024],
                    op0=ALU.mult, op1=ALU.add))
                v_o1 = op("vector", lambda e, tt=tt: e.scalar_tensor_tensor(
                    out=y1t, in0=y1t, scalar=r3c[:, tt:tt + 1], in1=x2rl[:, 1024:2048],
                    op0=ALU.mult, op1=ALU.add))
                P.wait("scalar", dv, v_o1)
                dma("sc", dy[tt * 128:(tt + 1) * 128, 0:1024], y0t, sem=yo_s)
                v_out = dma("sc", dy[tt * 128:(tt + 1) * 128, 1024:2048], y1t, sem=yo_s)
                P.wait("vector", yo_s, v_out)
        v_half_done = pe.n
    P.wait("sync", yo_s, v_out)

    _emit(nc, P)
    ctx.close()
    return nc


def _emit(nc, P):
    with nc.Block() as block:
        @block.sync
        def _(e):
            P.run("sync", e)

        @block.scalar
        def _(e):
            P.run("scalar", e)

        @block.vector
        def _(e):
            P.run("vector", e)

        @block.gpsimd
        def _(e):
            P.run("gpsimd", e)

        @block.tensor
        def _(e):
            P.run("tensor", e)


def host_prepare(inputs):
    x = np.ascontiguousarray(np.asarray(inputs["L0_attn_x"], dtype=np.float32))
    qkv_w = np.asarray(inputs["L0_attn_qkv_w"], dtype=np.float32)
    proj_w = np.asarray(inputs["L0_attn_proj_w"], dtype=np.float32)
    qkvT = np.ascontiguousarray(qkv_w.T)
    projT = np.ascontiguousarray(proj_w.T)
    gcol = np.ascontiguousarray(np.asarray(inputs["L0_attn_qkv_norm_w"], np.float32).reshape(NFC, 128).T)
    w1 = np.asarray(inputs["L0_attn_norm_w"], np.float32)
    w2 = np.asarray(inputs["L0_attn_proj_norm_w"], np.float32)
    w1col = np.ascontiguousarray(w1.reshape(NFC, 128).T)
    w2col = np.ascontiguousarray(w2.reshape(NFC, 128).T)
    w1row = np.ascontiguousarray(w1.reshape(1, DM))
    srow = np.ascontiguousarray(np.asarray(inputs["L0_attn_s"], np.float32).reshape(1, NH))
    gu = np.ascontiguousarray(np.asarray(inputs["L0_ffn_gate_up_packed"], np.int32))
    guw = np.ascontiguousarray(gu.view(np.uint16)[:, (np.arange(DM) // 4)].T)
    gus = np.ascontiguousarray(np.asarray(inputs["L0_ffn_gate_up_scales"], np.float16))
    dn = np.ascontiguousarray(np.asarray(inputs["L0_ffn_down_packed"], np.int32))
    dnw = np.ascontiguousarray(dn.view(np.uint16)[:, (np.arange(NI) // 4)].T)
    dns = np.ascontiguousarray(np.asarray(inputs["L0_ffn_down_scales"], np.float16))
    shared = dict(qkvT=qkvT, projT=projT, gcol=gcol, w1col=w1col, w2col=w2col,
                  w1row=w1row, srow=srow, guw=guw, gus=gus, dnw=dnw, dns=dns)
    return x, shared


def kernel(**inputs):
    x, shared = host_prepare(inputs)
    nc = build_nc("full")
    in_maps = []
    for core in range(8):
        m = dict(shared)
        m["x"] = np.ascontiguousarray(x[core])
        in_maps.append(m)
    res = run_bass_kernel_spmd(nc, in_maps, core_ids=list(range(8)))
    out = np.stack([res.results[i]["y"] for i in range(8)], axis=0)
    return out.astype(np.float32)


# version 24 (final)
# speedup vs baseline: 1.0113x; 1.0113x over previous
import numpy as np
import ml_dtypes
from contextlib import ExitStack

import concourse.bass as bass
import concourse.mybir as mybir
from concourse.bass_utils import run_bass_kernel_spmd

F32 = mybir.dt.float32
BF16 = mybir.dt.bfloat16
F16 = mybir.dt.float16
U16 = mybir.dt.uint16
I16 = mybir.dt.int16
AF = mybir.ActivationFunctionType
ALU = mybir.AluOpType
AX = mybir.AxisListType

DM = 2048
NT = 16
NH = 16
NI = 5632
NIC = NI // 128
NFC = 16
EPS32 = float(np.finfo(np.float32).eps)
SCALE_O = float(128 ** -0.5)
QN = 3 * DM * DM
PN = DM * DM


class Sem:
    def __init__(self, ctx, nc, name):
        self.h = ctx.enter_context(nc.semaphore(name))
        self.n = 0

    def bump(self, k=1):
        self.n += k
        return self.n


class Plan:
    def __init__(self):
        self.ops = []

    def on(self, eng, fn):
        self.ops.append((eng, fn))

    def wait(self, eng, sem, val):
        if val is None or val <= 0:
            return
        self.ops.append((eng, lambda e, s=sem.h, v=val: e.wait_ge(s, v)))

    def wait_lazy(self, eng, sem, fn):
        self.ops.append((eng, lambda e, s=sem.h, f=fn: e.wait_ge(s, f())))

    def run(self, eng_key, eng):
        for k, fn in self.ops:
            if k == eng_key:
                fn(eng)


def build_nc(stage="full", debug=False):
    nc = bass.Bass("TRN2", target_bir_lowering=False, debug=False)
    ctx = ExitStack()

    dx = nc.dram_tensor("x", [DM, DM], F32, kind="ExternalInput").ap()
    dqkvT = nc.dram_tensor("qkvT", [DM, 3 * DM], F32, kind="ExternalInput").ap()
    dprojT = nc.dram_tensor("projT", [DM, DM], F32, kind="ExternalInput").ap()
    dgcol = nc.dram_tensor("gcol", [128, NFC], F32, kind="ExternalInput").ap()
    dw1col = nc.dram_tensor("w1col", [128, NFC], F32, kind="ExternalInput").ap()
    dw2col = nc.dram_tensor("w2col", [128, NFC], F32, kind="ExternalInput").ap()
    dw1row = nc.dram_tensor("w1row", [1, DM], F32, kind="ExternalInput").ap()
    dsrow = nc.dram_tensor("srow", [1, NH], F32, kind="ExternalInput").ap()
    dguw = nc.dram_tensor("guw", [DM, 2 * NI], U16, kind="ExternalInput").ap()
    dgus = nc.dram_tensor("gus", [NFC, 2 * NI], F16, kind="ExternalInput").ap()
    ddnw = nc.dram_tensor("dnw", [NI, DM], U16, kind="ExternalInput").ap()
    ddns = nc.dram_tensor("dns", [NIC, DM], F16, kind="ExternalInput").ap()
    dy = nc.dram_tensor("y", [DM, DM], F32, kind="ExternalOutput").ap()
    if debug:
        dbg_hT = nc.dram_tensor("dbg_hT", [128, 32768], BF16, kind="ExternalOutput").ap()
        dbg_o = nc.dram_tensor("dbg_o", [128, 32768], BF16, kind="ExternalOutput").ap()
        dbg_small = nc.dram_tensor("dbg_small", [128, 64], F32, kind="ExternalOutput").ap()
        dbg_qkv = nc.dram_tensor("dbg_qkv", [128, 6144], BF16, kind="ExternalOutput").ap()
        dbg_fin = nc.dram_tensor("dbg_fin", [128, 32768], BF16, kind="ExternalOutput").ap()
        dbg_wdq = nc.dram_tensor("dbg_wdq", [128, 32768], BF16, kind="ExternalOutput").ap()
        dbg_hact = nc.dram_tensor("dbg_hact", [128, 8192], BF16, kind="ExternalOutput").ap()
        dbg_ssq = nc.dram_tensor("dbg_ssq", [128, 80], F32, kind="ExternalOutput").ap()
        dbg_p5 = nc.dram_tensor("dbg_p5", [128, 8192], BF16, kind="ExternalOutput").ap()

    dwpq = nc.dram_tensor("wpq_s", [NFC, 128, DM], BF16, kind="Internal").ap()
    dx2 = nc.dram_tensor("x2_s", [NT, 128, DM], F32, kind="Internal").ap()
    dffn = nc.dram_tensor("ffn_s", [DM, DM], BF16, kind="Internal").ap()
    dhactT = nc.dram_tensor("hactT_s", [NT, 128, NIC, 128], BF16,
                            kind="ExternalOutput" if debug else "Internal").ap()
    dyh = nc.dram_tensor("yh_s", [NT, 128, 1024], F32, kind="Internal").ap()

    sb = lambda n, s, d, **kw: ctx.enter_context(nc.sbuf_tensor(n, s, d, **kw))
    big1 = sb("big1", [128, 32768], BF16, is_transpose_dest=True)
    big2 = sb("big2", [128, 32768], BF16, is_transpose_dest=True)
    pool3 = sb("pool3", [128, 32768], BF16, is_transpose_dest=True)

    def v3(a, b, dt=BF16):
        ap = pool3[:, a:b]
        return ap if dt == BF16 else ap.bitcast(dt)

    wraw = [v3(0, 4096, F32), v3(4096, 8192, F32)]
    wq = pool3[:, 8192:14336]
    wq3 = wq.rearrange("p (t3 fc j) -> p t3 fc j", t3=3, j=128)
    qlbuf = pool3[:, 14336:16384]
    qkvsb = [pool3[:, 16384:22528], pool3[:, 22528:28672]]
    qkv3 = [q.rearrange("p (t3 t) -> p t3 t", t3=3) for q in qkvsb]
    passA = [v3(16384 + 4096 * i, 16384 + 4096 * (i + 1), F32) for i in range(3)]
    pstage = pool3[:, 28672:30720]
    w1bc = pool3[:, 30720:32768]
    ht = [pool3[:, 8192:10240], pool3[:, 10240:12288]]
    xrl = [v3(0, 4096, F32), v3(4096, 8192, F32)]
    oTt = [pool3[:, 8192:10240], pool3[:, 10240:12288]]
    fint = [pool3[:, 12288:14336], pool3[:, 14336:16384]]
    x2t = [v3(16384, 20480, F32), v3(20480, 24576, F32)]
    w1o = pool3[:, 24576:26624]
    finTt = pool3[:, 24576:26624]
    dmybf = pool3[:, 26624:28672]
    hact = [pool3[:, 0:8192], pool3[:, 8192:16384]]
    hactT = [pool3[:, 24576:32768]]
    wu16 = [v3(16384 + 512 * i, 16384 + 512 * (i + 1), U16) for i in range(4)]
    sbct = [pool3[:, 18432 + 512 * i: 18432 + 512 * (i + 1)] for i in range(4)]
    ntl = [v3(20480 + 512 * i, 20480 + 512 * (i + 1), U16) for i in range(2)]
    atl = [v3(21504 + 512 * i, 21504 + 512 * (i + 1), U16) for i in range(2)]
    btl = [v3(22528 + 512 * i, 22528 + 512 * (i + 1), U16) for i in range(2)]
    sg = [pool3[:, 23552 + 512 * i: 23552 + 512 * (i + 1)] for i in range(2)]
    dmy512 = pool3[:, 24064:24576]
    hA = [pool3[:, 0:5632], pool3[:, 5632:11264]]
    y0t = v3(11264, 13312, F32)
    y1t = v3(13312, 15360, F32)
    x2rl = v3(25600, 29696, F32)

    constt = sb("constt", [128, 8], F32)
    _cvals = [EPS32, 1e-6, -1.0, -0.5, SCALE_O, 1e-5, 1.0 / DM, 16384.0]
    for _i, _v in enumerate(_cvals):
        nc.const_aps.register(F32, float(_v), constt[:, _i:_i + 1])

    mk = [sb(f"mk{i}", [128, 128], BF16) for i in range(2)]
    ql = [sb(f"ql{i}", [128, 128], BF16) for i in range(2)]
    kl = [sb(f"kl{i}", [128, 128], BF16) for i in range(2)]
    vtk = [sb(f"vtk{i}", [128, 128], BF16) for i in range(2)]
    Sst = sb("Sst", [128, 128], F32)
    Sbf = [sb(f"Sbf{i}", [128, 128], BF16) for i in range(2)]
    masks = sb("masks", [128, NH * 128], BF16)
    lamb = [sb(f"lamb{i}", [128, 128], BF16) for i in range(2)]
    gcol = sb("gcolS", [128, NFC], F32)
    w1col = sb("w1colS", [128, NFC], F32)
    w2col = sb("w2colS", [128, NFC], F32)
    w12col = sb("w12colS", [128, NFC], F32)
    srep = sb("srep", [128, NH], F32)
    negs = sb("negs", [128, NH], F32)
    lamrev = sb("lamrev", [128, NH], F32)
    lamT = sb("lamT", [128, NH], F32)
    tcolt = sb("tcolt", [128, 128], F32)
    diff = sb("diff", [128, 128], F32)
    tri = sb("tri", [128, 128], F32)
    ident = sb("identS", [128, 128], BF16)
    trev = sb("trev", [128, 1], F32)
    c128 = sb("c128", [128, 1], F32)
    pcol = sb("pcol", [128, 1], I16)
    shiftc = sb("shiftc", [128, 1], I16)
    ones1 = sb("ones1", [1, 128], F32)
    onescol = sb("onescol", [128, 1], F32)
    absacc = sb("absacc", [128, 64], F32)
    redt = sb("redt", [128, 2], F32)
    mrow = sb("mrow", [1, 8], F32)
    m2row = sb("m2row", [1, 8], F32)
    brow = sb("brow", [1, 8], F32)
    scol = sb("scolS", [128, 8], F32)
    sumc = sb("sumc", [128, NT], F32)
    sqc = sb("sqc", [128, NT], F32)
    c1c = sb("c1c", [128, NT], F32)
    b1c = sb("b1c", [128, NT], F32)
    tnyA = sb("tnyA", [128, NT], F32)
    tnyB = sb("tnyB", [128, NT], F32)
    tnyC = sb("tnyC", [128, NT], F32)
    tnyD = sb("tnyD", [128, NT], F32)
    s1c = sb("s1c", [128, NT], F32)
    s2c = sb("s2c", [128, NT], F32)
    betac = sb("betac", [128, NT], F32)
    ssqs = sb("ssqs", [128, 4 * NT], F32)
    r3c = sb("r3c", [128, NT], F32)
    sum2 = sb("sum2", [128, NT], F32)
    sq2 = sb("sq2", [128, NT], F32)
    c2c = sb("c2c", [128, NT], F32)
    b2c = sb("b2c", [128, NT], F32)

    hTr = big1[:].rearrange("p (fc t) -> p fc t", t=2048)
    wpqv = big1[:].rearrange("p (fc o) -> p fc o", o=2048)
    finTv = big1[:].rearrange("p (fc t) -> p fc t", t=2048)
    o3 = big2[:].rearrange("p (c f) -> p c f", f=2048)
    wdqv = big2[:].rearrange("p (w h fc j) -> p w h fc j", w=2, h=2, j=512)
    wdn1 = big1[:].rearrange("p (ic o) -> p ic o", o=1024)
    wdn2 = big2[:, 0:12288].rearrange("p (ic o) -> p ic o", o=1024)
    dmyP1 = pool3[:, 28672:30720]
    mskP0 = pool3[:, 14336:14592].bitcast(F32)[:, 0:128]

    ps = lambda n: ctx.enter_context(nc.psum_tensor(n, [128, 512], F32))
    pA0, pA1, pB, pC, pD, pE, pF, pG = [ps(f"ps{i}") for i in range(8)]

    sy = Sem(ctx, nc, "sy")
    sc = Sem(ctx, nc, "sc")
    scb = Sem(ctx, nc, "scb")
    gq = Sem(ctx, nc, "gq")
    gqb = Sem(ctx, nc, "gqb")
    wr_s = [Sem(ctx, nc, f"wr{i}") for i in range(2)]
    pa_s = [Sem(ctx, nc, f"pa{i}") for i in range(3)]
    wu_s = [Sem(ctx, nc, f"wu{i}") for i in range(4)]
    sb_s = [Sem(ctx, nc, f"sb{i}") for i in range(4)]
    oT_s = [Sem(ctx, nc, f"oT{i}") for i in range(2)]
    hT_s = [Sem(ctx, nc, f"hT{i}") for i in range(2)]
    ha_s = [Sem(ctx, nc, f"ha{i}") for i in range(2)]
    ft_s = Sem(ctx, nc, "ft")
    hx_s = Sem(ctx, nc, "hx")
    wpq_st = Sem(ctx, nc, "wpqst")
    wpq_ld = Sem(ctx, nc, "wpqld")
    fl_s = Sem(ctx, nc, "fl")
    xy_s = Sem(ctx, nc, "xy")
    x2_s = Sem(ctx, nc, "x2s")
    fs_s = Sem(ctx, nc, "fss")
    hs_s = Sem(ctx, nc, "hss")
    ys_s = Sem(ctx, nc, "yss")
    yo_s = Sem(ctx, nc, "yos")
    dbg_s = Sem(ctx, nc, "dbgs")
    pe = Sem(ctx, nc, "pe")
    dv = Sem(ctx, nc, "dv")
    ac = Sem(ctx, nc, "ac")
    po = Sem(ctx, nc, "po")
    P = Plan()
    RING = {"sy": "sync", "sc": "scalar", "gq": "gpsimd"}
    RSEM = {"sy": sy, "sc": sc, "gq": gq}

    def dma(ring, dst, src, transpose=False, slow=False, sem=None):
        if sem is None:
            sem = RSEM[ring]
        v = sem.bump(16)
        kw = {}
        if transpose:
            kw["transpose"] = True
        if slow:
            kw["allow_slow_non_contiguous"] = True
        P.on(RING[ring], lambda e, d=dst, s=src, k=dict(kw): e.dma_start(out=d, in_=s, **k).then_inc(sem.h, 16))
        return v

    ESEM = {"vector": dv, "scalar": ac, "gpsimd": po, "tensor": pe}

    def op(eng, fn):
        sem = ESEM[eng]
        v = sem.bump(1)
        P.on(eng, lambda e, f=fn: f(e).then_inc(sem.h, 1))
        return v

    def selfsem(eng):
        v = op(eng, lambda e: e.nop())
        P.wait(eng, ESEM[eng], v)
        return v

    dma("gq", gcol[:], dgcol)
    dma("gq", w1col[:], dw1col)
    dma("gq", w2col[:], dw2col)
    dma("gq", srep[:], dsrow.broadcast_to([128, NH]), sem=gqb)
    v_w1bc = dma("gq", w1bc, dw1row.broadcast_to([128, DM]), sem=gqb)
    op("gpsimd", lambda e: e.iota(out=diff[:], pattern=[[1, 128]], base=0,
                                  channel_multiplier=-1, allow_small_or_imprecise_dtypes=True))
    op("gpsimd", lambda e: e.iota(out=tcolt[:], pattern=[[1, 128]], base=0,
                                  channel_multiplier=0, allow_small_or_imprecise_dtypes=True))
    op("gpsimd", lambda e: e.iota(out=trev[:], pattern=[[0, 1]], base=128,
                                  channel_multiplier=-1, allow_small_or_imprecise_dtypes=True))
    op("gpsimd", lambda e: e.iota(out=pcol[:], pattern=[[0, 1]], base=0,
                                  channel_multiplier=1))
    v_iota = op("gpsimd", lambda e: e.memset(c128[:], 128.0))
    P.wait("vector", po, v_iota)
    P.wait("vector", gqb, v_w1bc)
    P.wait("vector", gq, 48)
    op("vector", lambda e: e.tensor_scalar(out=tri[:], in0=diff[:], scalar1=0.0, scalar2=None, op0=ALU.is_ge))
    op("vector", lambda e: e.tensor_scalar(out=ident[:], in0=diff[:], scalar1=0.0, scalar2=None, op0=ALU.is_equal))
    op("vector", lambda e: e.tensor_scalar(out=negs[:], in0=srep[:], scalar1=-1.0, scalar2=None, op0=ALU.mult))
    op("vector", lambda e: e.tensor_tensor(out=w12col[:], in0=w1col[:], in1=w2col[:], op=ALU.mult))
    op("vector", lambda e: e.tensor_scalar(out=shiftc[:], in0=pcol[:], scalar1=3, scalar2=2,
                                           op0=ALU.bitwise_and, op1=ALU.logical_shift_left))
    op("vector", lambda e: e.memset(ones1[:], 1.0))
    for _i, _v in enumerate(_cvals):
        op("vector", lambda e, _i=_i, _v=_v: e.memset(constt[:, _i:_i + 1], float(_v)))
    v_dvsmall = selfsem("vector")
    P.wait("scalar", dv, v_dvsmall)
    v_mtt = 0
    for h in range(NH):
        op("scalar", lambda e, h=h: e.activation(out=lamrev[:, h:h + 1], in_=trev[:], func=AF.Exp,
                                                 scale=negs[:, h:h + 1]))
        op("scalar", lambda e, h=h: e.activation(out=lamT[:, h:h + 1], in_=c128[:], func=AF.Exp,
                                                 scale=negs[:, h:h + 1]))
        P.wait("scalar", dv, v_mtt)
        v_me = op("scalar", lambda e, h=h: e.activation(out=mskP0, in_=diff[:], func=AF.Exp,
                                                        scale=negs[:, h:h + 1]))
        P.wait("vector", ac, v_me)
        v_mtt = op("vector", lambda e, h=h: e.tensor_tensor(out=masks[:, h * 128:(h + 1) * 128],
                                                            in0=mskP0, in1=tri[:], op=ALU.mult))
    v_masks = dv.n
    v_lamtabs = ac.n

    qkvT3 = dqkvT.rearrange("(fc p) c -> fc p c", p=128)
    projT3 = dprojT.rearrange("(fc p) c -> fc p c", p=128)
    vA_red = {}
    PA_RING = {0: "gq", 1: "sy", 2: "sc"}
    PA_ENG = {"gq": "gpsimd", "sy": "sync", "sc": "scalar"}
    idx = 0
    srcs = [qkvT3[fc, :, j * 2048:(j + 1) * 2048] for fc in range(NFC) for j in range(3)]
    srcs += [projT3[fc] for fc in range(NFC)]
    vA_eng = {}
    for src in srcs:
        s3 = idx % 3
        ring = PA_RING[s3]
        eng = "vector" if (idx % 2 == 0) else "scalar"
        if idx >= 3:
            peng, pv = vA_eng[idx - 3], vA_red[idx - 3]
            P.wait(PA_ENG[ring], dv if peng == "vector" else ac, pv)
        vld = dma(ring, passA[s3], src, sem=pa_s[s3])
        P.wait(eng, pa_s[s3], vld)
        if eng == "vector":
            vA_red[idx] = op("vector", lambda e, s3=s3, i=idx: e.tensor_reduce(
                out=absacc[:, i:i + 1], in_=passA[s3], axis=AX.X, op=ALU.add, apply_absolute_value=True))
        else:
            vA_red[idx] = op("scalar", lambda e, s3=s3, i=idx: e.activation(
                out=dmyP1, in_=passA[s3], func=AF.Abs, accum_out=absacc[:, i:i + 1]))
        vA_eng[idx] = eng
        idx += 1
    v_passA_ac = ac.n
    P.wait("vector", ac, v_passA_ac)
    op("vector", lambda e: e.tensor_reduce(out=redt[:, 0:1], in_=absacc[:, 0:48], axis=AX.X, op=ALU.add))
    v_red = op("vector", lambda e: e.tensor_reduce(out=redt[:, 1:2], in_=absacc[:, 48:64], axis=AX.X, op=ALU.add))
    op("vector", lambda e: e.memset(onescol[:], 1.0))
    v_red = selfsem("vector")
    P.wait("tensor", dv, v_red)
    op("tensor", lambda e: e.matmul(pG[0:1, 0:1], redt[:, 0:1], onescol[:], start=True, stop=True))
    v_mmr = op("tensor", lambda e: e.matmul(pG[0:1, 1:2], redt[:, 1:2], onescol[:], start=True, stop=True))
    P.wait("vector", pe, v_mmr)
    op("vector", lambda e: e.tensor_copy(out=mrow[0:1, 0:2], in_=pG[0:1, 0:2]))
    selfsem("vector")
    op("vector", lambda e: e.tensor_scalar(out=mrow[0:1, 2:3], in0=mrow[0:1, 0:1],
                                           scalar1=1.0 / QN, scalar2=1e-5, op0=ALU.mult, op1=ALU.max))
    op("vector", lambda e: e.tensor_scalar(out=mrow[0:1, 3:4], in0=mrow[0:1, 1:2],
                                           scalar1=1.0 / PN, scalar2=1e-5, op0=ALU.mult, op1=ALU.max))
    selfsem("vector")
    op("vector", lambda e: e.tensor_scalar(out=brow[0:1, 0:1], in0=mrow[0:1, 2:3], scalar1=0.5, scalar2=None, op0=ALU.mult))
    op("vector", lambda e: e.tensor_scalar(out=brow[0:1, 1:2], in0=mrow[0:1, 2:3], scalar1=-0.5, scalar2=None, op0=ALU.mult))
    op("vector", lambda e: e.tensor_scalar(out=brow[0:1, 2:3], in0=mrow[0:1, 3:4], scalar1=0.5, scalar2=None, op0=ALU.mult))
    op("vector", lambda e: e.tensor_scalar(out=brow[0:1, 3:4], in0=mrow[0:1, 3:4], scalar1=-0.5, scalar2=None, op0=ALU.mult))
    op("vector", lambda e: e.tensor_tensor(out=m2row[0:1, 0:1], in0=mrow[0:1, 2:3], in1=mrow[0:1, 2:3], op=ALU.mult))
    selfsem("vector")
    op("vector", lambda e: e.tensor_tensor(out=m2row[0:1, 1:2], in0=m2row[0:1, 0:1], in1=mrow[0:1, 2:3], op=ALU.mult))
    selfsem("vector")
    op("vector", lambda e: e.tensor_tensor(out=m2row[0:1, 2:3], in0=m2row[0:1, 1:2], in1=mrow[0:1, 3:4], op=ALU.mult))
    op("vector", lambda e: e.tensor_tensor(out=m2row[0:1, 3:4], in0=m2row[0:1, 1:2], in1=m2row[0:1, 1:2], op=ALU.mult))
    v_mf = selfsem("vector")
    op("vector", lambda e: e.tensor_scalar(out=brow[0:1, 5:6], in0=m2row[0:1, 3:4],
                                           scalar1=1.0 / DM, scalar2=None, op0=ALU.mult))
    P.wait("scalar", dv, v_mf)
    v_lnmf = op("scalar", lambda e: e.activation(out=brow[0:1, 4:5], in_=m2row[0:1, 2:3], func=AF.Ln))
    P.wait("vector", ac, v_lnmf)
    v_brow = selfsem("vector")
    P.wait("tensor", dv, v_brow)
    v_bc = op("tensor", lambda e: e.matmul(pG[:, 0:6], ones1[:], brow[0:1, 0:6], start=True, stop=True))
    P.wait("vector", pe, v_bc)
    op("vector", lambda e: e.tensor_copy(out=scol[:, 0:6], in_=pG[:, 0:6]))
    v_scol = selfsem("vector")

    v_xload = [0] * NT
    v_hact1 = [0] * NT
    v_htrans = [0] * NT
    v_st1 = [0] * NT
    v_xstash_by = [0] * NT
    for tt in range(NT):
        s2 = tt % 2
        if tt >= 2:
            P.wait("scalar", ac, v_st1[tt - 2])
            P.wait("scalar", dv, v_xstash_by[tt - 2])
        v_xload[tt] = dma("sc", wraw[s2], dx[tt * 128:(tt + 1) * 128, :], sem=wr_s[s2])
        P.wait("scalar", wr_s[s2], v_xload[tt])
        op("scalar", lambda e, s2=s2, tt=tt: e.activation(out=dmyP1, in_=wraw[s2], func=AF.Copy,
                                                          accum_out=sumc[:, tt:tt + 1]))
        v_st1[tt] = op("scalar", lambda e, s2=s2, tt=tt: e.activation(out=dmyP1, in_=wraw[s2], func=AF.Square,
                                                                      accum_out=sqc[:, tt:tt + 1]))
        P.wait("vector", wr_s[s2], v_xload[tt])
        v_xstash_by[tt] = op("vector", lambda e, s2=s2, tt=tt: e.tensor_copy(out=o3[:, tt, :], in_=wraw[s2]))
    P.wait("vector", ac, v_st1[NT - 1])
    op("vector", lambda e: e.tensor_scalar(out=tnyA[:], in0=sumc[:], scalar1=1.0 / DM, scalar2=None, op0=ALU.mult))
    op("vector", lambda e: e.tensor_scalar(out=tnyB[:], in0=sqc[:], scalar1=1.0 / DM, scalar2=None, op0=ALU.mult))
    selfsem("vector")
    op("vector", lambda e: e.tensor_tensor(out=tnyC[:], in0=tnyA[:], in1=tnyA[:], op=ALU.mult))
    selfsem("vector")
    op("vector", lambda e: e.tensor_tensor(out=tnyD[:], in0=tnyB[:], in1=tnyC[:], op=ALU.subtract))
    selfsem("vector")
    v_t1 = op("vector", lambda e: e.tensor_scalar(out=tnyB[:], in0=tnyD[:], scalar1=1e-5, scalar2=None, op0=ALU.add))
    P.wait("scalar", dv, v_t1)
    op("scalar", lambda e: e.activation(out=tnyC[:], in_=tnyB[:], func=AF.Ln))
    selfsem("scalar")
    v_e1 = op("scalar", lambda e: e.activation(out=sqc[:], in_=tnyC[:], func=AF.Exp, scale=-1.0))
    P.wait("vector", ac, v_e1)
    v_q = op("vector", lambda e: e.scalar_tensor_tensor(out=tnyB[:], in0=tnyD[:], scalar=1.0, in1=sqc[:],
                                                        op0=ALU.mult, op1=ALU.mult))
    P.wait("scalar", dv, v_q)
    op("scalar", lambda e: e.activation(out=tnyB[:], in_=tnyB[:], func=AF.Ln, bias=EPS32))
    selfsem("scalar")
    v_l2 = ac.n
    P.wait("vector", ac, v_l2)
    v_ls = op("vector", lambda e: e.tensor_tensor(out=tnyD[:], in0=tnyC[:], in1=tnyB[:], op=ALU.add))
    P.wait("scalar", dv, v_ls)
    v_c1 = op("scalar", lambda e: e.activation(out=c1c[:], in_=tnyD[:], func=AF.Exp, scale=-0.5))
    P.wait("vector", ac, v_c1)
    op("vector", lambda e: e.tensor_scalar(out=b1c[:], in0=tnyA[:], scalar1=-1.0, scalar2=None, op0=ALU.mult))
    v_p1b = selfsem("vector")
    for tt in range(NT):
        s2 = tt % 2
        if tt >= 2:
            P.wait("vector", hT_s[s2], v_htrans[tt - 2])
        v_hact1[tt] = op("vector", lambda e, s2=s2, tt=tt: e.tensor_scalar(
            out=ht[s2], in0=o3[:, tt, :], scalar1=b1c[:, tt:tt + 1], scalar2=c1c[:, tt:tt + 1],
            op0=ALU.add, op1=ALU.mult))
        P.wait("sync", dv, v_hact1[tt])
        v_htrans[tt] = dma("sy", hTr[:, :, tt * 128:(tt + 1) * 128],
                           ht[s2].rearrange("p (fc j) -> p fc j", j=128), transpose=True, sem=hT_s[s2])
    v_hT_all2 = (hT_s[0], v_htrans[NT - 2]); v_hT_all = v_htrans[NT - 1]
    P.wait("vector", hT_s[0], v_htrans[NT - 2])
    P.wait("vector", hT_s[1], v_htrans[NT - 1])
    for fc in range(NFC):
        op("vector", lambda e, fc=fc: e.tensor_scalar(out=hTr[:, fc, :], in0=hTr[:, fc, :],
                                                      scalar1=gcol[:, fc:fc + 1], scalar2=None, op0=ALU.mult))
    v_hT_ready = dv.n
    if debug:
        P.wait("scalar", dv, v_hT_ready)
        dma("sc", dbg_hT, big1[:], sem=dbg_s)

    v_quant = {}
    v_qkvmm_done = {-1: 0}
    v_qkvevac_done = {}
    v_scan_done = {-2: 0, -1: 0}
    v_ql_last = {-2: 0, -1: 0}
    g_masked = 0
    g_klam = 0
    g_vevac = 0
    g_oevac = 0
    g_supd = 0
    g_sbf_prev = 0
    g_cross = {}
    g_intra = {}
    g_kv = {}
    cglob = 0
    scan_tmp = None
    g_masked2 = {}
    bankA_free = [0, 0]
    v_projq_store = [0] * NFC
    pieceidx = 0
    v_piece_red = {}
    v_lamb = {}
    dqkvTp = dqkvT.rearrange("(fc p) c -> p fc c", p=128)

    def emit_quant_steps(h):
        nonlocal pieceidx
        steps = []
        for t3 in range(3):
            pslot = pieceidx % 2
            pidx = pieceidx
            if pieceidx >= 2:
                P.wait_lazy("gpsimd", dv, lambda k=pieceidx - 2: v_piece_red[k])
            else:
                P.wait("gpsimd", dv, v_hact1[NT - 1])
            cols = t3 * 2048 + h * 128
            vld = dma("gq", wraw[pslot].rearrange("p (fc j) -> p fc j", j=128),
                      dqkvTp[:, :, cols:cols + 128], sem=wr_s[pslot])
            pieceidx += 1

            def step1(pslot=pslot, pidx=pidx, vld=vld, t3=t3):
                P.wait("vector", wr_s[pslot], vld)
                if pidx == 0:
                    P.wait("vector", dv, v_scol)
                    P.wait("vector", hT_s[0], v_htrans[NT - 2])
                    P.wait("vector", hT_s[1], v_htrans[NT - 1])
                P.wait("vector", pe, v_scan_done[h - 2])
                wqr = wq[:, t3 * 2048:(t3 + 1) * 2048]
                op("vector", lambda e, pslot=pslot, wqr=wqr: e.tensor_scalar(
                    out=wqr, in0=wraw[pslot], scalar1=scol[:, 0:1], scalar2=None, op0=ALU.is_gt))

            def step2(pslot=pslot, pidx=pidx, t3=t3, h=h):
                wqr = wq[:, t3 * 2048:(t3 + 1) * 2048]
                v_piece_red[pidx] = op("vector", lambda e, pslot=pslot, wqr=wqr: e.scalar_tensor_tensor(
                    out=wqr, in0=wraw[pslot], scalar=scol[:, 1:2], in1=wqr,
                    op0=ALU.is_lt, op1=ALU.subtract))
                v_quant[(h, t3)] = v_piece_red[pidx]
            steps.append(step1)
            steps.append(step2)
        return steps

    def emit_quant(h):
        for st_ in emit_quant_steps(h):
            st_()

    def emit_lamb(h):
        s2 = h % 2
        P.wait("scalar", dv, v_ql_last[h - 2])
        v_lamb[h] = op("scalar", lambda e, h=h, s2=s2: e.activation(
            out=lamb[s2][:], in_=tcolt[:], func=AF.Exp, scale=negs[:, h:h + 1]))

    first_mm = [True]

    def emit_qkv_group(h, g):
        s2 = h % 2
        t3, tb = g // 4, g % 4
        gb = g % 2
        pbank = [pA0, pA1][gb]
        P.wait("tensor", dv, v_quant[(h, t3)])
        if first_mm[0]:
            P.wait("tensor", dv, v_hT_ready)
            first_mm[0] = False
        if isinstance(bankA_free[gb], tuple):
            P.wait("tensor", ac, bankA_free[gb][1])
        else:
            P.wait("tensor", dv, bankA_free[gb])
        vmm = 0
        for fc in range(NFC):
            vmm = op("tensor", lambda e, t3=t3, tb=tb, fc=fc, pbank=pbank: e.matmul(
                pbank[:], wq3[:, t3, fc, :], hTr[:, fc, tb * 512:(tb + 1) * 512],
                start=(fc == 0), stop=(fc == NFC - 1)))
        P.wait("scalar", pe, vmm)
        P.wait("scalar", pe, v_scan_done[h - 2])
        bankA_free[gb] = ("ac", op("scalar", lambda e, s2=s2, t3=t3, tb=tb, pbank=pbank: e.activation(
            out=qkv3[s2][:, t3, tb * 512:(tb + 1) * 512], in_=pbank[:], func=AF.Copy)))

    def emit_scan_pre(h, c):
        nonlocal g_masked, g_klam, g_vevac, scan_tmp
        s2 = h % 2
        cs = c % 2
        qTc = qkv3[s2][:, 0, c * 128:(c + 1) * 128]
        kTc = qkv3[s2][:, 1, c * 128:(c + 1) * 128]
        vTc = qkv3[s2][:, 2, c * 128:(c + 1) * 128]
        if c == 0:
            P.wait("tensor", ac, v_qkvevac_done[h])
        sbank = [pB, pG][cglob % 2]
        P.wait("tensor", dv, g_masked2.get(cglob - 2, 0))
        v_sc_mm = op("tensor", lambda e, kTc=kTc, qTc=qTc, sbank=sbank: e.matmul(
            sbank[:, 0:128], kTc, qTc, start=True, stop=True))
        P.wait("tensor", ac, g_klam)
        v_kt = op("tensor", lambda e, kTc=kTc: e.matmul(pE[:, 0:128], kTc, ident[:], start=True, stop=True))
        P.wait("tensor", dv, g_vevac)
        v_vt = op("tensor", lambda e, vTc=vTc: e.matmul(pF[:, 0:128], vTc, ident[:], start=True, stop=True))
        P.wait("vector", pe, v_sc_mm)
        if cglob == 0:
            P.wait("vector", dv, v_masks)
        P.wait("vector", pe, g_intra.get(cglob - 2, 0))
        g_masked = op("vector", lambda e, cs=cs, h=h, sbank=sbank: e.tensor_tensor(
            out=mk[cs][:], in0=sbank[:, 0:128], in1=masks[:, h * 128:(h + 1) * 128], op=ALU.mult))
        g_masked2[cglob] = g_masked
        P.wait("vector", pe, v_vt)
        P.wait("vector", pe, g_kv.get(cglob - 2, 0))
        g_vevac = op("vector", lambda e, cs=cs: e.tensor_copy(out=vtk[cs][:], in_=pF[:, 0:128]))
        P.wait("scalar", pe, v_kt)
        if cglob == 0:
            P.wait("scalar", ac, v_lamtabs)
        P.wait("scalar", pe, g_kv.get(cglob - 2, 0))
        g_klam = op("scalar", lambda e, cs=cs, h=h: e.activation(
            out=kl[cs][:], in_=pE[:, 0:128], func=AF.Copy, scale=lamrev[:, h:h + 1]))
        scan_tmp = (v_kt, v_vt)

    def emit_scan_post(h, c):
        nonlocal cglob, g_masked, g_klam, g_vevac, g_oevac, g_supd, g_sbf_prev
        s2 = h % 2
        cs = c % 2
        P.wait("tensor", dv, g_masked)
        P.wait("tensor", dv, g_vevac)
        P.wait("tensor", ac, g_oevac)
        v_intra = op("tensor", lambda e, cs=cs, c=c: e.matmul(
            pC[:, 0:128], mk[cs][:], vtk[cs][:], start=True, stop=(c == 0)))
        g_intra[cglob] = v_intra
        if c > 0:
            P.wait("tensor", dv, v_qlb[h])
            P.wait("tensor", dv, g_sbf_prev)
            v_cross = op("tensor", lambda e, cs=cs, c=c: e.matmul(
                pC[:, 0:128], qlbuf[:, c * 128:(c + 1) * 128], Sbf[(cs + 1) % 2][:], start=False, stop=True))
        else:
            v_cross = v_intra
        g_cross[cglob] = v_cross
        P.wait("tensor", ac, g_klam)
        P.wait("tensor", dv, g_supd)
        v_kv = op("tensor", lambda e, cs=cs: e.matmul(pD[:, 0:128], kl[cs][:], vtk[cs][:], start=True, stop=True))
        g_kv[cglob] = v_kv
        P.wait("scalar", pe, v_cross)
        g_oevac = op("scalar", lambda e, c=c, h=h: e.activation(
            out=o3[:, c, h * 128:(h + 1) * 128], in_=pC[:, 0:128], func=AF.Copy, scale=SCALE_O))
        P.wait("vector", pe, v_kv)
        P.wait("vector", pe, g_cross.get(cglob - 1, 0))
        if c == 0:
            g_sbf_prev = op("vector", lambda e, cs=cs: e.tensor_copy(out=Sbf[cs][:], in_=pD[:, 0:128]))
            g_supd = op("vector", lambda e: e.tensor_copy(out=Sst[:], in_=pD[:, 0:128]))
        else:
            g_sbf_prev = op("vector", lambda e, cs=cs, h=h: e.scalar_tensor_tensor(
                out=Sbf[cs][:], in0=Sst[:], scalar=lamT[:, h:h + 1], in1=pD[:, 0:128],
                op0=ALU.mult, op1=ALU.add))
            g_supd = op("vector", lambda e, h=h: e.scalar_tensor_tensor(
                out=Sst[:], in0=Sst[:], scalar=lamT[:, h:h + 1], in1=pD[:, 0:128],
                op0=ALU.mult, op1=ALU.add))
        cglob += 1

    def emit_projq(h):
        nonlocal pieceidx
        for k in range(2):
            fc = (h - 8) * 2 + k
            pslot = pieceidx % 2
            P.wait("gpsimd", dv, v_piece_red[pieceidx - 2])
            vld = dma("gq", wraw[pslot], projT3[fc], sem=wr_s[pslot])
            P.wait("vector", wr_s[pslot], vld)
            op("vector", lambda e, pslot=pslot: e.tensor_scalar(
                out=pstage, in0=wraw[pslot], scalar1=scol[:, 2:3], scalar2=None, op0=ALU.is_gt))
            op("vector", lambda e, pslot=pslot: e.scalar_tensor_tensor(
                out=pstage, in0=wraw[pslot], scalar=scol[:, 3:4], in1=pstage,
                op0=ALU.is_lt, op1=ALU.subtract))
            v_piece_red[pieceidx] = op("vector", lambda e, fc=fc: e.tensor_scalar(
                out=pstage, in0=pstage, scalar1=w12col[:, fc:fc + 1], scalar2=None, op0=ALU.mult))
            pieceidx += 1
            P.wait("gpsimd", dv, v_piece_red[pieceidx - 1])
            v_projq_store[fc] = dma("gq", dwpq[fc], pstage, sem=wpq_st)

    emit_quant(0)
    emit_lamb(0)
    for g in range(12):
        emit_qkv_group(0, g)
    v_qkvevac_done[0] = ac.n
    v_scan_done[-1] = pe.n
    v_qlb = {}
    for h in range(NH):
        pendq = []
        if h + 1 < NH:
            pendq = emit_quant_steps(h + 1)
            emit_lamb(h + 1)
        P.wait("vector", ac, v_lamb[h])
        P.wait("vector", ac, v_qkvevac_done[h])
        P.wait("vector", pe, v_scan_done[h - 1])
        v_qlb[h] = op("vector", lambda e, s2=h % 2: e.tensor_tensor(
            out=qlbuf.rearrange("p (c t) -> p c t", t=128),
            in0=qkv3[s2][:, 0, :].rearrange("p (c t) -> p c t", t=128),
            in1=lamb[s2][:].unsqueeze(1).broadcast_to([128, NT, 128]),
            op=ALU.mult))
        for c in range(NT):
            emit_scan_pre(h, c)
            if c >= 4:
                for _ in range(2):
                    if pendq:
                        pendq.pop(0)()
            if h + 1 < NH and c >= 4:
                emit_qkv_group(h + 1, c - 4)
            emit_scan_post(h, c)
        if h + 1 < NH:
            v_qkvevac_done[h + 1] = ac.n
        v_scan_done[h] = pe.n
        v_ql_last[h] = dv.n
        if h >= 8:
            emit_projq(h)
    v_qkvmm_done[NH - 1] = pe.n
    v_o_all = ac.n
    v_lastpiece = v_piece_red[pieceidx - 1]

    v_beta = [0] * NT
    for tt in range(NT):
        if tt == 0:
            P.wait("vector", ac, v_o_all)
            P.wait("scalar", ac, v_o_all)
        v_w1o = op("vector", lambda e, tt=tt: e.tensor_tensor(out=w1o, in0=o3[:, tt, :], in1=w1bc, op=ALU.mult))
        op("scalar", lambda e, tt=tt: e.activation(out=dmybf, in_=o3[:, tt, :], func=AF.Square,
                                                   accum_out=s1c[:, tt:tt + 1]))
        P.wait("scalar", dv, v_w1o)
        op("scalar", lambda e, tt=tt: e.activation(out=dmybf, in_=w1o, func=AF.Square,
                                                   accum_out=s2c[:, tt:tt + 1]))
        op("scalar", lambda e, tt=tt: e.activation(out=tnyA[:, tt:tt + 1], in_=s1c[:, tt:tt + 1],
                                                   func=AF.Ln, scale=scol[:, 5:6], bias=1e-6))
        selfsem("scalar")
        op("scalar", lambda e, tt=tt: e.activation(out=tnyB[:, tt:tt + 1], in_=tnyA[:, tt:tt + 1],
                                                   func=AF.Exp, scale=-1.0))
        v_e2 = selfsem("scalar")
        P.wait("vector", ac, v_e2)
        op("vector", lambda e, tt=tt: e.scalar_tensor_tensor(out=tnyC[:, tt:tt + 1], in0=s2c[:, tt:tt + 1],
                                                             scalar=scol[:, 5:6], in1=tnyB[:, tt:tt + 1],
                                                             op0=ALU.mult, op1=ALU.mult))
        v_m2 = selfsem("vector")
        P.wait("scalar", dv, v_m2)
        op("scalar", lambda e, tt=tt: e.activation(out=tnyB[:, tt:tt + 1], in_=tnyC[:, tt:tt + 1],
                                                   func=AF.Ln, bias=EPS32))
        v_l2 = selfsem("scalar")
        P.wait("vector", ac, v_l2)
        op("vector", lambda e, tt=tt: e.tensor_tensor(out=tnyD[:, tt:tt + 1], in0=tnyA[:, tt:tt + 1],
                                                      in1=tnyB[:, tt:tt + 1], op=ALU.add))
        v_ls = selfsem("vector")
        P.wait("scalar", dv, v_ls)
        v_beta[tt] = op("scalar", lambda e, tt=tt: e.activation(out=betac[:, tt:tt + 1], in_=tnyD[:, tt:tt + 1],
                                                                func=AF.Exp, scale=-0.5, bias=scol[:, 4:5]))
    dv_p3pre = dv.n
    ac_p25 = ac.n
    if debug:
        P.wait("scalar", ac, ac.n)
        dma("sc", dbg_o, big2[:], sem=dbg_s)
        dma("sc", dbg_small[:, 0:8], scol[:], sem=dbg_s)
        dma("sc", dbg_small[:, 8:24], betac[:], sem=dbg_s)
        dma("sc", dbg_small[:, 24:40], c1c[:], sem=dbg_s)
        dma("sc", dbg_small[:, 40:56], s1c[:], sem=dbg_s)
        vdbg = dma("sc", dbg_small[0:1, 56:64], mrow[0:1, :], sem=dbg_s)
        P.wait("sync", dbg_s, vdbg)

    P.wait("gpsimd", pe, v_qkvmm_done[NH - 1])
    v_wpq = [0] * NFC
    for fc in range(NFC):
        P.wait("gpsimd", wpq_st, v_projq_store[fc])
        v_wpq[fc] = dma("gq", wpqv[:, fc, :], dwpq[fc], sem=wpq_ld)
    v_wpq_all = v_wpq[NFC - 1]

    PB3R = [[pA0, pA1, pB, pC], [pD, pE, pF, pG]]
    v_projmm = {-2: 0, -1: 0}
    v_evac3 = {-2: 0, -1: 0}
    v_x2add = {-2: 0, -1: 0}
    v_x2dma = [0] * NT
    v_finstore = [0] * NT
    v_fint_done = {-2: 0, -1: 0}
    v_oT = 0
    v_oT_by_slot = {0: 0, 1: 0}
    v_oT_by_tt = {}

    def emit_oT(tt):
        nonlocal v_oT
        s2 = tt % 2
        if tt <= 1:
            P.wait("sync", ac, v_o_all)
        P.wait("sync", pe, v_projmm[tt - 2])
        v_oT = dma("sy", oTt[s2].rearrange("p (fc j) -> p fc j", j=128),
                   o3[:, tt, :].rearrange("p (fc j) -> p fc j", j=128), transpose=True, sem=oT_s[s2])
        v_oT_by_slot[s2] = v_oT
        v_oT_by_tt[tt] = v_oT

    emit_oT(0)
    emit_oT(1)
    for tt in range(NT):
        s2 = tt % 2
        if tt >= 2:
            P.wait("scalar", dv, v_x2add[tt - 2])
        else:
            P.wait("scalar", dv, v_lastpiece)
        v_xr = dma("sc", xrl[s2], dx[tt * 128:(tt + 1) * 128, :], sem=wr_s[s2])
        P.wait("tensor", oT_s[s2], v_oT_by_tt[tt])
        PB3 = PB3R[tt % 2]
        if tt <= 1:
            P.wait("tensor", wpq_ld, v_wpq_all)
            P.wait("tensor", dv, dv_p3pre)
            P.wait("tensor", ac, v_o_all)
        pv = v_evac3.get(tt - 2, 0)
        if isinstance(pv, tuple):
            P.wait("tensor", ac, pv[0])
            P.wait("tensor", dv, pv[1])
        else:
            P.wait("tensor", ac, pv)
        vstop = [0] * 4
        for b in range(4):
            for fc in range(NFC):
                vstop[b] = op("tensor", lambda e, b=b, fc=fc, s2=s2, PB3=PB3: e.matmul(
                    PB3[b][:], oTt[s2].rearrange("p (fc j) -> p fc j", j=128)[:, fc, :],
                    wpqv[:, fc, b * 512:(b + 1) * 512], start=(fc == 0), stop=(fc == NFC - 1)))
        v_projmm[tt] = pe.n
        if tt + 2 < NT:
            emit_oT(tt + 2)
        P.wait("scalar", ac, v_beta[tt])
        if tt >= 2:
            P.wait("scalar", x2_s, v_x2dma[tt - 2])
            if stage == "full":
                P.wait("scalar", fs_s, v_finstore[tt - 2])
        v_eva = 0
        v_evd = 0
        for b in range(4):
            if b < 2:
                P.wait("scalar", pe, vstop[b])
                v_eva = op("scalar", lambda e, b=b, s2=s2, tt=tt, PB3=PB3: e.activation(
                    out=x2t[s2][:, b * 512:(b + 1) * 512], in_=PB3[b][:], func=AF.Copy,
                    scale=betac[:, tt:tt + 1]))
            else:
                P.wait("vector", pe, vstop[b])
                v_evd = op("vector", lambda e, b=b, s2=s2, tt=tt, PB3=PB3: e.tensor_scalar(
                    out=x2t[s2][:, b * 512:(b + 1) * 512], in0=PB3[b][:], scalar1=betac[:, tt:tt + 1],
                    scalar2=None, op0=ALU.mult))
        v_evac3[tt] = (v_eva, v_evd)
        P.wait("vector", ac, v_eva)
        P.wait("vector", wr_s[s2], v_xr)
        v_x2add[tt] = op("vector", lambda e, s2=s2: e.tensor_tensor(out=x2t[s2], in0=x2t[s2], in1=xrl[s2], op=ALU.add))
        if stage == "attn":
            P.wait("scalar", dv, v_x2add[tt])
            v_x2dma[tt] = dma("sc", dy[tt * 128:(tt + 1) * 128, :], x2t[s2], sem=x2_s)
            continue
        P.wait("scalar", dv, v_x2add[tt])
        v_x2dma[tt] = dma("sc", dx2[tt], x2t[s2], sem=x2_s)
        op("vector", lambda e, s2=s2, tt=tt: e.tensor_reduce(out=sum2[:, tt:tt + 1], in_=x2t[s2],
                                                             axis=AX.X, op=ALU.add))
        v_st2a = op("scalar", lambda e, s2=s2, tt=tt: e.activation(out=dmybf, in_=x2t[s2], func=AF.Square,
                                                                   accum_out=sq2[:, tt:tt + 1]))
        P.wait("vector", ac, v_st2a)
        op("vector", lambda e, tt=tt: e.tensor_scalar(out=tnyA[:, tt:tt + 1], in0=sum2[:, tt:tt + 1],
                                                      scalar1=1.0 / DM, scalar2=None, op0=ALU.mult))
        op("vector", lambda e, tt=tt: e.tensor_scalar(out=tnyB[:, tt:tt + 1], in0=sq2[:, tt:tt + 1],
                                                      scalar1=1.0 / DM, scalar2=None, op0=ALU.mult))
        selfsem("vector")
        op("vector", lambda e, tt=tt: e.tensor_tensor(out=tnyC[:, tt:tt + 1], in0=tnyA[:, tt:tt + 1],
                                                      in1=tnyA[:, tt:tt + 1], op=ALU.mult))
        selfsem("vector")
        op("vector", lambda e, tt=tt: e.tensor_tensor(out=tnyD[:, tt:tt + 1], in0=tnyB[:, tt:tt + 1],
                                                      in1=tnyC[:, tt:tt + 1], op=ALU.subtract))
        v_var = selfsem("vector")
        P.wait("scalar", dv, v_var)
        op("scalar", lambda e, tt=tt: e.activation(out=tnyB[:, tt:tt + 1], in_=tnyD[:, tt:tt + 1],
                                                   func=AF.Ln, bias=1e-5))
        selfsem("scalar")
        op("scalar", lambda e, tt=tt: e.activation(out=c2c[:, tt:tt + 1], in_=tnyB[:, tt:tt + 1],
                                                   func=AF.Exp, scale=-0.5))
        v_c2 = selfsem("scalar")
        P.wait("vector", ac, v_c2)
        op("vector", lambda e, tt=tt: e.scalar_tensor_tensor(out=b2c[:, tt:tt + 1], in0=tnyA[:, tt:tt + 1],
                                                             scalar=-1.0, in1=c2c[:, tt:tt + 1],
                                                             op0=ALU.mult, op1=ALU.mult))
        v_b2 = selfsem("vector")
        P.wait("scalar", dv, v_b2)
        P.wait("scalar", ft_s, v_fint_done[tt - 2])
        v_fin = op("scalar", lambda e, s2=s2, tt=tt: e.activation(
            out=fint[s2], in_=x2t[s2], func=AF.Identity, scale=c2c[:, tt:tt + 1], bias=b2c[:, tt:tt + 1]))
        P.wait("sync", ac, v_fin)
        if tt == 0:
            P.wait("sync", ac, ac_p25)
            P.wait("sync", dv, dv_p3pre)
        if tt >= 1:
            P.wait("sync", fs_s, v_finstore[tt - 1])
        v_ft = dma("sy", finTt.rearrange("p (fc j) -> p fc j", j=128),
                   fint[s2].rearrange("p (fc j) -> p fc j", j=128), transpose=True, sem=ft_s)
        v_fint_done[tt] = v_ft
        P.wait("scalar", ft_s, v_ft)
        v_finstore[tt] = dma("sc", dffn[tt * 128:(tt + 1) * 128, :], finTt, sem=fs_s)

    if stage == "attn":
        P.wait("sync", x2_s, v_x2dma[NT - 1])
        _emit(nc, P)
        ctx.close()
        return nc

    P.wait("gpsimd", pe, v_projmm[NT - 1])
    v_fload = [0] * NT
    for tt in range(NT):
        P.wait("gpsimd", fs_s, v_finstore[tt])
        v_fload[tt] = dma("gq", finTv[:, :, tt * 128:(tt + 1) * 128],
                          dffn[tt * 128:(tt + 1) * 128, :].rearrange("p (fc j) -> p fc j", j=128), sem=fl_s)
    v_fload_all = v_fload[NT - 1]
    if debug:
        P.wait("scalar", fl_s, v_fload_all)
        dma("sc", dbg_fin, big1[:], sem=dbg_s)

    deq_dv = {}
    deq_po = {}
    kglob = [0]

    def dequant_tile(dst_ap, wsrc, ssrc):
        k = kglob[0]
        kglob[0] += 1
        b4 = k % 4
        b2r = k % 2
        if k >= 4:
            P.wait("sync", dv, deq_dv[k - 4])
            P.wait("gpsimd", dv, deq_dv[k - 4])
        vw = dma("sy", wu16[b4], wsrc, sem=wu_s[b4])
        vs = dma("gq", sbct[b4], ssrc, sem=sb_s[b4])
        P.wait("vector", wu_s[b4], vw)
        op("vector", lambda e, b4=b4, b2r=b2r: e.tensor_scalar(
            out=ntl[b2r], in0=wu16[b4], scalar1=shiftc[:, 0:1], scalar2=15,
            op0=ALU.logical_shift_right, op1=ALU.bitwise_and))
        op("vector", lambda e, b4=b4, b2r=b2r: e.tensor_scalar(
            out=atl[b2r], in0=ntl[b2r], scalar1=9, scalar2=0x0E00,
            op0=ALU.logical_shift_left, op1=ALU.bitwise_and))
        op("vector", lambda e, b4=b4, b2r=b2r: e.tensor_scalar(
            out=btl[b2r], in0=ntl[b2r], scalar1=12, scalar2=0x8000,
            op0=ALU.logical_shift_left, op1=ALU.bitwise_and))
        op("vector", lambda e, b4=b4, b2r=b2r: e.tensor_tensor(
            out=wu16[b4], in0=atl[b2r], in1=btl[b2r], op=ALU.bitwise_or))
        P.wait("vector", sb_s[b4], vs)
        deq_dv[k] = op("vector", lambda e, b4=b4, dst_ap=dst_ap: e.scalar_tensor_tensor(
            out=dst_ap, in0=wu16[b4].bitcast(F16), scalar=16384.0, in1=sbct[b4],
            op0=ALU.mult, op1=ALU.mult))
        return deq_dv[k]

    PBG = [[pA0, pA1], [pB, pC], [pD, pE], [pF, pG]]
    v_deq_done = {}
    v_mm4_done = {-2: 0, -1: 0}
    v_silu = {}
    v_hactm = {}
    v_hstore = {-2: 0, -1: 0}
    v_htr_by_ib = {}

    def deq_args(ib):
        wslot = ib % 2
        out = []
        for half in range(2):
            base_o = half * NI + ib * 512
            for fc in range(NFC):
                out.append((wdqv[:, wslot, half, fc, :],
                            dguw[fc * 128:(fc + 1) * 128, base_o:base_o + 512],
                            dgus[fc:fc + 1, base_o:base_o + 512].broadcast_to([128, 512])))
        return out

    P.wait("vector", oT_s[0], v_oT_by_slot[0])
    P.wait("vector", oT_s[1], v_oT_by_slot[1])
    P.wait("vector", ac, v_beta[NT - 1])
    for a in deq_args(0):
        dequant_tile(*a)
    v_deq_done[0] = dv.n
    v_p5segA = 0
    for ib in range(11):
        wslot = ib % 2
        if ib < 10:
            pend = deq_args(ib + 1)
        else:
            pend = []
            for ic in range(16):
                for ob in range(2):
                    cols = ob * 512
                    pend.append((big2[:, 16384 + ic * 1024 + ob * 512: 16384 + ic * 1024 + (ob + 1) * 512],
                                 ddnw[ic * 128:(ic + 1) * 128, cols:cols + 512],
                                 ddns[ic:ic + 1, cols:cols + 512].broadcast_to([128, 512])))
        if pend:
            P.wait("vector", pe, v_mm4_done[ib - 1])
        pi = 0
        for tt in range(NT):
            bsel = tt % 4
            gbank, ubank = PBG[bsel]
            P.wait("tensor", dv, v_deq_done[ib])
            if ib == 0 and tt == 0:
                P.wait("tensor", fl_s, v_fload_all)
            if ib == 0 and tt <= 3:
                P.wait("tensor", ac, v_evac3[NT - 1][0])
                P.wait("tensor", dv, v_evac3[NT - 1][1])
            P.wait("tensor", ac, v_silu.get((ib, tt - 4), v_silu.get((ib - 1, tt + 12), 0)))
            P.wait("tensor", dv, v_hactm.get((ib, tt - 4), v_hactm.get((ib - 1, tt + 12), 0)))
            vg = 0
            for fc in range(NFC):
                vg = op("tensor", lambda e, fc=fc, tt=tt, wslot=wslot, gbank=gbank: e.matmul(
                    gbank[:], finTv[:, fc, tt * 128:(tt + 1) * 128], wdqv[:, wslot, 0, fc, :],
                    start=(fc == 0), stop=(fc == NFC - 1)))
            vu = 0
            for fc in range(NFC):
                vu = op("tensor", lambda e, fc=fc, tt=tt, wslot=wslot, ubank=ubank: e.matmul(
                    ubank[:], finTv[:, fc, tt * 128:(tt + 1) * 128], wdqv[:, wslot, 1, fc, :],
                    start=(fc == 0), stop=(fc == NFC - 1)))
            P.wait("scalar", pe, vg)
            if tt >= 2:
                P.wait("scalar", dv, v_hactm[(ib, tt - 2)])
            elif ib >= 1:
                P.wait("scalar", dv, v_hactm[(ib - 1, tt + 14)])
            v_silu[(ib, tt)] = op("scalar", lambda e, tt=tt, gbank=gbank: e.activation(
                out=sg[tt % 2][:], in_=gbank[:], func=AF.Silu))
            P.wait("vector", pe, vu)
            P.wait("vector", ac, v_silu[(ib, tt)])
            if tt == 0 and ib >= 2:
                P.wait("vector", hx_s, v_htr_by_ib[ib - 2])
            v_hactm[(ib, tt)] = op("vector", lambda e, tt=tt, ubank=ubank, ib=ib: e.tensor_tensor(
                out=hact[ib % 2][:, tt * 512:(tt + 1) * 512], in0=sg[tt % 2][:], in1=ubank[:], op=ALU.mult))
            while pend and pi < 3 * (tt + 1):
                dequant_tile(*pend.pop(0))
                pi += 1
        while pend:
            dequant_tile(*pend.pop(0))
        if ib < 10:
            v_deq_done[ib + 1] = dv.n
        else:
            v_p5segA = dv.n
        v_mm4_done[ib] = pe.n
        P.wait("sync", dv, v_hactm[(ib, NT - 1)])
        if ib >= 1:
            P.wait("sync", hs_s, v_hstore[ib - 1])
        v_htr_by_ib[ib] = dma("sy", hactT[0].rearrange("p (t j) -> p t j", j=128),
                              hact[ib % 2].rearrange("p (t j) -> p t j", j=128), transpose=True, sem=hx_s)
        P.wait("scalar", hx_s, v_htr_by_ib[ib])
        for jj in range(4):
            v_hstore[ib] = dma("sc", dhactT[:, :, ib * 4 + jj, :].rearrange("t p q -> p t q"),
                               hactT[0].rearrange("p (t j q) -> p t j q", j=4, q=128)[:, :, jj, :], sem=hs_s)
    v_p4_done = pe.n
    ac_p4_done = ac.n
    dv_p4_done = dv.n

    def wdn_ap(half, ic, lo, hi):
        if ic < 16:
            base = (16384 if half == 0 else 0) + ic * 1024
            return big2[:, base + lo: base + hi]
        base = (ic - 16) * 1024
        return big1[:, base + lo: base + hi]

    v_out = 0
    v_segA = {}
    v_segB = {}
    v_segA[0] = v_p5segA
    for half in range(2):
        if half == 0:
            P.wait("vector", pe, v_p4_done)
        else:
            P.wait("vector", pe, v_half_done)
        for ic in range(16, NIC):
            for ob in range(2):
                cols = half * 1024 + ob * 512
                dequant_tile(wdn_ap(half, ic, ob * 512, (ob + 1) * 512),
                             ddnw[ic * 128:(ic + 1) * 128, cols:cols + 512],
                             ddns[ic:ic + 1, cols:cols + 512].broadcast_to([128, 512]))
        v_segB[half] = dv.n
        P.wait("gpsimd", hs_s, v_hstore[10])
        v_dnmm = {-2: v_p4_done if half == 0 else v_half_done,
                  -1: v_p4_done if half == 0 else v_half_done}
        if half == 0:
            v_sq_prev = ac_p4_done
            v_yev_prev = dv_p4_done
        else:
            v_sq_prev = v_sq2[NT - 1]
            v_yev_prev = v_yev2[NT - 1]
        v_sq2 = {}
        v_yev2 = {}
        segA1_pend = []
        if half == 0:
            for ic in range(16):
                for ob in range(2):
                    cols = 1024 + ob * 512
                    segA1_pend.append((wdn_ap(1, ic, ob * 512, (ob + 1) * 512),
                                       ddnw[ic * 128:(ic + 1) * 128, cols:cols + 512],
                                       ddns[ic:ic + 1, cols:cols + 512].broadcast_to([128, 512])))
        for tt in range(NT):
            s2 = tt % 2
            if tt >= 2:
                P.wait("gpsimd", pe, v_dnmm[tt - 2])
            v_hl = dma("gq", hA[s2], dhactT[tt].rearrange("p ic q -> p (ic q)"), sem=ha_s[s2])
            P.wait("tensor", ha_s[s2], v_hl)
            P.wait("tensor", dv, v_segA[half])
            if tt >= 2 or half == 1:
                P.wait("tensor", dv, v_yev2.get(tt - 2, v_yev_prev))
                P.wait("tensor", ac, v_sq2.get(tt - 2, v_sq_prev))
            else:
                P.wait("tensor", dv, v_yev_prev)
                P.wait("tensor", ac, v_sq_prev)
            pd0, pd1 = ([pA0, pA1] if tt % 2 == 0 else [pB, pC])
            v0 = 0
            for ic in range(NIC):
                if tt == 0 and ic == 16:
                    P.wait("tensor", dv, v_segB[half])
                lhs = hA[s2][:, ic * 128:(ic + 1) * 128]
                rhs = wdn_ap(half, ic, 0, 512)
                v0 = op("tensor", lambda e, lhs=lhs, rhs=rhs, ic=ic, pd0=pd0: e.matmul(
                    pd0[:], lhs, rhs, start=(ic == 0), stop=(ic == NIC - 1)))
            v1 = 0
            for ic in range(NIC):
                lhs = hA[s2][:, ic * 128:(ic + 1) * 128]
                rhs = wdn_ap(half, ic, 512, 1024)
                v1 = op("tensor", lambda e, lhs=lhs, rhs=rhs, ic=ic, pd1=pd1: e.matmul(
                    pd1[:], lhs, rhs, start=(ic == 0), stop=(ic == NIC - 1)))
            v_dnmm[tt] = pe.n
            for _ in range(2):
                if segA1_pend:
                    dequant_tile(*segA1_pend.pop(0))
            if half == 0 and tt == NT - 1:
                while segA1_pend:
                    dequant_tile(*segA1_pend.pop(0))
                v_segA[1] = dv.n
            P.wait("scalar", pe, v0)
            col0 = (2 * half) * NT + tt
            op("scalar", lambda e, col0=col0, pd0=pd0: e.activation(
                out=dmy512, in_=pd0[:], func=AF.Square, accum_out=ssqs[:, col0:col0 + 1]))
            P.wait("scalar", pe, v1)
            col1 = (2 * half + 1) * NT + tt
            v_sq = op("scalar", lambda e, col1=col1, pd1=pd1: e.activation(
                out=dmy512, in_=pd1[:], func=AF.Square, accum_out=ssqs[:, col1:col1 + 1]))
            v_sq2[tt] = v_sq
            ybuf = y0t if half == 0 else y1t
            P.wait("vector", ac, v_sq)
            op("vector", lambda e, ybuf=ybuf, pd0=pd0: e.tensor_copy(out=ybuf[:, 0:512], in_=pd0[:]))
            v_yev = op("vector", lambda e, ybuf=ybuf, pd1=pd1: e.tensor_copy(out=ybuf[:, 512:1024], in_=pd1[:]))
            v_yev2[tt] = v_yev
            if half == 0:
                P.wait("scalar", dv, v_yev)
                vst = dma("sc", dyh[tt], y0t, sem=ys_s)
                P.wait("vector", ys_s, vst)
            else:
                P.wait("vector", ac, v_sq)
                op("vector", lambda e, tt=tt: e.tensor_tensor(
                    out=tnyA[:, tt:tt + 1], in0=ssqs[:, tt:tt + 1], in1=ssqs[:, NT + tt:NT + tt + 1], op=ALU.add))
                selfsem("vector")
                op("vector", lambda e, tt=tt: e.tensor_tensor(
                    out=tnyB[:, tt:tt + 1], in0=ssqs[:, 2 * NT + tt:2 * NT + tt + 1],
                    in1=ssqs[:, 3 * NT + tt:3 * NT + tt + 1], op=ALU.add))
                selfsem("vector")
                v_sst = op("vector", lambda e, tt=tt: e.tensor_tensor(
                    out=tnyC[:, tt:tt + 1], in0=tnyA[:, tt:tt + 1], in1=tnyB[:, tt:tt + 1], op=ALU.add))
                P.wait("scalar", dv, v_sst)
                op("scalar", lambda e, tt=tt: e.activation(out=tnyD[:, tt:tt + 1], in_=tnyC[:, tt:tt + 1],
                                                           func=AF.Ln, scale=1.0 / DM, bias=1e-6))
                selfsem("scalar")
                v_r3 = op("scalar", lambda e, tt=tt: e.activation(out=r3c[:, tt:tt + 1], in_=tnyD[:, tt:tt + 1],
                                                                  func=AF.Exp, scale=-0.5))
                v_x2l = dma("sc", x2rl, dx2[tt], sem=xy_s)
                v_y0l = dma("sc", y0t, dyh[tt], sem=xy_s)
                P.wait("vector", xy_s, v_y0l)
                P.wait("vector", ac, v_r3)
                op("vector", lambda e, tt=tt: e.scalar_tensor_tensor(
                    out=y0t, in0=y0t, scalar=r3c[:, tt:tt + 1], in1=x2rl[:, 0:1024],
                    op0=ALU.mult, op1=ALU.add))
                v_o1 = op("vector", lambda e, tt=tt: e.scalar_tensor_tensor(
                    out=y1t, in0=y1t, scalar=r3c[:, tt:tt + 1], in1=x2rl[:, 1024:2048],
                    op0=ALU.mult, op1=ALU.add))
                P.wait("scalar", dv, v_o1)
                dma("sc", dy[tt * 128:(tt + 1) * 128, 0:1024], y0t, sem=yo_s)
                v_out = dma("sc", dy[tt * 128:(tt + 1) * 128, 1024:2048], y1t, sem=yo_s)
                P.wait("vector", yo_s, v_out)
        v_half_done = pe.n
    P.wait("sync", yo_s, v_out)

    _emit(nc, P)
    ctx.close()
    return nc


def _emit(nc, P):
    with nc.Block() as block:
        @block.sync
        def _(e):
            P.run("sync", e)

        @block.scalar
        def _(e):
            P.run("scalar", e)

        @block.vector
        def _(e):
            P.run("vector", e)

        @block.gpsimd
        def _(e):
            P.run("gpsimd", e)

        @block.tensor
        def _(e):
            P.run("tensor", e)


def host_prepare(inputs):
    x = np.ascontiguousarray(np.asarray(inputs["L0_attn_x"], dtype=np.float32))
    qkv_w = np.asarray(inputs["L0_attn_qkv_w"], dtype=np.float32)
    proj_w = np.asarray(inputs["L0_attn_proj_w"], dtype=np.float32)
    qkvT = np.ascontiguousarray(qkv_w.T)
    projT = np.ascontiguousarray(proj_w.T)
    gcol = np.ascontiguousarray(np.asarray(inputs["L0_attn_qkv_norm_w"], np.float32).reshape(NFC, 128).T)
    w1 = np.asarray(inputs["L0_attn_norm_w"], np.float32)
    w2 = np.asarray(inputs["L0_attn_proj_norm_w"], np.float32)
    w1col = np.ascontiguousarray(w1.reshape(NFC, 128).T)
    w2col = np.ascontiguousarray(w2.reshape(NFC, 128).T)
    w1row = np.ascontiguousarray(w1.reshape(1, DM))
    srow = np.ascontiguousarray(np.asarray(inputs["L0_attn_s"], np.float32).reshape(1, NH))
    gu = np.ascontiguousarray(np.asarray(inputs["L0_ffn_gate_up_packed"], np.int32))
    guw = np.ascontiguousarray(gu.view(np.uint16)[:, (np.arange(DM) // 4)].T)
    gus = np.ascontiguousarray(np.asarray(inputs["L0_ffn_gate_up_scales"], np.float16))
    dn = np.ascontiguousarray(np.asarray(inputs["L0_ffn_down_packed"], np.int32))
    dnw = np.ascontiguousarray(dn.view(np.uint16)[:, (np.arange(NI) // 4)].T)
    dns = np.ascontiguousarray(np.asarray(inputs["L0_ffn_down_scales"], np.float16))
    shared = dict(qkvT=qkvT, projT=projT, gcol=gcol, w1col=w1col, w2col=w2col,
                  w1row=w1row, srow=srow, guw=guw, gus=gus, dnw=dnw, dns=dns)
    return x, shared


def kernel(**inputs):
    x, shared = host_prepare(inputs)
    nc = build_nc("full")
    in_maps = []
    for core in range(8):
        m = dict(shared)
        m["x"] = np.ascontiguousarray(x[core])
        in_maps.append(m)
    res = run_bass_kernel_spmd(nc, in_maps, core_ids=list(range(8)))
    out = np.stack([res.results[i]["y"] for i in range(8)], axis=0)
    return out.astype(np.float32)


# version 25
# speedup vs baseline: 1.0766x; 1.0645x over previous
import numpy as np
import ml_dtypes
from contextlib import ExitStack

import concourse.bass as bass
import concourse.mybir as mybir
from concourse.bass_utils import run_bass_kernel_spmd

F32 = mybir.dt.float32
BF16 = mybir.dt.bfloat16
F16 = mybir.dt.float16
U16 = mybir.dt.uint16
I16 = mybir.dt.int16
AF = mybir.ActivationFunctionType
ALU = mybir.AluOpType
AX = mybir.AxisListType

DM = 2048
NT = 16
NH = 16
NI = 5632
NIC = NI // 128
NFC = 16
EPS32 = float(np.finfo(np.float32).eps)
SCALE_O = float(128 ** -0.5)
QN = 3 * DM * DM
PN = DM * DM


class Sem:
    def __init__(self, ctx, nc, name):
        self.h = ctx.enter_context(nc.semaphore(name))
        self.n = 0

    def bump(self, k=1):
        self.n += k
        return self.n


class Plan:
    def __init__(self):
        self.ops = []

    def on(self, eng, fn):
        self.ops.append((eng, fn))

    def wait(self, eng, sem, val):
        if val is None or val <= 0:
            return
        self.ops.append((eng, lambda e, s=sem.h, v=val: e.wait_ge(s, v)))

    def wait_lazy(self, eng, sem, fn):
        self.ops.append((eng, lambda e, s=sem.h, f=fn: e.wait_ge(s, f())))

    def run(self, eng_key, eng):
        for k, fn in self.ops:
            if k == eng_key:
                fn(eng)


def build_nc(stage="full", debug=False):
    nc = bass.Bass("TRN2", target_bir_lowering=False, debug=False)
    ctx = ExitStack()

    dx = nc.dram_tensor("x", [DM, DM], F32, kind="ExternalInput").ap()
    dqkvT = nc.dram_tensor("qkvT", [DM, 3 * DM], F32, kind="ExternalInput").ap()
    dprojT = nc.dram_tensor("projT", [DM, DM], F32, kind="ExternalInput").ap()
    dgcol = nc.dram_tensor("gcol", [128, NFC], F32, kind="ExternalInput").ap()
    dw1col = nc.dram_tensor("w1col", [128, NFC], F32, kind="ExternalInput").ap()
    dw2col = nc.dram_tensor("w2col", [128, NFC], F32, kind="ExternalInput").ap()
    dw1row = nc.dram_tensor("w1row", [1, DM], F32, kind="ExternalInput").ap()
    dsrow = nc.dram_tensor("srow", [1, NH], F32, kind="ExternalInput").ap()
    dguw = nc.dram_tensor("guw", [DM, 2 * NI], U16, kind="ExternalInput").ap()
    dgus = nc.dram_tensor("gus", [NFC, 2 * NI], F16, kind="ExternalInput").ap()
    ddnw = nc.dram_tensor("dnw", [NI, DM], U16, kind="ExternalInput").ap()
    ddns = nc.dram_tensor("dns", [NIC, DM], F16, kind="ExternalInput").ap()
    dy = nc.dram_tensor("y", [DM, DM], F32, kind="ExternalOutput").ap()
    if debug:
        dbg_hT = nc.dram_tensor("dbg_hT", [128, 32768], BF16, kind="ExternalOutput").ap()
        dbg_o = nc.dram_tensor("dbg_o", [128, 32768], BF16, kind="ExternalOutput").ap()
        dbg_small = nc.dram_tensor("dbg_small", [128, 64], F32, kind="ExternalOutput").ap()
        dbg_qkv = nc.dram_tensor("dbg_qkv", [128, 6144], BF16, kind="ExternalOutput").ap()
        dbg_fin = nc.dram_tensor("dbg_fin", [128, 32768], BF16, kind="ExternalOutput").ap()
        dbg_wdq = nc.dram_tensor("dbg_wdq", [128, 32768], BF16, kind="ExternalOutput").ap()
        dbg_hact = nc.dram_tensor("dbg_hact", [128, 8192], BF16, kind="ExternalOutput").ap()
        dbg_ssq = nc.dram_tensor("dbg_ssq", [128, 80], F32, kind="ExternalOutput").ap()
        dbg_p5 = nc.dram_tensor("dbg_p5", [128, 8192], BF16, kind="ExternalOutput").ap()

    dwpq = nc.dram_tensor("wpq_s", [NFC, 128, DM], BF16, kind="Internal").ap()
    dx2 = nc.dram_tensor("x2_s", [NT, 128, DM], F32, kind="Internal").ap()
    dffn = nc.dram_tensor("ffn_s", [DM, DM], BF16, kind="Internal").ap()
    dhactT = nc.dram_tensor("hactT_s", [NT, 128, NIC, 128], BF16,
                            kind="ExternalOutput" if debug else "Internal").ap()
    dyh = nc.dram_tensor("yh_s", [NT, 128, 1024], F32, kind="Internal").ap()

    sb = lambda n, s, d, **kw: ctx.enter_context(nc.sbuf_tensor(n, s, d, **kw))
    big1 = sb("big1", [128, 32768], BF16, is_transpose_dest=True)
    big2 = sb("big2", [128, 32768], BF16, is_transpose_dest=True)
    pool3 = sb("pool3", [128, 32768], BF16, is_transpose_dest=True)

    def v3(a, b, dt=BF16):
        ap = pool3[:, a:b]
        return ap if dt == BF16 else ap.bitcast(dt)

    wraw = [v3(0, 4096, F32), v3(4096, 8192, F32)]
    wq = pool3[:, 8192:14336]
    wq3 = wq.rearrange("p (t3 fc j) -> p t3 fc j", t3=3, j=128)
    qlbuf = pool3[:, 14336:16384]
    qkvsb = [pool3[:, 16384:22528], pool3[:, 22528:28672]]
    qkv3 = [q.rearrange("p (t3 t) -> p t3 t", t3=3) for q in qkvsb]
    passA = [v3(16384 + 4096 * i, 16384 + 4096 * (i + 1), F32) for i in range(3)]
    pstage = pool3[:, 28672:30720]
    w1bc = pool3[:, 30720:32768]
    ht = [pool3[:, 8192:10240], pool3[:, 10240:12288]]
    xrl = [v3(0, 4096, F32), v3(4096, 8192, F32)]
    oTt = [pool3[:, 8192:10240], pool3[:, 10240:12288]]
    fint = [pool3[:, 12288:14336], pool3[:, 14336:16384]]
    x2t = [v3(16384, 20480, F32), v3(20480, 24576, F32)]
    w1o = pool3[:, 24576:26624]
    finTt = pool3[:, 24576:26624]
    dmybf = pool3[:, 26624:28672]
    hact = [pool3[:, 0:8192], pool3[:, 8192:16384]]
    hactT = [pool3[:, 24576:32768]]
    wu16 = [v3(16384 + 512 * i, 16384 + 512 * (i + 1), U16) for i in range(4)]
    sbct = [pool3[:, 18432 + 512 * i: 18432 + 512 * (i + 1)] for i in range(4)]
    ntl = [v3(20480 + 512 * i, 20480 + 512 * (i + 1), U16) for i in range(2)]
    atl = [v3(21504 + 512 * i, 21504 + 512 * (i + 1), U16) for i in range(2)]
    btl = [v3(22528 + 512 * i, 22528 + 512 * (i + 1), U16) for i in range(2)]
    sg = [pool3[:, 23552 + 512 * i: 23552 + 512 * (i + 1)] for i in range(2)]
    dmy512 = pool3[:, 24064:24576]
    hA = [pool3[:, 0:5632], pool3[:, 5632:11264]]
    y0t = v3(11264, 13312, F32)
    y1t = v3(13312, 15360, F32)
    x2rl = v3(25600, 29696, F32)

    constt = sb("constt", [128, 8], F32)
    _cvals = [EPS32, 1e-6, -1.0, -0.5, SCALE_O, 1e-5, 1.0 / DM, 16384.0]
    for _i, _v in enumerate(_cvals):
        nc.const_aps.register(F32, float(_v), constt[:, _i:_i + 1])

    mk = [sb(f"mk{i}", [128, 128], BF16) for i in range(2)]
    ql = [sb(f"ql{i}", [128, 128], BF16) for i in range(2)]
    kl = [sb(f"kl{i}", [128, 128], BF16) for i in range(2)]
    vtk = [sb(f"vtk{i}", [128, 128], BF16) for i in range(2)]
    Sst = sb("Sst", [128, 128], F32)
    Sbf = [sb(f"Sbf{i}", [128, 128], BF16) for i in range(2)]
    masks = sb("masks", [128, NH * 128], BF16)
    lamb = [sb(f"lamb{i}", [128, 128], BF16) for i in range(2)]
    gcol = sb("gcolS", [128, NFC], F32)
    w1col = sb("w1colS", [128, NFC], F32)
    w2col = sb("w2colS", [128, NFC], F32)
    w12col = sb("w12colS", [128, NFC], F32)
    srep = sb("srep", [128, NH], F32)
    negs = sb("negs", [128, NH], F32)
    lamrev = sb("lamrev", [128, NH], F32)
    lamT = sb("lamT", [128, NH], F32)
    tcolt = sb("tcolt", [128, 128], F32)
    diff = sb("diff", [128, 128], F32)
    tri = sb("tri", [128, 128], F32)
    ident = sb("identS", [128, 128], BF16)
    trev = sb("trev", [128, 1], F32)
    c128 = sb("c128", [128, 1], F32)
    pcol = sb("pcol", [128, 1], I16)
    shiftc = sb("shiftc", [128, 1], I16)
    ones1 = sb("ones1", [1, 128], F32)
    onescol = sb("onescol", [128, 1], F32)
    absacc = sb("absacc", [128, 64], F32)
    redt = sb("redt", [128, 2], F32)
    mrow = sb("mrow", [1, 8], F32)
    m2row = sb("m2row", [1, 8], F32)
    brow = sb("brow", [1, 8], F32)
    scol = sb("scolS", [128, 8], F32)
    sumc = sb("sumc", [128, NT], F32)
    sqc = sb("sqc", [128, NT], F32)
    c1c = sb("c1c", [128, NT], F32)
    b1c = sb("b1c", [128, NT], F32)
    tnyA = sb("tnyA", [128, NT], F32)
    tnyB = sb("tnyB", [128, NT], F32)
    tnyC = sb("tnyC", [128, NT], F32)
    tnyD = sb("tnyD", [128, NT], F32)
    s1c = sb("s1c", [128, NT], F32)
    s2c = sb("s2c", [128, NT], F32)
    betac = sb("betac", [128, NT], F32)
    ssqs = sb("ssqs", [128, 4 * NT], F32)
    r3c = sb("r3c", [128, NT], F32)
    sum2 = sb("sum2", [128, NT], F32)
    sq2 = sb("sq2", [128, NT], F32)
    c2c = sb("c2c", [128, NT], F32)
    b2c = sb("b2c", [128, NT], F32)

    hTr = big1[:].rearrange("p (fc t) -> p fc t", t=2048)
    wpqv = big1[:].rearrange("p (fc o) -> p fc o", o=2048)
    finTv = big1[:].rearrange("p (fc t) -> p fc t", t=2048)
    o3 = big2[:].rearrange("p (c f) -> p c f", f=2048)
    wdqv = big2[:].rearrange("p (w h fc j) -> p w h fc j", w=2, h=2, j=512)
    wdn1 = big1[:].rearrange("p (ic o) -> p ic o", o=1024)
    wdn2 = big2[:, 0:12288].rearrange("p (ic o) -> p ic o", o=1024)
    dmyP1 = pool3[:, 28672:30720]
    mskP0 = pool3[:, 14336:14592].bitcast(F32)[:, 0:128]

    ps = lambda n: ctx.enter_context(nc.psum_tensor(n, [128, 512], F32))
    pA0, pA1, pB, pC, pD, pE, pF, pG = [ps(f"ps{i}") for i in range(8)]

    sy = Sem(ctx, nc, "sy")
    sc = Sem(ctx, nc, "sc")
    scb = Sem(ctx, nc, "scb")
    gq = Sem(ctx, nc, "gq")
    gqb = Sem(ctx, nc, "gqb")
    wr_s = [Sem(ctx, nc, f"wr{i}") for i in range(2)]
    pa_s = [Sem(ctx, nc, f"pa{i}") for i in range(3)]
    wu_s = [Sem(ctx, nc, f"wu{i}") for i in range(4)]
    sb_s = [Sem(ctx, nc, f"sb{i}") for i in range(4)]
    oT_s = [Sem(ctx, nc, f"oT{i}") for i in range(2)]
    hT_s = [Sem(ctx, nc, f"hT{i}") for i in range(2)]
    ha_s = [Sem(ctx, nc, f"ha{i}") for i in range(2)]
    ft_s = Sem(ctx, nc, "ft")
    hx_s = Sem(ctx, nc, "hx")
    wpq_st = Sem(ctx, nc, "wpqst")
    wpq_ld = Sem(ctx, nc, "wpqld")
    fl_s = Sem(ctx, nc, "fl")
    xy_s = Sem(ctx, nc, "xy")
    x2_s = Sem(ctx, nc, "x2s")
    fs_s = Sem(ctx, nc, "fss")
    hs_s = Sem(ctx, nc, "hss")
    ys_s = Sem(ctx, nc, "yss")
    yo_s = Sem(ctx, nc, "yos")
    dbg_s = Sem(ctx, nc, "dbgs")
    pe = Sem(ctx, nc, "pe")
    dv = Sem(ctx, nc, "dv")
    ac = Sem(ctx, nc, "ac")
    po = Sem(ctx, nc, "po")
    P = Plan()
    RING = {"sy": "sync", "sc": "scalar", "gq": "gpsimd"}
    RSEM = {"sy": sy, "sc": sc, "gq": gq}

    def dma(ring, dst, src, transpose=False, slow=False, sem=None):
        if sem is None:
            sem = RSEM[ring]
        v = sem.bump(16)
        kw = {}
        if transpose:
            kw["transpose"] = True
        if slow:
            kw["allow_slow_non_contiguous"] = True
        P.on(RING[ring], lambda e, d=dst, s=src, k=dict(kw): e.dma_start(out=d, in_=s, **k).then_inc(sem.h, 16))
        return v

    ESEM = {"vector": dv, "scalar": ac, "gpsimd": po, "tensor": pe}

    def op(eng, fn):
        sem = ESEM[eng]
        v = sem.bump(1)
        P.on(eng, lambda e, f=fn: f(e).then_inc(sem.h, 1))
        return v

    def selfsem(eng):
        v = op(eng, lambda e: e.nop())
        P.wait(eng, ESEM[eng], v)
        return v

    dma("gq", gcol[:], dgcol)
    dma("gq", w1col[:], dw1col)
    dma("gq", w2col[:], dw2col)
    dma("gq", srep[:], dsrow.broadcast_to([128, NH]), sem=gqb)
    v_w1bc = dma("gq", w1bc, dw1row.broadcast_to([128, DM]), sem=gqb)
    op("gpsimd", lambda e: e.iota(out=diff[:], pattern=[[1, 128]], base=0,
                                  channel_multiplier=-1, allow_small_or_imprecise_dtypes=True))
    op("gpsimd", lambda e: e.iota(out=tcolt[:], pattern=[[1, 128]], base=0,
                                  channel_multiplier=0, allow_small_or_imprecise_dtypes=True))
    op("gpsimd", lambda e: e.iota(out=trev[:], pattern=[[0, 1]], base=128,
                                  channel_multiplier=-1, allow_small_or_imprecise_dtypes=True))
    op("gpsimd", lambda e: e.iota(out=pcol[:], pattern=[[0, 1]], base=0,
                                  channel_multiplier=1))
    v_iota = op("gpsimd", lambda e: e.memset(c128[:], 128.0))
    P.wait("vector", po, v_iota)
    P.wait("vector", gqb, v_w1bc)
    P.wait("vector", gq, 48)
    op("vector", lambda e: e.tensor_scalar(out=tri[:], in0=diff[:], scalar1=0.0, scalar2=None, op0=ALU.is_ge))
    op("vector", lambda e: e.tensor_scalar(out=ident[:], in0=diff[:], scalar1=0.0, scalar2=None, op0=ALU.is_equal))
    op("vector", lambda e: e.tensor_scalar(out=negs[:], in0=srep[:], scalar1=-1.0, scalar2=None, op0=ALU.mult))
    op("vector", lambda e: e.tensor_tensor(out=w12col[:], in0=w1col[:], in1=w2col[:], op=ALU.mult))
    op("vector", lambda e: e.tensor_scalar(out=shiftc[:], in0=pcol[:], scalar1=3, scalar2=2,
                                           op0=ALU.bitwise_and, op1=ALU.logical_shift_left))
    op("vector", lambda e: e.memset(ones1[:], 1.0))
    for _i, _v in enumerate(_cvals):
        op("vector", lambda e, _i=_i, _v=_v: e.memset(constt[:, _i:_i + 1], float(_v)))
    v_dvsmall = selfsem("vector")
    P.wait("scalar", dv, v_dvsmall)
    v_mtt = 0
    for h in range(NH):
        op("scalar", lambda e, h=h: e.activation(out=lamrev[:, h:h + 1], in_=trev[:], func=AF.Exp,
                                                 scale=negs[:, h:h + 1]))
        op("scalar", lambda e, h=h: e.activation(out=lamT[:, h:h + 1], in_=c128[:], func=AF.Exp,
                                                 scale=negs[:, h:h + 1]))
        P.wait("scalar", dv, v_mtt)
        v_me = op("scalar", lambda e, h=h: e.activation(out=mskP0, in_=diff[:], func=AF.Exp,
                                                        scale=negs[:, h:h + 1]))
        P.wait("vector", ac, v_me)
        v_mtt = op("vector", lambda e, h=h: e.tensor_tensor(out=masks[:, h * 128:(h + 1) * 128],
                                                            in0=mskP0, in1=tri[:], op=ALU.mult))
    v_masks = dv.n
    v_lamtabs = ac.n

    qkvT3 = dqkvT.rearrange("(fc p) c -> fc p c", p=128)
    projT3 = dprojT.rearrange("(fc p) c -> fc p c", p=128)
    vA_red = {}
    PA_RING = {0: "gq", 1: "sy", 2: "sc"}
    PA_ENG = {"gq": "gpsimd", "sy": "sync", "sc": "scalar"}
    idx = 0
    srcs = [qkvT3[fc, :, j * 2048:(j + 1) * 2048] for fc in range(NFC) for j in range(3)]
    srcs += [projT3[fc] for fc in range(NFC)]
    vA_eng = {}
    for src in srcs:
        s3 = idx % 3
        ring = PA_RING[s3]
        eng = "vector" if (idx % 2 == 0) else "scalar"
        if idx >= 3:
            peng, pv = vA_eng[idx - 3], vA_red[idx - 3]
            P.wait(PA_ENG[ring], dv if peng == "vector" else ac, pv)
        vld = dma(ring, passA[s3], src, sem=pa_s[s3])
        P.wait(eng, pa_s[s3], vld)
        if eng == "vector":
            vA_red[idx] = op("vector", lambda e, s3=s3, i=idx: e.tensor_reduce(
                out=absacc[:, i:i + 1], in_=passA[s3], axis=AX.X, op=ALU.add, apply_absolute_value=True))
        else:
            vA_red[idx] = op("scalar", lambda e, s3=s3, i=idx: e.activation(
                out=dmyP1, in_=passA[s3], func=AF.Abs, accum_out=absacc[:, i:i + 1]))
        vA_eng[idx] = eng
        idx += 1
    v_passA_ac = ac.n
    P.wait("vector", ac, v_passA_ac)
    op("vector", lambda e: e.tensor_reduce(out=redt[:, 0:1], in_=absacc[:, 0:48], axis=AX.X, op=ALU.add))
    v_red = op("vector", lambda e: e.tensor_reduce(out=redt[:, 1:2], in_=absacc[:, 48:64], axis=AX.X, op=ALU.add))
    op("vector", lambda e: e.memset(onescol[:], 1.0))
    v_red = selfsem("vector")
    P.wait("tensor", dv, v_red)
    op("tensor", lambda e: e.matmul(pG[0:1, 0:1], redt[:, 0:1], onescol[:], start=True, stop=True))
    v_mmr = op("tensor", lambda e: e.matmul(pG[0:1, 1:2], redt[:, 1:2], onescol[:], start=True, stop=True))
    P.wait("vector", pe, v_mmr)
    op("vector", lambda e: e.tensor_copy(out=mrow[0:1, 0:2], in_=pG[0:1, 0:2]))
    selfsem("vector")
    op("vector", lambda e: e.tensor_scalar(out=mrow[0:1, 2:3], in0=mrow[0:1, 0:1],
                                           scalar1=1.0 / QN, scalar2=1e-5, op0=ALU.mult, op1=ALU.max))
    op("vector", lambda e: e.tensor_scalar(out=mrow[0:1, 3:4], in0=mrow[0:1, 1:2],
                                           scalar1=1.0 / PN, scalar2=1e-5, op0=ALU.mult, op1=ALU.max))
    selfsem("vector")
    op("vector", lambda e: e.tensor_scalar(out=brow[0:1, 0:1], in0=mrow[0:1, 2:3], scalar1=0.5, scalar2=None, op0=ALU.mult))
    op("vector", lambda e: e.tensor_scalar(out=brow[0:1, 1:2], in0=mrow[0:1, 2:3], scalar1=-0.5, scalar2=None, op0=ALU.mult))
    op("vector", lambda e: e.tensor_scalar(out=brow[0:1, 2:3], in0=mrow[0:1, 3:4], scalar1=0.5, scalar2=None, op0=ALU.mult))
    op("vector", lambda e: e.tensor_scalar(out=brow[0:1, 3:4], in0=mrow[0:1, 3:4], scalar1=-0.5, scalar2=None, op0=ALU.mult))
    op("vector", lambda e: e.tensor_tensor(out=m2row[0:1, 0:1], in0=mrow[0:1, 2:3], in1=mrow[0:1, 2:3], op=ALU.mult))
    selfsem("vector")
    op("vector", lambda e: e.tensor_tensor(out=m2row[0:1, 1:2], in0=m2row[0:1, 0:1], in1=mrow[0:1, 2:3], op=ALU.mult))
    selfsem("vector")
    op("vector", lambda e: e.tensor_tensor(out=m2row[0:1, 2:3], in0=m2row[0:1, 1:2], in1=mrow[0:1, 3:4], op=ALU.mult))
    op("vector", lambda e: e.tensor_tensor(out=m2row[0:1, 3:4], in0=m2row[0:1, 1:2], in1=m2row[0:1, 1:2], op=ALU.mult))
    v_mf = selfsem("vector")
    op("vector", lambda e: e.tensor_scalar(out=brow[0:1, 5:6], in0=m2row[0:1, 3:4],
                                           scalar1=1.0 / DM, scalar2=None, op0=ALU.mult))
    P.wait("scalar", dv, v_mf)
    v_lnmf = op("scalar", lambda e: e.activation(out=brow[0:1, 4:5], in_=m2row[0:1, 2:3], func=AF.Ln))
    P.wait("vector", ac, v_lnmf)
    v_brow = selfsem("vector")
    P.wait("tensor", dv, v_brow)
    v_bc = op("tensor", lambda e: e.matmul(pG[:, 0:6], ones1[:], brow[0:1, 0:6], start=True, stop=True))
    P.wait("vector", pe, v_bc)
    op("vector", lambda e: e.tensor_copy(out=scol[:, 0:6], in_=pG[:, 0:6]))
    v_scol = selfsem("vector")

    v_xload = [0] * NT
    v_hact1 = [0] * NT
    v_htrans = [0] * NT
    v_st1 = [0] * NT
    v_xstash_by = [0] * NT
    for tt in range(NT):
        s2 = tt % 2
        if tt >= 2:
            P.wait("scalar", ac, v_st1[tt - 2])
            P.wait("scalar", dv, v_xstash_by[tt - 2])
        v_xload[tt] = dma("sc", wraw[s2], dx[tt * 128:(tt + 1) * 128, :], sem=wr_s[s2])
        P.wait("scalar", wr_s[s2], v_xload[tt])
        op("scalar", lambda e, s2=s2, tt=tt: e.activation(out=dmyP1, in_=wraw[s2], func=AF.Copy,
                                                          accum_out=sumc[:, tt:tt + 1]))
        v_st1[tt] = op("scalar", lambda e, s2=s2, tt=tt: e.activation(out=dmyP1, in_=wraw[s2], func=AF.Square,
                                                                      accum_out=sqc[:, tt:tt + 1]))
        P.wait("vector", wr_s[s2], v_xload[tt])
        v_xstash_by[tt] = op("vector", lambda e, s2=s2, tt=tt: e.tensor_copy(out=o3[:, tt, :], in_=wraw[s2]))
    P.wait("vector", ac, v_st1[NT - 1])
    op("vector", lambda e: e.tensor_scalar(out=tnyA[:], in0=sumc[:], scalar1=1.0 / DM, scalar2=None, op0=ALU.mult))
    op("vector", lambda e: e.tensor_scalar(out=tnyB[:], in0=sqc[:], scalar1=1.0 / DM, scalar2=None, op0=ALU.mult))
    selfsem("vector")
    op("vector", lambda e: e.tensor_tensor(out=tnyC[:], in0=tnyA[:], in1=tnyA[:], op=ALU.mult))
    selfsem("vector")
    op("vector", lambda e: e.tensor_tensor(out=tnyD[:], in0=tnyB[:], in1=tnyC[:], op=ALU.subtract))
    selfsem("vector")
    v_t1 = op("vector", lambda e: e.tensor_scalar(out=tnyB[:], in0=tnyD[:], scalar1=1e-5, scalar2=None, op0=ALU.add))
    P.wait("scalar", dv, v_t1)
    op("scalar", lambda e: e.activation(out=tnyC[:], in_=tnyB[:], func=AF.Ln))
    selfsem("scalar")
    v_e1 = op("scalar", lambda e: e.activation(out=sqc[:], in_=tnyC[:], func=AF.Exp, scale=-1.0))
    P.wait("vector", ac, v_e1)
    v_q = op("vector", lambda e: e.scalar_tensor_tensor(out=tnyB[:], in0=tnyD[:], scalar=1.0, in1=sqc[:],
                                                        op0=ALU.mult, op1=ALU.mult))
    P.wait("scalar", dv, v_q)
    op("scalar", lambda e: e.activation(out=tnyB[:], in_=tnyB[:], func=AF.Ln, bias=EPS32))
    selfsem("scalar")
    v_l2 = ac.n
    P.wait("vector", ac, v_l2)
    v_ls = op("vector", lambda e: e.tensor_tensor(out=tnyD[:], in0=tnyC[:], in1=tnyB[:], op=ALU.add))
    P.wait("scalar", dv, v_ls)
    v_c1 = op("scalar", lambda e: e.activation(out=c1c[:], in_=tnyD[:], func=AF.Exp, scale=-0.5))
    P.wait("vector", ac, v_c1)
    op("vector", lambda e: e.tensor_scalar(out=b1c[:], in0=tnyA[:], scalar1=-1.0, scalar2=None, op0=ALU.mult))
    v_p1b = selfsem("vector")
    for tt in range(NT):
        s2 = tt % 2
        if tt >= 2:
            P.wait("vector", hT_s[s2], v_htrans[tt - 2])
        v_hact1[tt] = op("vector", lambda e, s2=s2, tt=tt: e.tensor_scalar(
            out=ht[s2], in0=o3[:, tt, :], scalar1=b1c[:, tt:tt + 1], scalar2=c1c[:, tt:tt + 1],
            op0=ALU.add, op1=ALU.mult))
        P.wait("sync", dv, v_hact1[tt])
        v_htrans[tt] = dma("sy", hTr[:, :, tt * 128:(tt + 1) * 128],
                           ht[s2].rearrange("p (fc j) -> p fc j", j=128), transpose=True, sem=hT_s[s2])
    v_hT_all2 = (hT_s[0], v_htrans[NT - 2]); v_hT_all = v_htrans[NT - 1]
    P.wait("vector", hT_s[0], v_htrans[NT - 2])
    P.wait("vector", hT_s[1], v_htrans[NT - 1])
    for fc in range(NFC):
        op("vector", lambda e, fc=fc: e.tensor_scalar(out=hTr[:, fc, :], in0=hTr[:, fc, :],
                                                      scalar1=gcol[:, fc:fc + 1], scalar2=None, op0=ALU.mult))
    v_hT_ready = dv.n
    if debug:
        P.wait("scalar", dv, v_hT_ready)
        dma("sc", dbg_hT, big1[:], sem=dbg_s)

    v_quant = {}
    v_qkvmm_done = {-1: 0}
    v_qkvevac_done = {}
    v_scan_done = {-2: 0, -1: 0}
    v_ql_last = {-2: 0, -1: 0}
    g_masked = 0
    g_klam = 0
    g_vevac = 0
    g_oevac = 0
    g_supd = 0
    g_sbf_prev = 0
    g_cross = {}
    g_intra = {}
    g_kv = {}
    cglob = 0
    scan_tmp = None
    g_masked2 = {}
    bankA_free = [0, 0]
    v_projq_store = [0] * NFC
    pieceidx = 0
    v_piece_red = {}
    v_lamb = {}
    dqkvTp = dqkvT.rearrange("(fc p) c -> p fc c", p=128)

    def emit_quant_steps(h):
        nonlocal pieceidx
        steps = []
        for t3 in range(3):
            pslot = pieceidx % 2
            pidx = pieceidx
            if pieceidx >= 2:
                P.wait_lazy("gpsimd", dv, lambda k=pieceidx - 2: v_piece_red[k])
            else:
                P.wait("gpsimd", dv, v_hact1[NT - 1])
            cols = t3 * 2048 + h * 128
            vld = dma("gq", wraw[pslot].rearrange("p (fc j) -> p fc j", j=128),
                      dqkvTp[:, :, cols:cols + 128], sem=wr_s[pslot])
            pieceidx += 1

            def step1(pslot=pslot, pidx=pidx, vld=vld, t3=t3):
                P.wait("vector", wr_s[pslot], vld)
                if pidx == 0:
                    P.wait("vector", dv, v_scol)
                    P.wait("vector", hT_s[0], v_htrans[NT - 2])
                    P.wait("vector", hT_s[1], v_htrans[NT - 1])
                P.wait("vector", pe, v_scan_done[h - 2])
                wqr = wq[:, t3 * 2048:(t3 + 1) * 2048]
                op("vector", lambda e, pslot=pslot, wqr=wqr: e.tensor_scalar(
                    out=wqr, in0=wraw[pslot], scalar1=scol[:, 0:1], scalar2=None, op0=ALU.is_gt))

            def step2(pslot=pslot, pidx=pidx, t3=t3, h=h):
                wqr = wq[:, t3 * 2048:(t3 + 1) * 2048]
                v_piece_red[pidx] = op("vector", lambda e, pslot=pslot, wqr=wqr: e.scalar_tensor_tensor(
                    out=wqr, in0=wraw[pslot], scalar=scol[:, 1:2], in1=wqr,
                    op0=ALU.is_lt, op1=ALU.subtract))
                v_quant[(h, t3)] = v_piece_red[pidx]
            steps.append(step1)
            steps.append(step2)
        return steps

    def emit_quant(h):
        for st_ in emit_quant_steps(h):
            st_()

    def emit_lamb(h):
        s2 = h % 2
        P.wait("scalar", dv, v_ql_last[h - 2])
        v_lamb[h] = op("scalar", lambda e, h=h, s2=s2: e.activation(
            out=lamb[s2][:], in_=tcolt[:], func=AF.Exp, scale=negs[:, h:h + 1]))

    first_mm = [True]

    def emit_qkv_group(h, g):
        s2 = h % 2
        t3, tb = g // 4, g % 4
        gb = g % 2
        pbank = [pA0, pA1][gb]
        P.wait("tensor", dv, v_quant[(h, t3)])
        if first_mm[0]:
            P.wait("tensor", dv, v_hT_ready)
            first_mm[0] = False
        if isinstance(bankA_free[gb], tuple):
            P.wait("tensor", ac, bankA_free[gb][1])
        else:
            P.wait("tensor", dv, bankA_free[gb])
        vmm = 0
        for fc in range(NFC):
            vmm = op("tensor", lambda e, t3=t3, tb=tb, fc=fc, pbank=pbank: e.matmul(
                pbank[:], wq3[:, t3, fc, :], hTr[:, fc, tb * 512:(tb + 1) * 512],
                start=(fc == 0), stop=(fc == NFC - 1)))
        P.wait("scalar", pe, vmm)
        P.wait("scalar", pe, v_scan_done[h - 2])
        bankA_free[gb] = ("ac", op("scalar", lambda e, s2=s2, t3=t3, tb=tb, pbank=pbank: e.activation(
            out=qkv3[s2][:, t3, tb * 512:(tb + 1) * 512], in_=pbank[:], func=AF.Copy)))

    def emit_scan_pre(h, c):
        nonlocal g_masked, g_klam, g_vevac, scan_tmp
        s2 = h % 2
        cs = c % 2
        qTc = qkv3[s2][:, 0, c * 128:(c + 1) * 128]
        kTc = qkv3[s2][:, 1, c * 128:(c + 1) * 128]
        vTc = qkv3[s2][:, 2, c * 128:(c + 1) * 128]
        if c == 0:
            P.wait("tensor", ac, v_qkvevac_done[h])
        sbank = [pB, pG][cglob % 2]
        P.wait("tensor", dv, g_masked2.get(cglob - 2, 0))
        v_sc_mm = op("tensor", lambda e, kTc=kTc, qTc=qTc, sbank=sbank: e.matmul(
            sbank[:, 0:128], kTc, qTc, start=True, stop=True))
        P.wait("tensor", ac, g_klam)
        v_kt = op("tensor", lambda e, kTc=kTc: e.matmul(pE[:, 0:128], kTc, ident[:], start=True, stop=True))
        P.wait("tensor", dv, g_vevac)
        v_vt = op("tensor", lambda e, vTc=vTc: e.matmul(pF[:, 0:128], vTc, ident[:], start=True, stop=True))
        P.wait("vector", pe, v_sc_mm)
        if cglob == 0:
            P.wait("vector", dv, v_masks)
        P.wait("vector", pe, g_intra.get(cglob - 2, 0))
        g_masked = op("vector", lambda e, cs=cs, h=h, sbank=sbank: e.tensor_tensor(
            out=mk[cs][:], in0=sbank[:, 0:128], in1=masks[:, h * 128:(h + 1) * 128], op=ALU.mult))
        g_masked2[cglob] = g_masked
        P.wait("vector", pe, v_vt)
        P.wait("vector", pe, g_kv.get(cglob - 2, 0))
        g_vevac = op("vector", lambda e, cs=cs: e.tensor_copy(out=vtk[cs][:], in_=pF[:, 0:128]))
        P.wait("scalar", pe, v_kt)
        if cglob == 0:
            P.wait("scalar", ac, v_lamtabs)
        P.wait("scalar", pe, g_kv.get(cglob - 2, 0))
        g_klam = op("scalar", lambda e, cs=cs, h=h: e.activation(
            out=kl[cs][:], in_=pE[:, 0:128], func=AF.Copy, scale=lamrev[:, h:h + 1]))
        scan_tmp = (v_kt, v_vt)

    def emit_scan_post(h, c):
        nonlocal cglob, g_masked, g_klam, g_vevac, g_oevac, g_supd, g_sbf_prev
        s2 = h % 2
        cs = c % 2
        P.wait("tensor", dv, g_masked)
        P.wait("tensor", dv, g_vevac)
        P.wait("tensor", ac, g_oevac)
        v_intra = op("tensor", lambda e, cs=cs, c=c: e.matmul(
            pC[:, 0:128], mk[cs][:], vtk[cs][:], start=True, stop=(c == 0)))
        g_intra[cglob] = v_intra
        if c > 0:
            P.wait("tensor", dv, v_qlb[h])
            P.wait("tensor", dv, g_sbf_prev)
            v_cross = op("tensor", lambda e, cs=cs, c=c: e.matmul(
                pC[:, 0:128], qlbuf[:, c * 128:(c + 1) * 128], Sbf[(cs + 1) % 2][:], start=False, stop=True))
        else:
            v_cross = v_intra
        g_cross[cglob] = v_cross
        P.wait("tensor", ac, g_klam)
        P.wait("tensor", dv, g_supd)
        v_kv = op("tensor", lambda e, cs=cs: e.matmul(pD[:, 0:128], kl[cs][:], vtk[cs][:], start=True, stop=True))
        g_kv[cglob] = v_kv
        P.wait("scalar", pe, v_cross)
        g_oevac = op("scalar", lambda e, c=c, h=h: e.activation(
            out=o3[:, c, h * 128:(h + 1) * 128], in_=pC[:, 0:128], func=AF.Copy, scale=SCALE_O))
        P.wait("vector", pe, v_kv)
        P.wait("vector", pe, g_cross.get(cglob - 1, 0))
        if c == 0:
            g_sbf_prev = op("vector", lambda e, cs=cs: e.tensor_copy(out=Sbf[cs][:], in_=pD[:, 0:128]))
            g_supd = op("vector", lambda e: e.tensor_copy(out=Sst[:], in_=pD[:, 0:128]))
        else:
            g_sbf_prev = op("vector", lambda e, cs=cs, h=h: e.scalar_tensor_tensor(
                out=Sbf[cs][:], in0=Sst[:], scalar=lamT[:, h:h + 1], in1=pD[:, 0:128],
                op0=ALU.mult, op1=ALU.add))
            g_supd = op("vector", lambda e, h=h: e.scalar_tensor_tensor(
                out=Sst[:], in0=Sst[:], scalar=lamT[:, h:h + 1], in1=pD[:, 0:128],
                op0=ALU.mult, op1=ALU.add))
        cglob += 1

    def emit_projq(h):
        nonlocal pieceidx
        for k in range(2):
            fc = (h - 8) * 2 + k
            pslot = pieceidx % 2
            P.wait("gpsimd", dv, v_piece_red[pieceidx - 2])
            vld = dma("gq", wraw[pslot], projT3[fc], sem=wr_s[pslot])
            P.wait("vector", wr_s[pslot], vld)
            op("vector", lambda e, pslot=pslot: e.tensor_scalar(
                out=pstage, in0=wraw[pslot], scalar1=scol[:, 2:3], scalar2=None, op0=ALU.is_gt))
            op("vector", lambda e, pslot=pslot: e.scalar_tensor_tensor(
                out=pstage, in0=wraw[pslot], scalar=scol[:, 3:4], in1=pstage,
                op0=ALU.is_lt, op1=ALU.subtract))
            v_piece_red[pieceidx] = op("vector", lambda e, fc=fc: e.tensor_scalar(
                out=pstage, in0=pstage, scalar1=w12col[:, fc:fc + 1], scalar2=None, op0=ALU.mult))
            pieceidx += 1
            P.wait("gpsimd", dv, v_piece_red[pieceidx - 1])
            v_projq_store[fc] = dma("gq", dwpq[fc], pstage, sem=wpq_st)

    emit_quant(0)
    emit_lamb(0)
    for g in range(12):
        emit_qkv_group(0, g)
    v_qkvevac_done[0] = ac.n
    v_scan_done[-1] = pe.n
    v_qlb = {}
    for h in range(NH):
        pendq = []
        if h + 1 < NH:
            pendq = emit_quant_steps(h + 1)
            emit_lamb(h + 1)
        P.wait("vector", ac, v_lamb[h])
        P.wait("vector", ac, v_qkvevac_done[h])
        P.wait("vector", pe, v_scan_done[h - 1])
        v_qlb[h] = op("vector", lambda e, s2=h % 2: e.tensor_tensor(
            out=qlbuf.rearrange("p (c t) -> p c t", t=128),
            in0=qkv3[s2][:, 0, :].rearrange("p (c t) -> p c t", t=128),
            in1=lamb[s2][:].unsqueeze(1).broadcast_to([128, NT, 128]),
            op=ALU.mult))
        for c in range(NT):
            emit_scan_pre(h, c)
            if c >= 4:
                for _ in range(2):
                    if pendq:
                        pendq.pop(0)()
            if h + 1 < NH and c >= 4:
                emit_qkv_group(h + 1, c - 4)
            emit_scan_post(h, c)
        if h + 1 < NH:
            v_qkvevac_done[h + 1] = ac.n
        v_scan_done[h] = pe.n
        v_ql_last[h] = dv.n
        if h >= 8:
            emit_projq(h)
    v_qkvmm_done[NH - 1] = pe.n
    v_o_all = ac.n
    v_lastpiece = v_piece_red[pieceidx - 1]

    v_beta = [0] * NT
    for tt in range(NT):
        if tt == 0:
            P.wait("vector", ac, v_o_all)
            P.wait("scalar", ac, v_o_all)
        v_w1o = op("vector", lambda e, tt=tt: e.tensor_tensor(out=w1o, in0=o3[:, tt, :], in1=w1bc, op=ALU.mult))
        op("scalar", lambda e, tt=tt: e.activation(out=dmybf, in_=o3[:, tt, :], func=AF.Square,
                                                   accum_out=s1c[:, tt:tt + 1]))
        P.wait("scalar", dv, v_w1o)
        op("scalar", lambda e, tt=tt: e.activation(out=dmybf, in_=w1o, func=AF.Square,
                                                   accum_out=s2c[:, tt:tt + 1]))
        op("scalar", lambda e, tt=tt: e.activation(out=tnyA[:, tt:tt + 1], in_=s1c[:, tt:tt + 1],
                                                   func=AF.Ln, scale=scol[:, 5:6], bias=1e-6))
        selfsem("scalar")
        op("scalar", lambda e, tt=tt: e.activation(out=tnyB[:, tt:tt + 1], in_=tnyA[:, tt:tt + 1],
                                                   func=AF.Exp, scale=-1.0))
        v_e2 = selfsem("scalar")
        P.wait("vector", ac, v_e2)
        op("vector", lambda e, tt=tt: e.scalar_tensor_tensor(out=tnyC[:, tt:tt + 1], in0=s2c[:, tt:tt + 1],
                                                             scalar=scol[:, 5:6], in1=tnyB[:, tt:tt + 1],
                                                             op0=ALU.mult, op1=ALU.mult))
        v_m2 = selfsem("vector")
        P.wait("scalar", dv, v_m2)
        op("scalar", lambda e, tt=tt: e.activation(out=tnyB[:, tt:tt + 1], in_=tnyC[:, tt:tt + 1],
                                                   func=AF.Ln, bias=EPS32))
        v_l2 = selfsem("scalar")
        P.wait("vector", ac, v_l2)
        op("vector", lambda e, tt=tt: e.tensor_tensor(out=tnyD[:, tt:tt + 1], in0=tnyA[:, tt:tt + 1],
                                                      in1=tnyB[:, tt:tt + 1], op=ALU.add))
        v_ls = selfsem("vector")
        P.wait("scalar", dv, v_ls)
        v_beta[tt] = op("scalar", lambda e, tt=tt: e.activation(out=betac[:, tt:tt + 1], in_=tnyD[:, tt:tt + 1],
                                                                func=AF.Exp, scale=-0.5, bias=scol[:, 4:5]))
    dv_p3pre = dv.n
    ac_p25 = ac.n
    if debug:
        P.wait("scalar", ac, ac.n)
        dma("sc", dbg_o, big2[:], sem=dbg_s)
        dma("sc", dbg_small[:, 0:8], scol[:], sem=dbg_s)
        dma("sc", dbg_small[:, 8:24], betac[:], sem=dbg_s)
        dma("sc", dbg_small[:, 24:40], c1c[:], sem=dbg_s)
        dma("sc", dbg_small[:, 40:56], s1c[:], sem=dbg_s)
        vdbg = dma("sc", dbg_small[0:1, 56:64], mrow[0:1, :], sem=dbg_s)
        P.wait("sync", dbg_s, vdbg)

    P.wait("gpsimd", pe, v_qkvmm_done[NH - 1])
    v_wpq = [0] * NFC
    for fc in range(NFC):
        P.wait("gpsimd", wpq_st, v_projq_store[fc])
        v_wpq[fc] = dma("gq", wpqv[:, fc, :], dwpq[fc], sem=wpq_ld)
    v_wpq_all = v_wpq[NFC - 1]

    PB3R = [[pA0, pA1, pB, pC], [pD, pE, pF, pG]]
    v_projmm = {-2: 0, -1: 0}
    v_evac3 = {-2: 0, -1: 0}
    v_x2add = {-2: 0, -1: 0}
    v_x2dma = [0] * NT
    v_finstore = [0] * NT
    v_fint_done = {-2: 0, -1: 0}
    v_oT = 0
    v_oT_by_slot = {0: 0, 1: 0}
    v_oT_by_tt = {}

    def emit_oT(tt):
        nonlocal v_oT
        s2 = tt % 2
        if tt <= 1:
            P.wait("sync", ac, v_o_all)
        P.wait("sync", pe, v_projmm[tt - 2])
        v_oT = dma("sy", oTt[s2].rearrange("p (fc j) -> p fc j", j=128),
                   o3[:, tt, :].rearrange("p (fc j) -> p fc j", j=128), transpose=True, sem=oT_s[s2])
        v_oT_by_slot[s2] = v_oT
        v_oT_by_tt[tt] = v_oT

    emit_oT(0)
    emit_oT(1)
    for tt in range(NT):
        s2 = tt % 2
        if tt >= 2:
            P.wait("scalar", dv, v_x2add[tt - 2])
        else:
            P.wait("scalar", dv, v_lastpiece)
        v_xr = dma("sc", xrl[s2], dx[tt * 128:(tt + 1) * 128, :], sem=wr_s[s2])
        P.wait("tensor", oT_s[s2], v_oT_by_tt[tt])
        PB3 = PB3R[tt % 2]
        if tt <= 1:
            P.wait("tensor", wpq_ld, v_wpq_all)
            P.wait("tensor", dv, dv_p3pre)
            P.wait("tensor", ac, v_o_all)
        pv = v_evac3.get(tt - 2, 0)
        if isinstance(pv, tuple):
            P.wait("tensor", ac, pv[0])
            P.wait("tensor", dv, pv[1])
        else:
            P.wait("tensor", ac, pv)
        vstop = [0] * 4
        for b in range(4):
            for fc in range(NFC):
                vstop[b] = op("tensor", lambda e, b=b, fc=fc, s2=s2, PB3=PB3: e.matmul(
                    PB3[b][:], oTt[s2].rearrange("p (fc j) -> p fc j", j=128)[:, fc, :],
                    wpqv[:, fc, b * 512:(b + 1) * 512], start=(fc == 0), stop=(fc == NFC - 1)))
        v_projmm[tt] = pe.n
        if tt + 2 < NT:
            emit_oT(tt + 2)
        P.wait("vector", ac, v_beta[tt])
        if tt >= 2:
            P.wait("vector", x2_s, v_x2dma[tt - 2])
            if stage == "full":
                P.wait("scalar", fs_s, v_finstore[tt - 2])
        v_evd = 0
        for b in range(4):
            P.wait("vector", pe, vstop[b])
            v_evd = op("vector", lambda e, b=b, s2=s2, tt=tt, PB3=PB3: e.tensor_scalar(
                out=x2t[s2][:, b * 512:(b + 1) * 512], in0=PB3[b][:], scalar1=betac[:, tt:tt + 1],
                scalar2=None, op0=ALU.mult))
        v_evac3[tt] = (0, v_evd)
        P.wait("vector", wr_s[s2], v_xr)
        v_x2add[tt] = op("vector", lambda e, s2=s2: e.tensor_tensor(out=x2t[s2], in0=x2t[s2], in1=xrl[s2], op=ALU.add))
        if stage == "attn":
            P.wait("scalar", dv, v_x2add[tt])
            v_x2dma[tt] = dma("sc", dy[tt * 128:(tt + 1) * 128, :], x2t[s2], sem=x2_s)
            continue
        P.wait("scalar", dv, v_x2add[tt])
        v_x2dma[tt] = dma("sc", dx2[tt], x2t[s2], sem=x2_s)
        op("vector", lambda e, s2=s2, tt=tt: e.tensor_reduce(out=sum2[:, tt:tt + 1], in_=x2t[s2],
                                                             axis=AX.X, op=ALU.add))
        v_st2a = op("scalar", lambda e, s2=s2, tt=tt: e.activation(out=dmybf, in_=x2t[s2], func=AF.Square,
                                                                   accum_out=sq2[:, tt:tt + 1]))
        P.wait("vector", ac, v_st2a)
        op("vector", lambda e, tt=tt: e.tensor_scalar(out=tnyA[:, tt:tt + 1], in0=sum2[:, tt:tt + 1],
                                                      scalar1=1.0 / DM, scalar2=None, op0=ALU.mult))
        op("vector", lambda e, tt=tt: e.tensor_scalar(out=tnyB[:, tt:tt + 1], in0=sq2[:, tt:tt + 1],
                                                      scalar1=1.0 / DM, scalar2=None, op0=ALU.mult))
        selfsem("vector")
        op("vector", lambda e, tt=tt: e.tensor_tensor(out=tnyC[:, tt:tt + 1], in0=tnyA[:, tt:tt + 1],
                                                      in1=tnyA[:, tt:tt + 1], op=ALU.mult))
        selfsem("vector")
        op("vector", lambda e, tt=tt: e.tensor_tensor(out=tnyD[:, tt:tt + 1], in0=tnyB[:, tt:tt + 1],
                                                      in1=tnyC[:, tt:tt + 1], op=ALU.subtract))
        v_var = selfsem("vector")
        P.wait("scalar", dv, v_var)
        op("scalar", lambda e, tt=tt: e.activation(out=tnyB[:, tt:tt + 1], in_=tnyD[:, tt:tt + 1],
                                                   func=AF.Ln, bias=1e-5))
        selfsem("scalar")
        op("scalar", lambda e, tt=tt: e.activation(out=c2c[:, tt:tt + 1], in_=tnyB[:, tt:tt + 1],
                                                   func=AF.Exp, scale=-0.5))
        v_c2 = selfsem("scalar")
        P.wait("vector", ac, v_c2)
        op("vector", lambda e, tt=tt: e.scalar_tensor_tensor(out=b2c[:, tt:tt + 1], in0=tnyA[:, tt:tt + 1],
                                                             scalar=-1.0, in1=c2c[:, tt:tt + 1],
                                                             op0=ALU.mult, op1=ALU.mult))
        v_b2 = selfsem("vector")
        P.wait("scalar", dv, v_b2)
        P.wait("scalar", ft_s, v_fint_done[tt - 2])
        v_fin = op("scalar", lambda e, s2=s2, tt=tt: e.activation(
            out=fint[s2], in_=x2t[s2], func=AF.Identity, scale=c2c[:, tt:tt + 1], bias=b2c[:, tt:tt + 1]))
        P.wait("sync", ac, v_fin)
        if tt == 0:
            P.wait("sync", ac, ac_p25)
            P.wait("sync", dv, dv_p3pre)
        if tt >= 1:
            P.wait("sync", fs_s, v_finstore[tt - 1])
        v_ft = dma("sy", finTt.rearrange("p (fc j) -> p fc j", j=128),
                   fint[s2].rearrange("p (fc j) -> p fc j", j=128), transpose=True, sem=ft_s)
        v_fint_done[tt] = v_ft
        P.wait("scalar", ft_s, v_ft)
        v_finstore[tt] = dma("sc", dffn[tt * 128:(tt + 1) * 128, :], finTt, sem=fs_s)

    if stage == "attn":
        P.wait("sync", x2_s, v_x2dma[NT - 1])
        _emit(nc, P)
        ctx.close()
        return nc

    P.wait("gpsimd", pe, v_projmm[NT - 1])
    v_fload = [0] * NT
    for tt in range(NT):
        P.wait("gpsimd", fs_s, v_finstore[tt])
        v_fload[tt] = dma("gq", finTv[:, :, tt * 128:(tt + 1) * 128],
                          dffn[tt * 128:(tt + 1) * 128, :].rearrange("p (fc j) -> p fc j", j=128), sem=fl_s)
    v_fload_all = v_fload[NT - 1]
    if debug:
        P.wait("scalar", fl_s, v_fload_all)
        dma("sc", dbg_fin, big1[:], sem=dbg_s)

    deq_dv = {}
    deq_po = {}
    kglob = [0]

    def dequant_tile(dst_ap, wsrc, ssrc):
        k = kglob[0]
        kglob[0] += 1
        b4 = k % 4
        b2r = k % 2
        if k >= 4:
            P.wait("sync", dv, deq_dv[k - 4])
            P.wait("gpsimd", dv, deq_dv[k - 4])
        vw = dma("sy", wu16[b4], wsrc, sem=wu_s[b4])
        vs = dma("gq", sbct[b4], ssrc, sem=sb_s[b4])
        P.wait("vector", wu_s[b4], vw)
        op("vector", lambda e, b4=b4, b2r=b2r: e.tensor_scalar(
            out=ntl[b2r], in0=wu16[b4], scalar1=shiftc[:, 0:1], scalar2=15,
            op0=ALU.logical_shift_right, op1=ALU.bitwise_and))
        op("vector", lambda e, b4=b4, b2r=b2r: e.tensor_scalar(
            out=atl[b2r], in0=ntl[b2r], scalar1=9, scalar2=0x0E00,
            op0=ALU.logical_shift_left, op1=ALU.bitwise_and))
        op("vector", lambda e, b4=b4, b2r=b2r: e.tensor_scalar(
            out=btl[b2r], in0=ntl[b2r], scalar1=12, scalar2=0x8000,
            op0=ALU.logical_shift_left, op1=ALU.bitwise_and))
        op("vector", lambda e, b4=b4, b2r=b2r: e.tensor_tensor(
            out=wu16[b4], in0=atl[b2r], in1=btl[b2r], op=ALU.bitwise_or))
        P.wait("vector", sb_s[b4], vs)
        deq_dv[k] = op("vector", lambda e, b4=b4, dst_ap=dst_ap: e.scalar_tensor_tensor(
            out=dst_ap, in0=wu16[b4].bitcast(F16), scalar=16384.0, in1=sbct[b4],
            op0=ALU.mult, op1=ALU.mult))
        return deq_dv[k]

    PBG = [[pA0, pA1], [pB, pC], [pD, pE], [pF, pG]]
    v_deq_done = {}
    v_mm4_done = {-2: 0, -1: 0}
    v_silu = {}
    v_hactm = {}
    v_hstore = {-2: 0, -1: 0}
    v_htr_by_ib = {}

    def deq_args(ib):
        wslot = ib % 2
        out = []
        for half in range(2):
            base_o = half * NI + ib * 512
            for fc in range(NFC):
                out.append((wdqv[:, wslot, half, fc, :],
                            dguw[fc * 128:(fc + 1) * 128, base_o:base_o + 512],
                            dgus[fc:fc + 1, base_o:base_o + 512].broadcast_to([128, 512])))
        return out

    P.wait("vector", oT_s[0], v_oT_by_slot[0])
    P.wait("vector", oT_s[1], v_oT_by_slot[1])
    P.wait("vector", ac, v_beta[NT - 1])
    for a in deq_args(0):
        dequant_tile(*a)
    v_deq_done[0] = dv.n
    v_p5segA = 0
    for ib in range(11):
        wslot = ib % 2
        if ib < 10:
            pend = deq_args(ib + 1)
        else:
            pend = []
            for ic in range(16):
                for ob in range(2):
                    cols = ob * 512
                    pend.append((big2[:, 16384 + ic * 1024 + ob * 512: 16384 + ic * 1024 + (ob + 1) * 512],
                                 ddnw[ic * 128:(ic + 1) * 128, cols:cols + 512],
                                 ddns[ic:ic + 1, cols:cols + 512].broadcast_to([128, 512])))
        if pend:
            P.wait("vector", pe, v_mm4_done[ib - 1])
        pi = 0
        for tt in range(NT):
            bsel = tt % 4
            gbank, ubank = PBG[bsel]
            P.wait("tensor", dv, v_deq_done[ib])
            if ib == 0 and tt == 0:
                P.wait("tensor", fl_s, v_fload_all)
            if ib == 0 and tt <= 3:
                P.wait("tensor", ac, v_evac3[NT - 1][0])
                P.wait("tensor", dv, v_evac3[NT - 1][1])
            P.wait("tensor", ac, v_silu.get((ib, tt - 4), v_silu.get((ib - 1, tt + 12), 0)))
            P.wait("tensor", dv, v_hactm.get((ib, tt - 4), v_hactm.get((ib - 1, tt + 12), 0)))
            vg = 0
            for fc in range(NFC):
                vg = op("tensor", lambda e, fc=fc, tt=tt, wslot=wslot, gbank=gbank: e.matmul(
                    gbank[:], finTv[:, fc, tt * 128:(tt + 1) * 128], wdqv[:, wslot, 0, fc, :],
                    start=(fc == 0), stop=(fc == NFC - 1)))
            vu = 0
            for fc in range(NFC):
                vu = op("tensor", lambda e, fc=fc, tt=tt, wslot=wslot, ubank=ubank: e.matmul(
                    ubank[:], finTv[:, fc, tt * 128:(tt + 1) * 128], wdqv[:, wslot, 1, fc, :],
                    start=(fc == 0), stop=(fc == NFC - 1)))
            P.wait("scalar", pe, vg)
            if tt >= 2:
                P.wait("scalar", dv, v_hactm[(ib, tt - 2)])
            elif ib >= 1:
                P.wait("scalar", dv, v_hactm[(ib - 1, tt + 14)])
            v_silu[(ib, tt)] = op("scalar", lambda e, tt=tt, gbank=gbank: e.activation(
                out=sg[tt % 2][:], in_=gbank[:], func=AF.Silu))
            P.wait("vector", pe, vu)
            P.wait("vector", ac, v_silu[(ib, tt)])
            if tt == 0 and ib >= 2:
                P.wait("vector", hx_s, v_htr_by_ib[ib - 2])
            v_hactm[(ib, tt)] = op("vector", lambda e, tt=tt, ubank=ubank, ib=ib: e.tensor_tensor(
                out=hact[ib % 2][:, tt * 512:(tt + 1) * 512], in0=sg[tt % 2][:], in1=ubank[:], op=ALU.mult))
            while pend and pi < 3 * (tt + 1):
                dequant_tile(*pend.pop(0))
                pi += 1
        while pend:
            dequant_tile(*pend.pop(0))
        if ib < 10:
            v_deq_done[ib + 1] = dv.n
        else:
            v_p5segA = dv.n
        v_mm4_done[ib] = pe.n
        P.wait("sync", dv, v_hactm[(ib, NT - 1)])
        if ib >= 1:
            P.wait("sync", hs_s, v_hstore[ib - 1])
        v_htr_by_ib[ib] = dma("sy", hactT[0].rearrange("p (t j) -> p t j", j=128),
                              hact[ib % 2].rearrange("p (t j) -> p t j", j=128), transpose=True, sem=hx_s)
        P.wait("scalar", hx_s, v_htr_by_ib[ib])
        for jj in range(4):
            v_hstore[ib] = dma("sc", dhactT[:, :, ib * 4 + jj, :].rearrange("t p q -> p t q"),
                               hactT[0].rearrange("p (t j q) -> p t j q", j=4, q=128)[:, :, jj, :], sem=hs_s)
    v_p4_done = pe.n
    ac_p4_done = ac.n
    dv_p4_done = dv.n

    def wdn_ap(half, ic, lo, hi):
        if ic < 16:
            base = (16384 if half == 0 else 0) + ic * 1024
            return big2[:, base + lo: base + hi]
        base = (ic - 16) * 1024
        return big1[:, base + lo: base + hi]

    v_out = 0
    v_segA = {}
    v_segB = {}
    v_segA[0] = v_p5segA
    for half in range(2):
        if half == 0:
            P.wait("vector", pe, v_p4_done)
        else:
            P.wait("vector", pe, v_half_done)
        for ic in range(16, NIC):
            for ob in range(2):
                cols = half * 1024 + ob * 512
                dequant_tile(wdn_ap(half, ic, ob * 512, (ob + 1) * 512),
                             ddnw[ic * 128:(ic + 1) * 128, cols:cols + 512],
                             ddns[ic:ic + 1, cols:cols + 512].broadcast_to([128, 512]))
        v_segB[half] = dv.n
        P.wait("gpsimd", hs_s, v_hstore[10])
        v_dnmm = {-2: v_p4_done if half == 0 else v_half_done,
                  -1: v_p4_done if half == 0 else v_half_done}
        if half == 0:
            v_sq_prev = ac_p4_done
            v_yev_prev = dv_p4_done
        else:
            v_sq_prev = v_sq2[NT - 1]
            v_yev_prev = v_yev2[NT - 1]
        v_sq2 = {}
        v_yev2 = {}
        segA1_pend = []
        if half == 0:
            for ic in range(16):
                for ob in range(2):
                    cols = 1024 + ob * 512
                    segA1_pend.append((wdn_ap(1, ic, ob * 512, (ob + 1) * 512),
                                       ddnw[ic * 128:(ic + 1) * 128, cols:cols + 512],
                                       ddns[ic:ic + 1, cols:cols + 512].broadcast_to([128, 512])))
        for tt in range(NT):
            s2 = tt % 2
            if tt >= 2:
                P.wait("gpsimd", pe, v_dnmm[tt - 2])
            v_hl = dma("gq", hA[s2], dhactT[tt].rearrange("p ic q -> p (ic q)"), sem=ha_s[s2])
            P.wait("tensor", ha_s[s2], v_hl)
            P.wait("tensor", dv, v_segA[half])
            if tt >= 2 or half == 1:
                P.wait("tensor", dv, v_yev2.get(tt - 2, v_yev_prev))
                P.wait("tensor", ac, v_sq2.get(tt - 2, v_sq_prev))
            else:
                P.wait("tensor", dv, v_yev_prev)
                P.wait("tensor", ac, v_sq_prev)
            pd0, pd1 = ([pA0, pA1] if tt % 2 == 0 else [pB, pC])
            v0 = 0
            for ic in range(NIC):
                if tt == 0 and ic == 16:
                    P.wait("tensor", dv, v_segB[half])
                lhs = hA[s2][:, ic * 128:(ic + 1) * 128]
                rhs = wdn_ap(half, ic, 0, 512)
                v0 = op("tensor", lambda e, lhs=lhs, rhs=rhs, ic=ic, pd0=pd0: e.matmul(
                    pd0[:], lhs, rhs, start=(ic == 0), stop=(ic == NIC - 1)))
            v1 = 0
            for ic in range(NIC):
                lhs = hA[s2][:, ic * 128:(ic + 1) * 128]
                rhs = wdn_ap(half, ic, 512, 1024)
                v1 = op("tensor", lambda e, lhs=lhs, rhs=rhs, ic=ic, pd1=pd1: e.matmul(
                    pd1[:], lhs, rhs, start=(ic == 0), stop=(ic == NIC - 1)))
            v_dnmm[tt] = pe.n
            for _ in range(2):
                if segA1_pend:
                    dequant_tile(*segA1_pend.pop(0))
            if half == 0 and tt == NT - 1:
                while segA1_pend:
                    dequant_tile(*segA1_pend.pop(0))
                v_segA[1] = dv.n
            P.wait("scalar", pe, v0)
            col0 = (2 * half) * NT + tt
            op("scalar", lambda e, col0=col0, pd0=pd0: e.activation(
                out=dmy512, in_=pd0[:], func=AF.Square, accum_out=ssqs[:, col0:col0 + 1]))
            P.wait("scalar", pe, v1)
            col1 = (2 * half + 1) * NT + tt
            v_sq = op("scalar", lambda e, col1=col1, pd1=pd1: e.activation(
                out=dmy512, in_=pd1[:], func=AF.Square, accum_out=ssqs[:, col1:col1 + 1]))
            v_sq2[tt] = v_sq
            ybuf = y0t if half == 0 else y1t
            P.wait("vector", ac, v_sq)
            op("vector", lambda e, ybuf=ybuf, pd0=pd0: e.tensor_copy(out=ybuf[:, 0:512], in_=pd0[:]))
            v_yev = op("vector", lambda e, ybuf=ybuf, pd1=pd1: e.tensor_copy(out=ybuf[:, 512:1024], in_=pd1[:]))
            v_yev2[tt] = v_yev
            if half == 0:
                P.wait("scalar", dv, v_yev)
                vst = dma("sc", dyh[tt], y0t, sem=ys_s)
                P.wait("vector", ys_s, vst)
            else:
                P.wait("vector", ac, v_sq)
                op("vector", lambda e, tt=tt: e.tensor_tensor(
                    out=tnyA[:, tt:tt + 1], in0=ssqs[:, tt:tt + 1], in1=ssqs[:, NT + tt:NT + tt + 1], op=ALU.add))
                selfsem("vector")
                op("vector", lambda e, tt=tt: e.tensor_tensor(
                    out=tnyB[:, tt:tt + 1], in0=ssqs[:, 2 * NT + tt:2 * NT + tt + 1],
                    in1=ssqs[:, 3 * NT + tt:3 * NT + tt + 1], op=ALU.add))
                selfsem("vector")
                v_sst = op("vector", lambda e, tt=tt: e.tensor_tensor(
                    out=tnyC[:, tt:tt + 1], in0=tnyA[:, tt:tt + 1], in1=tnyB[:, tt:tt + 1], op=ALU.add))
                P.wait("scalar", dv, v_sst)
                op("scalar", lambda e, tt=tt: e.activation(out=tnyD[:, tt:tt + 1], in_=tnyC[:, tt:tt + 1],
                                                           func=AF.Ln, scale=1.0 / DM, bias=1e-6))
                selfsem("scalar")
                v_r3 = op("scalar", lambda e, tt=tt: e.activation(out=r3c[:, tt:tt + 1], in_=tnyD[:, tt:tt + 1],
                                                                  func=AF.Exp, scale=-0.5))
                v_x2l = dma("sc", x2rl, dx2[tt], sem=xy_s)
                v_y0l = dma("sc", y0t, dyh[tt], sem=xy_s)
                P.wait("vector", xy_s, v_y0l)
                P.wait("vector", ac, v_r3)
                op("vector", lambda e, tt=tt: e.scalar_tensor_tensor(
                    out=y0t, in0=y0t, scalar=r3c[:, tt:tt + 1], in1=x2rl[:, 0:1024],
                    op0=ALU.mult, op1=ALU.add))
                v_o1 = op("vector", lambda e, tt=tt: e.scalar_tensor_tensor(
                    out=y1t, in0=y1t, scalar=r3c[:, tt:tt + 1], in1=x2rl[:, 1024:2048],
                    op0=ALU.mult, op1=ALU.add))
                P.wait("scalar", dv, v_o1)
                dma("sc", dy[tt * 128:(tt + 1) * 128, 0:1024], y0t, sem=yo_s)
                v_out = dma("sc", dy[tt * 128:(tt + 1) * 128, 1024:2048], y1t, sem=yo_s)
                P.wait("vector", yo_s, v_out)
        v_half_done = pe.n
    P.wait("sync", yo_s, v_out)

    _emit(nc, P)
    ctx.close()
    return nc


def _emit(nc, P):
    with nc.Block() as block:
        @block.sync
        def _(e):
            P.run("sync", e)

        @block.scalar
        def _(e):
            P.run("scalar", e)

        @block.vector
        def _(e):
            P.run("vector", e)

        @block.gpsimd
        def _(e):
            P.run("gpsimd", e)

        @block.tensor
        def _(e):
            P.run("tensor", e)


def host_prepare(inputs):
    x = np.ascontiguousarray(np.asarray(inputs["L0_attn_x"], dtype=np.float32))
    qkv_w = np.asarray(inputs["L0_attn_qkv_w"], dtype=np.float32)
    proj_w = np.asarray(inputs["L0_attn_proj_w"], dtype=np.float32)
    qkvT = np.ascontiguousarray(qkv_w.T)
    projT = np.ascontiguousarray(proj_w.T)
    gcol = np.ascontiguousarray(np.asarray(inputs["L0_attn_qkv_norm_w"], np.float32).reshape(NFC, 128).T)
    w1 = np.asarray(inputs["L0_attn_norm_w"], np.float32)
    w2 = np.asarray(inputs["L0_attn_proj_norm_w"], np.float32)
    w1col = np.ascontiguousarray(w1.reshape(NFC, 128).T)
    w2col = np.ascontiguousarray(w2.reshape(NFC, 128).T)
    w1row = np.ascontiguousarray(w1.reshape(1, DM))
    srow = np.ascontiguousarray(np.asarray(inputs["L0_attn_s"], np.float32).reshape(1, NH))
    gu = np.ascontiguousarray(np.asarray(inputs["L0_ffn_gate_up_packed"], np.int32))
    guw = np.ascontiguousarray(gu.view(np.uint16)[:, (np.arange(DM) // 4)].T)
    gus = np.ascontiguousarray(np.asarray(inputs["L0_ffn_gate_up_scales"], np.float16))
    dn = np.ascontiguousarray(np.asarray(inputs["L0_ffn_down_packed"], np.int32))
    dnw = np.ascontiguousarray(dn.view(np.uint16)[:, (np.arange(NI) // 4)].T)
    dns = np.ascontiguousarray(np.asarray(inputs["L0_ffn_down_scales"], np.float16))
    shared = dict(qkvT=qkvT, projT=projT, gcol=gcol, w1col=w1col, w2col=w2col,
                  w1row=w1row, srow=srow, guw=guw, gus=gus, dnw=dnw, dns=dns)
    return x, shared


def kernel(**inputs):
    x, shared = host_prepare(inputs)
    nc = build_nc("full")
    in_maps = []
    for core in range(8):
        m = dict(shared)
        m["x"] = np.ascontiguousarray(x[core])
        in_maps.append(m)
    res = run_bass_kernel_spmd(nc, in_maps, core_ids=list(range(8)))
    out = np.stack([res.results[i]["y"] for i in range(8)], axis=0)
    return out.astype(np.float32)
